# version 07
# baseline (speedup 1.0000x reference)
from contextlib import ExitStack

import numpy as np
import ml_dtypes

import concourse.bass as bass
import concourse.mybir as mybir
from concourse.bass_utils import run_bass_kernel_spmd

F32 = mybir.dt.float32
BF16 = mybir.dt.bfloat16
AF = mybir.ActivationFunctionType
ALU = mybir.AluOpType
AX = mybir.AxisListType

ENGINES = ("tensor", "vector", "scalar", "gpsimd", "sync")

B, S, D = 2, 4096, 2048
NCORE = 8
T = 1024
TH = 512
DT = 16
NF = 8
NE = 8
NHEAD = 16
RT = 256
NINV = 1.0 / D


class Sem:
    __slots__ = ("h", "count", "name")

    def __init__(self, h, name):
        self.h = h
        self.name = name
        self.count = 0


class Prog:
    def __init__(self, nc, stack):
        self.nc = nc
        self.stack = stack
        self.ops = {e: [] for e in ENGINES}
        self._waited = {e: {} for e in ENGINES}

    def sem(self, name):
        return Sem(self.stack.enter_context(self.nc.semaphore(name)), name)

    def tok(self, sem):
        return (sem, sem.count)

    auto_self = None

    def emit(self, eng, fn, waits=(), inc=None, inc_n=1):
        if self.auto_self and eng in self.auto_self:
            waits = list(waits) + [self.tok(self.auto_self[eng])]
        waited = self._waited[eng]
        rw = []
        for s, v in waits:
            if v > 0 and waited.get(s.name, 0) < v:
                rw.append((s, v))
                waited[s.name] = v
        if inc is not None:
            inc.count += inc_n

        def run(e, _fn=fn, _w=rw, _i=inc, _n=inc_n):
            for s, v in _w:
                e.wait_ge(s.h, v)
            inst = _fn(e)
            if _i is not None:
                if inst is None:
                    e.sem_inc(_i.h, _n)
                else:
                    inst.then_inc(_i.h, _n)

        self.ops[eng].append(run)
        return None if inc is None else (inc, inc.count)

    def dma(self, eng, out, in_, sem, waits=(), **kw):
        return self.emit(eng, lambda e: e.dma_start(out=out, in_=in_, **kw),
                         waits=waits, inc=sem, inc_n=16)

    def finalize(self):
        with self.nc.Block() as block:
            for eng_name in ENGINES:
                ops = self.ops[eng_name]

                def body(e, _ops=ops):
                    for f in _ops:
                        f(e)

                getattr(block, eng_name)(body)


def build_program(taps=(), stop=None, detect=False):
    nc = bass.Bass("TRN2", target_bir_lowering=False, debug=False,
                   detect_race_conditions=detect)
    stack = ExitStack()
    p = Prog(nc, stack)
    taps = set(taps)

    def din(name, shape, dt=F32):
        return nc.dram_tensor(name, shape, dt, kind="ExternalInput").ap()

    x_my = din("x_my", [T, D])
    x_rt = din("x_rt", [2 * RT, D])
    Wq = din("Wq", [D, D]); Wk = din("Wk", [D, D]); Wv = din("Wv", [D, D])
    Ws = din("Ws", [D, D]); Wt = din("Wt", [D, D])
    W1 = din("W1", [2 * D, 4 * D]); W2 = din("W2", [4 * D, 2 * D])
    w3t = din("w3t", [D, D])
    b1c = din("b1c", [128, 64]); b2c = din("b2c", [128, 32]); b3c = din("b3c", [128, 16])
    fw1t = din("fw1t", [NE, 128, 128]); fw2t = din("fw2t", [NE, 128, 128])
    fb1c = din("fb1c", [128, NE]); fb2 = din("fb2", [NE, 128])
    swc = din("swc", [NE, RT, D])
    swb_bc = din("swb_bc", [2, NE])
    mask_bc = din("mask_bc", [128, NCORE])
    bselc = din("bselc", [2, 1])
    Fr = din("Fr", [D, D], BF16); Fi = din("Fi", [D, D], BF16)
    identc = din("identc", [128, 128], BF16)
    RevA = din("RevA", [128, 128], BF16); RevAn = din("RevAn", [128, 128], BF16)
    Pick0 = din("Pick0", [128, 128], BF16); Pick0n = din("Pick0n", [128, 128], BF16)
    onesr = din("onesr", [1, 128], BF16)
    f1024 = din("f1024", [128, 16], BF16)

    y = nc.dram_tensor("y", [T, D], F32, kind="ExternalOutput").ap()

    wq_bf = nc.dram_tensor("wq_bf", [D, D], BF16).ap()
    wk_bf = nc.dram_tensor("wk_bf", [D, D], BF16).ap()
    wv_bf = nc.dram_tensor("wv_bf", [D, D], BF16).ap()
    ws_bf = nc.dram_tensor("ws_bf", [D, D], BF16).ap()
    wt_bf = nc.dram_tensor("wt_bf", [D, D], BF16).ap()
    w3_bf = nc.dram_tensor("w3_bf", [D, D], BF16).ap()
    w1_bf = nc.dram_tensor("w1_bf", [2 * D, 4 * D], BF16).ap()
    w2_bf = nc.dram_tensor("w2_bf", [4 * D, 2 * D], BF16).ap()
    cc_in = nc.dram_tensor("cc_in", [1, 2320], F32)
    cc_out = nc.dram_tensor("cc_out", [NCORE, 2320], F32)
    ta_sp = nc.dram_tensor("ta_sp", [17 * 128, T], BF16).ap()
    zs_sp = nc.dram_tensor("zs_sp", [17 * 128, T], BF16).ap()
    t1_sp = nc.dram_tensor("t1_sp", [D, T], BF16).ap()
    ff_sp = nc.dram_tensor("ff_sp", [D, T], BF16).ap()

    tap_out = {}

    def tap(name, shape, dt=F32):
        if name in taps:
            tap_out[name] = nc.dram_tensor("tap_" + name, shape, dt,
                                           kind="ExternalOutput").ap()
            return True
        return False

    gq = p.sem("gq"); sq = p.sem("sq"); mm = p.sem("mm"); dv = p.sem("dv")
    sc = p.sem("sc"); cc = p.sem("cc"); st = p.sem("st")
    c1 = p.sem("c1"); c2 = p.sem("c2"); lq = p.sem("lq")
    wp = [p.sem(f"wp{i}") for i in range(4)]
    w1s = [p.sem(f"w1s{i}") for i in range(4)]
    w2s = [p.sem(f"w2s{i}") for i in range(4)]
    w3s = [p.sem(f"w3s{i}") for i in range(4)]
    fbs = [p.sem(f"fbs{i}") for i in range(2)]
    fjs = [p.sem(f"fjs{i}") for i in range(2)]
    fis = [p.sem(f"fis{i}") for i in range(2)]
    sws = [p.sem(f"sws{i}") for i in range(2)]
    ci_s = p.sem("ci_s"); gl_s = p.sem("gl_s"); sp_s = p.sem("sp_s")
    ta_s = p.sem("ta_s"); zs_s = p.sem("zs_s"); t1_s = p.sem("t1_s")
    ff_s = p.sem("ff_s"); xr_s = p.sem("xr_s")
    pc_q = p.sem("pc_q"); pc_k = p.sem("pc_k"); pc_v = p.sem("pc_v")
    pc_s = p.sem("pc_s"); pc_t = p.sem("pc_t"); pc_3 = p.sem("pc_3")
    pc_1 = p.sem("pc_1"); pc_2 = p.sem("pc_2")
    p.auto_self = {"vector": dv, "scalar": sc}

    def _stop(name):
        if stop == name:
            p.emit("sync", lambda e: None, waits=[p.tok(st), p.tok(dv),
                                                  p.tok(sc), p.tok(gq),
                                                  p.tok(sq)])
            p.finalize()
            return True
        return False


    R1 = stack.enter_context(nc.sbuf_tensor("R1", [128, 16384], BF16))
    R2 = stack.enter_context(nc.sbuf_tensor("R2", [128, 16384], BF16))
    R3 = stack.enter_context(nc.sbuf_tensor("R3", [128, 16384], BF16))
    R4 = stack.enter_context(nc.sbuf_tensor("R4", [128, 16384], BF16))
    R5 = stack.enter_context(nc.sbuf_tensor("R5", [128, 8192], BF16))
    RE = stack.enter_context(nc.sbuf_tensor("RE", [128, 17920], BF16))
    FA = stack.enter_context(nc.sbuf_tensor("FA", [128, 4608], F32))

    def vw(arena, off, n):
        return arena.ap()[:, off:off + n]

    sbt = lambda nm, sh, dt: stack.enter_context(nc.sbuf_tensor(nm, sh, dt))
    b1cb = sbt("b1cb", [128, 64], F32)
    b2cb = sbt("b2cb", [128, 32], F32)
    b3cb = sbt("b3cb", [128, 16], F32)
    fw1b = sbt("fw1b", [128, NE * 128], BF16)
    fw2b = sbt("fw2b", [128, NE * 128], BF16)
    w2gb = sbt("w2gb", [128, NE * 128], BF16)
    fb1b = sbt("fb1b", [128, NE], F32)
    fb2b = sbt("fb2b", [NE, 128], BF16)
    maskb = sbt("maskb", [128, NCORE], F32)
    bselb = sbt("bselb", [2, 1], BF16)
    identb = sbt("identb", [128, 128], BF16)
    RevAb = sbt("RevAb", [128, 128], BF16)
    RevAnb = sbt("RevAnb", [128, 128], BF16)
    Pick0b = sbt("Pick0b", [128, 128], BF16)
    Pick0nb = sbt("Pick0nb", [128, 128], BF16)
    onerb = sbt("onerb", [1, 128], BF16)
    F1B = sbt("F1B", [128, 16], BF16)
    swbb = sbt("swbb", [2, NE], F32)
    COLL = sbt("COLL", [128, 18], F32)
    CARR = sbt("CARR", [128, 2], F32)
    OFFC = sbt("OFFC", [128, 18], F32)
    OFFG = sbt("OFFG", [128, 18 * NCORE], F32)
    ST1 = sbt("ST1", [128, 32], F32)
    ST2B = sbt("ST2B", [128, 16], BF16)
    ONE1 = sbt("ONE1", [128, 1], BF16)
    RLOG = sbt("RLOG", [16, 1], F32)
    LG = sbt("LG", [2, 8 * NCORE], F32)
    GT = sbt("GT", [2, NE], F32)
    GTt1 = sbt("GTt1", [2, NE], F32)
    GTt2 = sbt("GTt2", [2, NE], F32)
    GTB = sbt("GTB", [2, NE], BF16)
    GRED = sbt("GRED", [2, 1], F32)
    GRED2 = sbt("GRED2", [128, 8], F32)
    EPSC = sbt("EPSC", [128, 2], F32)
    MGC = sbt("MGC", [NE, 1], BF16)
    MGR = sbt("MGR", [1, NE], BF16)
    GBC = sbt("GBC", [128, NE], F32)
    B2G = sbt("B2G", [128, 1], F32)

    LNS = FA.ap()[:, 0:2048]
    LNT = FA.ap()[:, 2048:4096]
    SLOT = [FA.ap()[:, i * 512:(i + 1) * 512] for i in range(8)]
    TSG = FA.ap()[:, 2048:3072]
    ZMR = FA.ap()[:, 3072:4096]
    SCAN = FA.ap()[:, 4096:4608]
    OUTB = FA.ap()[:, 0:2048]
    XRL = [FA.ap()[:, 2048:4096]]

    ps = [stack.enter_context(nc.psum_tensor(f"ps{i}", [128, TH], F32))
          for i in range(6)]
    pt = stack.enter_context(nc.psum_tensor("pt", [128, TH], BF16))
    px = stack.enter_context(nc.psum_tensor("px", [128, TH], F32))

    xn = [vw(R1, i * 1024, 1024) for i in range(DT)]
    qv = [vw(R2, i * 1024, 1024) for i in range(DT)]
    kv = [vw(R3, i * 1024, 1024) for i in range(DT)]
    vv = [vw(R4, i * 1024, 1024) for i in range(DT)]
    XH = {}
    for i in range(9):
        XH[(0, i)] = vw(RE, i * 1024, 1024)
    for i in range(8):
        XH[(1, i)] = vw(RE, (9 + i) * 1024, 1024)
    TA = {}
    for i in range(9):
        TA[(0, i)] = vw(R1, i * 1024, 1024)
    for i in range(7):
        TA[(1, i)] = vw(R1, (9 + i) * 1024, 1024)
    TA[(1, 7)] = vw(R3, 0, 1024)
    XNTM = [vw(RE, 0, 2048), vw(RE, 2048, 2048)]
    RPRD = vw(RE, 0, 2048)
    XRT = [vw(RE, 4096 + i * 2048, 2048) for i in range(4)]
    SWB = [vw(RE, 12288 + i * 2048, 2048) for i in range(2)]
    HB = [vw(R1, i * 512, 512) for i in range(32)] + \
         [vw(R3, i * 512, 512) for i in range(32)]
    MIR = [vw(RE, i * 512, 512) for i in range(16)]
    TAH = {}
    _ti = 0
    for _pl in range(2):
        for _i in range(9 if _pl == 0 else 8):
            TAH[(_pl, _i)] = vw(R2, _ti * 512, 512)
            _ti += 1
    ZSH = {}
    _ti = 0
    for _pl in range(2):
        for _i in range(9 if _pl == 0 else 8):
            ZSH[(_pl, _i)] = vw(RE, _ti * 512, 512)
            _ti += 1
    T1H = [vw(R2, i * 512, 512) for i in range(16)]
    ZB = [vw(R2, 8192 + i * 512, 512) for i in range(16)] + \
         [vw(RE, 8704 + i * 512, 512) for i in range(16)]
    AOS = [vw(R2, i * 512, 512) for i in range(16)]
    FFH = [vw(R4, i * 512, 512) for i in range(16)]
    W1B = [vw(R4, i * 4096, 4096) for i in range(4)]
    W2B = [vw(R4, i * 4096, 4096) for i in range(4)]
    WBpr = [vw(R5, i * 2048, 2048) for i in range(4)]
    WB3 = [vw(R5, i * 2048, 2048) for i in range(4)]
    HSTs = [vw(R5, 4096 + i * 512, 512) for i in range(2)]
    FB = [vw(R5, i * 4096, 4096) for i in range(2)]
    FBj = [vw(R5, i * 4096, 4096) for i in range(2)]
    FBI = [vw(R5, i * 4096, 4096) for i in range(2)]

    for dst, src in [(b1cb, b1c), (b2cb, b2c), (b3cb, b3c), (fb1b, fb1c),
                     (maskb, mask_bc), (swbb, swb_bc)]:
        p.dma("sync", dst.ap()[:, :], src[:, :], c1)
    for dst, src in [(identb, identc), (RevAb, RevA), (RevAnb, RevAn),
                     (Pick0b, Pick0), (Pick0nb, Pick0n), (onerb, onesr)]:
        p.dma("sync", dst.ap()[:, :], src[:, :], c1)
    p.dma("gpsimd", fw1b.ap().rearrange("p (e c) -> p e c", c=128),
          fw1t.rearrange("e p c -> p e c"), c2)
    p.dma("gpsimd", fw2b.ap().rearrange("p (e c) -> p e c", c=128),
          fw2t.rearrange("e p c -> p e c"), c2)
    p.dma("gpsimd", fb2b.ap()[:, :], fb2[:, :], c2)
    p.dma("gpsimd", bselb.ap()[:, :], bselc[:, :], c2)
    _pc_hist = []

    def _pc_dma(dst_ap, src_ap, sem8):
        w = []
        if len(_pc_hist) >= 6:
            w = [_pc_hist[-6]]
        p.dma("gpsimd", dst_ap, src_ap, sem8, waits=w)
        _pc_hist.append(p.tok(sem8))

    for dst8, src8, sem8 in [(wq_bf, Wq, pc_q), (wk_bf, Wk, pc_k),
                             (wv_bf, Wv, pc_v), (ws_bf, Ws, pc_s),
                             (wt_bf, Wt, pc_t), (w3_bf, w3t, pc_3)]:
        for ch in range(2):
            _pc_dma(dst8[ch * 1024:(ch + 1) * 1024, :],
                    src8[ch * 1024:(ch + 1) * 1024, :], sem8)
    for ch in range(16):
        _pc_dma(w1_bf[ch * 256:(ch + 1) * 256, :],
                W1[ch * 256:(ch + 1) * 256, :], pc_1)
    for ch in range(16):
        _pc_dma(w2_bf[ch * 512:(ch + 1) * 512, :],
                W2[ch * 512:(ch + 1) * 512, :], pc_2)
    t_pc = {id(Wq): p.tok(pc_q), id(Wk): p.tok(pc_k), id(Wv): p.tok(pc_v),
            id(Ws): p.tok(pc_s), id(Wt): p.tok(pc_t)}
    bf_of = {id(Wq): wq_bf, id(Wk): wk_bf, id(Wv): wv_bf, id(Ws): ws_bf,
             id(Wt): wt_bf}
    t_pc3 = p.tok(pc_3); t_pc1 = p.tok(pc_1); t_pc2 = p.tok(pc_2)
    p.dma("sync", F1B.ap()[:, :], f1024[:, :], c1)
    t_cs = p.tok(c1)
    t_cg = p.tok(c2)
    p.emit("vector", lambda e: e.memset(COLL.ap()[:, :], 0.0), inc=dv)
    p.emit("vector", lambda e: e.memset(ONE1.ap()[:, :], 1.0), inc=dv)
    p.emit("vector", lambda e: e.memset(EPSC.ap()[:, 0:1], 1e-5), inc=dv)
    p.emit("vector", lambda e: e.memset(EPSC.ap()[:, 1:2], NINV), inc=dv)
    t_zero = p.tok(dv)

    ln_done = []
    tp_evac = []
    ln_idx = [0]

    def emit_ln_tile(src_rows, dst, dst_waits=()):
        par = (ln_idx[0] % 2) * 4
        ln_idx[0] += 1
        g2 = GRED2.ap()
        w = [t_zero, t_cs]
        if ln_done:
            w.append((dv, ln_done[-1]))
        p.dma("sync", LNS, src_rows, lq, waits=w)
        t_l = p.tok(lq)
        p.emit("vector", lambda e: e.reduce_sum(g2[:, par:par + 1], LNS,
                                                axis=AX.X), waits=[t_l], inc=dv)
        p.emit("vector", lambda e: e.tensor_scalar(
            g2[:, par + 1:par + 2], g2[:, par:par + 1], -NINV, None, ALU.mult),
            waits=[p.tok(dv)], inc=dv)
        t_m = p.tok(dv)
        p.emit("scalar", lambda e: e.activation(
            LNT, LNS, AF.Square, bias=g2[:, par + 1:par + 2],
            accum_out=g2[:, par + 2:par + 3]), waits=[t_m], inc=sc)
        t_s = p.tok(sc)
        p.emit("scalar", lambda e: e.activation(
            g2[:, par + 3:par + 4], g2[:, par + 2:par + 3], AF.Sqrt,
            bias=EPSC.ap()[:, 0:1], scale=EPSC.ap()[:, 1:2]),
            waits=[t_s], inc=sc)
        t_sq = p.tok(sc)
        p.emit("vector", lambda e: e.reciprocal(
            g2[:, par + 3:par + 4], g2[:, par + 3:par + 4]),
            waits=[t_sq], inc=dv)
        t_r = p.tok(dv)
        p.emit("vector", lambda e, d=dst: e.tensor_scalar(
            d, LNS, g2[:, par + 1:par + 2], g2[:, par + 3:par + 4],
            ALU.add, ALU.mult), waits=[t_r] + list(dst_waits), inc=dv)
        ln_done.append(dv.count)
        return p.tok(dv)

    tp_mm = []
    for i in range(8):
        dstbuf = XNTM[i % 2]
        dw = []
        if i >= 2:
            dw.append((mm, tp_mm[4 * (i - 2) + 3]))
        t_n = emit_ln_tile(x_my[i * 128:(i + 1) * 128, :], dstbuf, dst_waits=dw)
        for g in range(4):
            w = [t_n, t_cs]
            if tp_evac:
                w.append((dv, tp_evac[-1]))
            for j in range(4):
                dt_ = g * 4 + j
                p.emit("tensor", lambda e, s2=dstbuf, jj=j, d2=dt_:
                       e.transpose(pt[:, jj * 128:(jj + 1) * 128],
                                   s2[:, d2 * 128:(d2 + 1) * 128],
                                   identb.ap()[:, :]),
                       waits=w if j == 0 else (), inc=mm if j == 3 else None)
            t_t = p.tok(mm)
            for j in range(4):
                dt_ = g * 4 + j
                p.emit("vector", lambda e, jj=j, d2=dt_, i2=i: e.tensor_copy(
                    xn[d2][:, i2 * 128:(i2 + 1) * 128],
                    pt[:, jj * 128:(jj + 1) * 128]),
                    waits=[t_t] if j == 0 else (), inc=dv if j == 3 else None)
            tp_evac.append(dv.count)
            tp_mm.append(mm.count)
    t_xn = p.tok(dv)
    if tap("xn", [DT * 128, T], BF16):
        for i in range(DT):
            p.dma("sync", tap_out["xn"][i * 128:(i + 1) * 128, :], xn[i], st,
                  waits=[t_xn])

    if _stop("p1"):
        return nc, stack, sorted(taps & set(tap_out))
    for i in range(4):
        emit_ln_tile(x_rt[i * 128:(i + 1) * 128, :], XRT[i])
    t_rt = p.tok(dv)

    r_done = []
    for e_i in range(NE):
        for st_i in range(2):
            sl = (e_i * 2 + st_i) % 2
            buf = SWB[sl]
            w = [t_cg, t_rt]
            if len(r_done) >= 2:
                w.append((dv, r_done[-2]))
            p.dma("gpsimd", buf, swc[e_i, st_i * 128:(st_i + 1) * 128, :],
                  sws[sl], waits=w)
            t_w = p.tok(sws[sl])
            for b_i in range(2):
                col = b_i * 16 + e_i * 2 + st_i
                p.emit("vector", lambda e, bu=buf, b2=b_i, s2=st_i:
                       e.tensor_mul(RPRD, bu, XRT[b2 * 2 + s2]),
                       waits=[t_w], inc=dv)
                p.emit("vector", lambda e, c2=col: e.reduce_sum(
                    ST1.ap()[:, c2:c2 + 1], RPRD, axis=AX.X),
                    waits=[p.tok(dv)], inc=dv)
            r_done.append(dv.count)
    for b_i in range(2):
        for e_i in range(NE):
            c0 = b_i * 16 + e_i * 2
            o = b_i * 8 + e_i
            p.emit("vector", lambda e, c2=c0, o2=o: e.tensor_add(
                ST2B.ap()[:, o2:o2 + 1], ST1.ap()[:, c2:c2 + 1],
                ST1.ap()[:, c2 + 1:c2 + 2]), waits=[p.tok(dv)], inc=dv)
    t_st2 = p.tok(dv)
    p.emit("tensor", lambda e: e.matmul(px[0:16, 0:1], ST2B.ap()[:, :],
                                        ONE1.ap()[:, :], start=True, stop=True),
           waits=[t_st2], inc=mm)
    t_rl = p.tok(mm)
    p.emit("vector", lambda e: e.tensor_copy(RLOG.ap()[:, :], px[0:16, 0:1]),
           waits=[t_rl], inc=dv)
    t_rlog = p.tok(dv)

    if _stop("p2"):
        return nc, stack, sorted(taps & set(tap_out))
    prst = {"blocks": 0, "block_mm": [], "evac": [], "bank": {}}

    def emit_proj(Wap, dst, rhs, first_waits=()):
        for mg in range(16):
            sl = mg % 4
            buf = WBpr[sl]
            w = list(first_waits) + [t_pc[id(Wap)]]
            if prst["blocks"] >= 4:
                w.append((mm, prst["block_mm"][-4]))
            p.dma("sync", buf.rearrange("p (kc c) -> p kc c", c=128),
                  bf_of[id(Wap)].rearrange("(kc p) m -> p kc m", p=128)[
                      :, :, mg * 128:(mg + 1) * 128], wp[sl], waits=w)
            t_w = p.tok(wp[sl])
            mtile = mg
            for half in range(2):
                bidx = (mtile * 2 + half) % 6
                bank = ps[bidx]
                mmw = [t_w]
                if bidx in prst["bank"]:
                    mmw.append((dv, prst["bank"][bidx]))
                for kc in range(16):
                    lhsT = buf[:, kc * 128:(kc + 1) * 128]
                    p.emit("tensor", lambda e, o=bank, l=lhsT, k2=kc, h=half:
                           e.matmul(o[:, :], l,
                                    rhs[k2][:, h * TH:(h + 1) * TH],
                                    start=(k2 == 0), stop=(k2 == 15)),
                           waits=mmw if kc == 0 else (),
                           inc=mm if kc == 15 else None)
                t_mmv = p.tok(mm)
                p.emit("vector", lambda e, d=dst[mtile], h=half, b2=bank:
                       e.tensor_copy(d[:, h * TH:(h + 1) * TH], b2[:, :]),
                       waits=[t_mmv], inc=dv)
                prst["evac"].append(dv.count)
                prst["bank"][bidx] = dv.count
            prst["block_mm"].append(mm.count)
            prst["blocks"] += 1
        return p.tok(dv)

    t_q = emit_proj(Wq, qv, xn, first_waits=[t_xn])
    if tap("qf", [D, T], BF16):
        for i in range(DT):
            p.dma("sync", tap_out["qf"][i * 128:(i + 1) * 128, :], qv[i], st,
                  waits=[t_q])
    t_k = emit_proj(Wk, kv, xn)
    t_v = emit_proj(Wv, vv, xn)

    if _stop("p3"):
        return nc, stack, sorted(taps & set(tap_out))
    def dma_fcols(fbuf, col0, n, wl, sem):
        if n == 1:
            return
        p.dma("sync", fbuf[:, 0:2048].rearrange("p (kc c) -> p kc c", c=128),
              Fr.rearrange("(kc p) m -> p kc m", p=128)[:, :, col0:col0 + n],
              sem, waits=wl)
        p.dma("sync", fbuf[:, 2048:4096].rearrange("p (kc c) -> p kc c", c=128),
              Fi.rearrange("(kc p) m -> p kc m", p=128)[:, :, col0:col0 + n],
              sem)

    p.emit("vector", lambda e: e.memset(XH[(0, NF)][:, :], 0.0), inc=dv)
    ps_free = []
    fblk_mm = []
    for m in range(NF + 1):
        m1 = (m == NF)
        fbuf = FB[m % 2]
        w = [t_cs]
        if m >= 2:
            w.append((mm, fblk_mm[-2]))
        if m < 2:
            w += [t_v]
        dma_fcols(fbuf, m * 128 if not m1 else 1024, 1 if m1 else 128, w,
                  fbs[m % 2])
        t_f = p.tok(fbs[m % 2])
        pr = slice(0, 1) if m1 else slice(0, 128)
        nplanes = 1 if m1 else 2
        for half in range(2):
            hs = slice(half * TH, (half + 1) * TH)
            first = True
            for ri, rhs in ((0, qv), (1, kv), (2, vv)):
                for pl in range(nplanes):
                    bank = ps[ri * 2 + pl]
                    mmw = [t_f, t_k, t_v]
                    if ps_free:
                        mmw.append((dv, ps_free[-1]))
                    for kc in range(16):
                        base = pl * 2048 + kc * 128
                        l = F1B.ap()[:, kc:kc + 1] if m1 else fbuf[:, base:base + 128]
                        p.emit("tensor", lambda e, o=bank, l2=l, k2=kc, r=rhs,
                               h2=hs, pr2=pr:
                               e.matmul(o[pr2, :], l2, r[k2][:, h2],
                                        start=(k2 == 0), stop=(k2 == 15)),
                               waits=mmw if (kc == 0 and first) else (),
                               inc=mm if kc == 15 else None)
                        first = False
            t_qkv = p.tok(mm)
            banks_used = [ri * 2 + pl for ri in range(3) for pl in range(nplanes)]
            for n_i, i in enumerate(banks_used):
                p.emit("vector", lambda e, i2=i, pr2=pr: e.tensor_copy(
                    SLOT[i2][pr2, :], ps[i2][pr2, :]),
                    waits=[t_qkv] if n_i == 0 else (),
                    inc=dv if n_i == len(banks_used) - 1 else None)
            t_ev = p.tok(dv)
            if m == 0 and half == 0 and tap("fqs", [2 * 128, TH]):
                p.dma("sync", tap_out["fqs"][0:128, :], SLOT[0], st,
                      waits=[t_ev])
                p.dma("sync", tap_out["fqs"][128:256, :], SLOT[1], st)
            for pl in range(nplanes):
                bank = ps[pl]
                for kc in range(16):
                    base = pl * 2048 + kc * 128
                    l = F1B.ap()[:, kc:kc + 1] if m1 else fbuf[:, base:base + 128]
                    p.emit("tensor", lambda e, o=bank, l2=l, k2=kc, h2=hs, pr2=pr:
                           e.matmul(o[pr2, :], l2, xn[k2][:, h2],
                                    start=(k2 == 0), stop=(k2 == 15)),
                           waits=[t_ev] if (kc == 0 and pl == 0) else (),
                           inc=mm if kc == 15 else None)
            t_xf = p.tok(mm)

            sQr, sQi, sKr, sKi, sVr, sVi, sT1, sT2 = SLOT

            def vop(fn, waits=(), inc=dv):
                p.emit("vector", fn, waits=list(waits) + [p.tok(dv)], inc=inc)

            if not m1:
                vop(lambda e, pr=pr: e.tensor_mul(sT1[pr, :], sQr[pr, :], sKr[pr, :]))
                vop(lambda e, pr=pr: e.tensor_mul(sT2[pr, :], sQi[pr, :], sKi[pr, :]))
                vop(lambda e, pr=pr: e.tensor_add(sT1[pr, :], sT1[pr, :], sT2[pr, :]))
                vop(lambda e, pr=pr: e.tensor_mul(sT2[pr, :], sQr[pr, :], sKi[pr, :]))
                vop(lambda e, pr=pr: e.tensor_mul(sKr[pr, :], sQi[pr, :], sKr[pr, :]))
                vop(lambda e, pr=pr: e.tensor_sub(sT2[pr, :], sT2[pr, :], sKr[pr, :]))
                vop(lambda e, pr=pr: e.tensor_mul(sQr[pr, :], sT1[pr, :], sVr[pr, :]))
                vop(lambda e, pr=pr: e.tensor_mul(sKr[pr, :], sT2[pr, :], sVi[pr, :]))
                vop(lambda e, pr=pr: e.tensor_mul(sQi[pr, :], sT1[pr, :], sVi[pr, :]))
                vop(lambda e, pr=pr: e.tensor_mul(sKi[pr, :], sT2[pr, :], sVr[pr, :]))
                vop(lambda e, pr=pr: e.tensor_sub(sQr[pr, :], sQr[pr, :], sKr[pr, :]))
                vop(lambda e, pr=pr: e.tensor_add(sQi[pr, :], sQi[pr, :], sKi[pr, :]))
            else:
                vop(lambda e, pr=pr: e.tensor_mul(sQr[pr, :], sQr[pr, :], sKr[pr, :]))
                vop(lambda e, pr=pr: e.tensor_mul(sQr[pr, :], sQr[pr, :], sVr[pr, :]))
            vop(lambda e, pr=pr: e.tensor_copy(sKr[pr, :], ps[0][pr, :]), waits=[t_xf])
            if not m1:
                vop(lambda e, pr=pr: e.tensor_copy(sKi[pr, :], ps[1][pr, :]))
            ps_free.append(dv.count)
            if m == 0 and half == 0 and tap("attn0", [2 * 128, TH]):
                p.dma("sync", tap_out["attn0"][0:128, :], SLOT[0], st,
                      waits=[p.tok(dv)])
                p.dma("sync", tap_out["attn0"][128:256, :], SLOT[1], st)
            for pl in range(nplanes):
                asl = (sQr, sQi)[pl]
                xsl = (sKr, sKi)[pl]
                vop(lambda e, a=asl, pr=pr: e.tensor_tensor_scan(
                    SCAN[pr, :], a[pr, :], a[pr, :], 0.0, ALU.add, ALU.bypass))
                if half == 0:
                    vop(lambda e, pl2=pl, pr=pr: e.tensor_copy(
                        CARR.ap()[pr, pl2:pl2 + 1], SCAN[pr, TH - 1:TH]),
                        waits=[p.tok(dv)])
                    vop(lambda e, pl2=pl, m2=m, x2=xsl, pr=pr: e.tensor_add(
                        XH[(pl2, m2)][pr, 0:TH], SCAN[pr, :], x2[pr, :]))
                else:
                    vop(lambda e, pl2=pl, pr=pr: e.tensor_scalar(
                        SCAN[pr, :], SCAN[pr, :], CARR.ap()[pr, pl2:pl2 + 1],
                        None, ALU.add), waits=[p.tok(dv)])
                    vop(lambda e, pl2=pl, m2=m, x2=xsl, pr=pr: e.tensor_add(
                        XH[(pl2, m2)][pr, TH:T], SCAN[pr, :], x2[pr, :]))
                    vop(lambda e, pl2=pl, m2=m, pr=pr: e.tensor_copy(
                        COLL.ap()[pr, m2 * 2 + pl2:m2 * 2 + pl2 + 1],
                        SCAN[pr, TH - 1:TH]), waits=[p.tok(dv)])
        fblk_mm.append(mm.count)
    t_p4 = p.tok(dv)
    if tap("xh", [17 * 128, T], BF16):
        _ti = 0
        for pl in range(2):
            for i in range(9 if pl == 0 else 8):
                p.dma("sync", tap_out["xh"][_ti * 128:(_ti + 1) * 128, :],
                      XH[(pl, i)], st, waits=[t_p4])
                _ti += 1
    if tap("coll", [128, 18]):
        p.dma("sync", tap_out["coll"][:, :], COLL.ap()[:, :], st, waits=[t_p4])

    if _stop("p4"):
        return nc, stack, sorted(taps & set(tap_out))
    p.dma("gpsimd",
          cc_in.ap()[0:1, 0:2304].rearrange("one (p c) -> (one p) c", p=128),
          COLL.ap()[:, :], ci_s, waits=[t_p4])
    p.dma("gpsimd",
          cc_in.ap()[0:1, 2304:2320].rearrange("one (p c) -> (one p) c", p=16),
          RLOG.ap()[:, :], ci_s, waits=[t_rlog])
    t_ci = p.tok(ci_s)
    p.emit("gpsimd", lambda e: e.collective_compute(
        "AllGather", ALU.bypass, replica_groups=[list(range(NCORE))],
        ins=[cc_in.ap()], outs=[cc_out.ap()]), waits=[t_ci], inc=cc)
    t_cc = p.tok(cc)

    t_s1 = emit_proj(Ws, qv, xn, first_waits=[t_p4])
    t_t1 = emit_proj(Wt, kv, xn)
    for i in range(DT):
        p.dma("sync", t1_sp[i * 128:(i + 1) * 128, :], kv[i], sp_s,
              waits=[t_t1] if i == 0 else ())
    t_t1sp = p.tok(sp_s)

    if _stop("p5"):
        return nc, stack, sorted(taps & set(tap_out))
    p.dma("sync", OFFG.ap().rearrange("p (c k) -> p c k", k=18),
          cc_out.ap()[:, 0:2304].rearrange("c (p k) -> p c k", p=128),
          gl_s, waits=[t_cc])
    p.dma("sync", LG.ap().rearrange("p (c e2) -> p c e2", e2=8),
          cc_out.ap()[:, 2304:2320].rearrange("c (b e2) -> b c e2", b=2),
          gl_s, waits=[t_cc])
    t_gl = p.tok(gl_s)
    p.emit("vector", lambda e: e.memset(OFFC.ap()[:, :], 0.0), inc=dv)
    for c_i in range(NCORE):
        p.emit("vector", lambda e, c2=c_i: e.scalar_tensor_tensor(
            OFFC.ap()[:, :], OFFG.ap()[:, c2 * 18:(c2 + 1) * 18],
            maskb.ap()[:, c2:c2 + 1], OFFC.ap()[:, :], ALU.mult, ALU.add),
            waits=[t_gl, p.tok(dv)], inc=dv)
    t_off = p.tok(dv)

    p.emit("vector", lambda e: e.tensor_add(
        GT.ap()[:, :], LG.ap()[:, 0:8], LG.ap()[:, 8:16]), waits=[t_gl], inc=dv)
    for c_i in range(2, NCORE):
        p.emit("vector", lambda e, c2=c_i: e.tensor_add(
            GT.ap()[:, :], GT.ap()[:, :], LG.ap()[:, c2 * 8:(c2 + 1) * 8]),
            waits=[p.tok(dv)], inc=dv)
    p.emit("vector", lambda e: e.tensor_add(GT.ap()[:, :], GT.ap()[:, :],
                                            swbb.ap()[:, :]),
           waits=[p.tok(dv)], inc=dv)
    t_lgt = p.tok(dv)
    if tap("logits", [2, 8]):
        p.dma("sync", tap_out["logits"][:, :], GT.ap()[:, :], st, waits=[t_lgt])
    p.emit("vector", lambda e: e.reduce_max(GRED.ap()[:, :], GT.ap()[:, :],
                                            axis=AX.X, negate=True),
           waits=[p.tok(dv)], inc=dv)
    t_gm = p.tok(dv)
    p.emit("scalar", lambda e: e.activation(GTt1.ap()[:, :], GT.ap()[:, :],
                                            AF.Exp, bias=GRED.ap()[:, 0:1]),
           waits=[t_gm], inc=sc)
    t_ge = p.tok(sc)
    p.emit("vector", lambda e: e.reduce_sum(GRED.ap()[:, :], GTt1.ap()[:, :],
                                            axis=AX.X), waits=[t_ge], inc=dv)
    p.emit("vector", lambda e: e.reciprocal(GRED.ap()[:, :], GRED.ap()[:, :]),
           waits=[p.tok(dv)], inc=dv)
    p.emit("vector", lambda e: e.tensor_scalar(GTt1.ap()[:, :], GTt1.ap()[:, :],
                                               GRED.ap()[:, 0:1], None,
                                               ALU.mult), waits=[p.tok(dv)], inc=dv)
    p.emit("vector", lambda e: e.reduce_max(GRED.ap()[:, :], GTt1.ap()[:, :],
                                            axis=AX.X), waits=[p.tok(dv)], inc=dv)
    p.emit("vector", lambda e: e.tensor_scalar(GTt2.ap()[:, :], GTt1.ap()[:, :],
                                               GRED.ap()[:, 0:1], None,
                                               ALU.is_lt), waits=[p.tok(dv)], inc=dv)
    p.emit("vector", lambda e: e.tensor_mul(GTt2.ap()[:, :], GTt2.ap()[:, :],
                                            GTt1.ap()[:, :]), waits=[p.tok(dv)], inc=dv)
    p.emit("vector", lambda e: e.reduce_max(GRED.ap()[:, :], GTt2.ap()[:, :],
                                            axis=AX.X), waits=[p.tok(dv)], inc=dv)
    p.emit("vector", lambda e: e.tensor_scalar(GTt2.ap()[:, :], GTt1.ap()[:, :],
                                               GRED.ap()[:, 0:1], None,
                                               ALU.is_ge), waits=[p.tok(dv)], inc=dv)
    p.emit("vector", lambda e: e.tensor_mul(GT.ap()[:, :], GTt1.ap()[:, :],
                                            GTt2.ap()[:, :]), waits=[p.tok(dv)], inc=dv)
    p.emit("vector", lambda e: e.tensor_copy(GTB.ap()[:, :], GT.ap()[:, :]),
           waits=[p.tok(dv)], inc=dv)
    t_gt = p.tok(dv)
    if tap("gates", [2, 8]):
        p.dma("sync", tap_out["gates"][:, :], GT.ap()[:, :], st, waits=[t_gt])
    p.emit("tensor", lambda e: e.matmul(px[0:NE, 0:1], GTB.ap()[:, :],
                                        bselb.ap()[:, :], start=True, stop=True),
           waits=[t_gt, t_cg], inc=mm)
    t_a = p.tok(mm)
    p.emit("vector", lambda e: e.tensor_copy(MGC.ap()[:, :], px[0:NE, 0:1]),
           waits=[t_a], inc=dv)
    t_b = p.tok(dv)
    p.emit("tensor", lambda e: e.matmul(px[0:128, 1:2], fb2b.ap()[:, :],
                                        MGC.ap()[:, :], start=True, stop=True),
           waits=[t_b], inc=mm)
    p.emit("tensor", lambda e: e.matmul(px[0:1, 2:10], bselb.ap()[:, :],
                                        GTB.ap()[:, :], start=True, stop=True),
           inc=mm)
    t_c = p.tok(mm)
    p.emit("vector", lambda e: e.tensor_copy(B2G.ap()[:, :], px[0:128, 1:2]),
           waits=[t_c], inc=dv)
    p.emit("vector", lambda e: e.tensor_copy(MGR.ap()[:, :], px[0:1, 2:10]),
           inc=dv)
    t_d = p.tok(dv)
    p.emit("tensor", lambda e: e.matmul(px[0:128, 16:24], onerb.ap()[:, :],
                                        MGR.ap()[:, :], start=True, stop=True),
           waits=[t_d], inc=mm)
    t_e2 = p.tok(mm)
    p.emit("vector", lambda e: e.tensor_copy(GBC.ap()[:, :], px[0:128, 16:24]),
           waits=[t_e2], inc=dv)
    t_gbc = p.tok(dv)
    for e_i in range(NE):
        p.emit("vector", lambda e, e3=e_i: e.tensor_scalar(
            w2gb.ap()[:, e3 * 128:(e3 + 1) * 128],
            fw2b.ap()[:, e3 * 128:(e3 + 1) * 128],
            GBC.ap()[:, e3:e3 + 1], None, ALU.mult), waits=[t_gbc], inc=dv)
    t_w2g = p.tok(dv)

    ob_guard = {}
    for band in range(NHEAD):
        for half in range(2):
            hs = slice(half * TH, (half + 1) * TH)
            obank = ps[4 + half]
            gelu_sc = []

            def mm1(e_i):
                hbank = ps[e_i % 2]
                w = [t_w2g]
                if e_i >= 2:
                    w.append((mm, mm2_mm[e_i - 2]))
                p.emit("tensor", lambda e, b2=band, e3=e_i, h2=hs, hb=hbank:
                       e.matmul(hb[:, :],
                                fw1b.ap()[:, e3 * 128:(e3 + 1) * 128],
                                xn[b2][:, h2], start=True, stop=True),
                       waits=w, inc=mm)
                mm1_mm.append(mm.count)

            mm1_mm = []
            mm2_mm = []
            mm1(0)
            mm1(1)
            for e_i in range(NE):
                hbank = ps[e_i % 2]
                hst = HSTs[e_i % 2]
                w = [(mm, mm1_mm[e_i])]
                if e_i >= 2:
                    w.append((mm, mm2_mm[e_i - 2]))
                p.emit("scalar", lambda e, e3=e_i, hb=hbank, h4=hst:
                       e.activation(h4, hb[:, :], AF.Gelu,
                                    bias=fb1b.ap()[:, e3:e3 + 1]),
                       waits=w, inc=sc)
                gelu_sc.append(sc.count)
                ow = [(sc, gelu_sc[e_i])]
                if e_i == 0 and half in ob_guard:
                    ow.append((dv, ob_guard[half]))
                p.emit("tensor", lambda e, e3=e_i, ob=obank, h4=hst:
                       e.matmul(ob[:, :],
                                w2gb.ap()[:, e3 * 128:(e3 + 1) * 128],
                                h4[:, 0:TH], start=(e3 == 0),
                                stop=(e3 == NE - 1)),
                       waits=ow, inc=mm)
                mm2_mm.append(mm.count)
                if e_i + 2 < NE:
                    mm1(e_i + 2)
            t_o = p.tok(mm)
            p.emit("vector", lambda e, ob=obank: e.tensor_scalar(
                SLOT[6][:, :], ob[:, :], B2G.ap()[:, 0:1], None, ALU.add),
                waits=[t_o, t_t1sp, t_w2g], inc=dv)
            p.emit("vector", lambda e, b2=band, h2=hs: e.tensor_add(
                kv[b2][:, h2], SLOT[6][:, :], xn[b2][:, h2]), inc=dv)
            ob_guard[half] = dv.count
    t_u = p.tok(dv)
    if tap("u", [DT * 128, T], BF16):
        for i in range(DT):
            p.dma("sync", tap_out["u"][i * 128:(i + 1) * 128, :], kv[i], st,
                  waits=[t_u])

    w3_mm = []
    w3_ev = []
    for mg in range(16):
        sl = mg % 4
        buf = WB3[sl]
        w = [t_u]
        if mg >= 4:
            w.append((mm, w3_mm[-4]))
        p.dma("sync", buf.rearrange("p (kc c) -> p kc c", c=128),
              w3_bf.rearrange("(kc p) m -> p kc m", p=128)[
                  :, :, mg * 128:(mg + 1) * 128], w3s[sl], waits=w + [t_pc3],
              )
        t_w = p.tok(w3s[sl])
        for half in range(2):
            bank = ps[half]
            mmw = [t_w]
            if len(w3_ev) >= 2:
                mmw.append((dv, w3_ev[-2]))
            for kc in range(16):
                lhsT = buf[:, kc * 128:(kc + 1) * 128]
                p.emit("tensor", lambda e, o=bank, l=lhsT, k2=kc, h=half:
                       e.matmul(o[:, :], l, kv[k2][:, h * TH:(h + 1) * TH],
                                start=(k2 == 0), stop=(k2 == 15)),
                       waits=mmw if kc == 0 else (), inc=mm if kc == 15 else None)
            t_m2 = p.tok(mm)
            p.emit("vector", lambda e, h=half, m2=mg, b2=bank: e.tensor_scalar(
                vv[m2][:, h * TH:(h + 1) * TH], b2[:, :],
                b3cb.ap()[:, m2:m2 + 1], None, ALU.add), waits=[t_m2], inc=dv)
            w3_ev.append(dv.count)
        w3_mm.append(mm.count)
        t_ffv = p.tok(dv)
        p.dma("sync", ff_sp[mg * 128:(mg + 1) * 128, :], vv[mg], sp_s,
              waits=[t_ffv])
    t_ffsp = p.tok(sp_s)
    if tap("ffn", [D, T], BF16):
        for i in range(DT):
            p.dma("sync", tap_out["ffn"][i * 128:(i + 1) * 128, :], vv[i], st,
                  waits=[t_ffsp])

    if _stop("p6"):
        return nc, stack, sorted(taps & set(tap_out))
    p.emit("vector", lambda e: e.memset(TA[(0, NF)][:, :], 0.0), inc=dv)
    p7_mm = []
    p7_free = []
    for m in range(NF + 1):
        m1 = (m == NF)
        fbuf = FB[m % 2]
        w = [t_s1, t_ffsp, t_off]
        if m >= 2:
            w.append((mm, p7_mm[-2]))
        dma_fcols(fbuf, m * 128 if not m1 else 1024, 1 if m1 else 128, w,
                  fbs[m % 2])
        t_f = p.tok(fbs[m % 2])
        pr = slice(0, 1) if m1 else slice(0, 128)
        nplanes = 1 if m1 else 2
        for half in range(2):
            hs = slice(half * TH, (half + 1) * TH)
            for pl in range(nplanes):
                bank = ps[pl]
                mmw = [t_f]
                if p7_free:
                    mmw.append((dv, p7_free[-1]))
                for kc in range(16):
                    base = pl * 2048 + kc * 128
                    l = F1B.ap()[:, kc:kc + 1] if m1 else fbuf[:, base:base + 128]
                    p.emit("tensor", lambda e, o=bank, l2=l, k2=kc, h2=hs,
                           pr2=pr:
                           e.matmul(o[pr2, :], l2, qv[k2][:, h2],
                                    start=(k2 == 0), stop=(k2 == 15)),
                           waits=mmw if (kc == 0 and pl == 0) else (),
                           inc=mm if kc == 15 else None)
            t_sf = p.tok(mm)
            sQr, sQi, sKr, sKi, sVr, sVi, sT1, sT2 = SLOT

            def vop(fn, waits=(), inc=dv):
                p.emit("vector", fn, waits=list(waits) + [p.tok(dv)], inc=inc)

            vop(lambda e, pr=pr: e.tensor_copy(sT1[pr, :], ps[0][pr, :]), waits=[t_sf])
            if not m1:
                vop(lambda e, pr=pr: e.tensor_copy(sT2[pr, :], ps[1][pr, :]))
            p7_free.append(dv.count)
            vop(lambda e, m2=m, pr=pr, hs=hs: e.tensor_scalar(
                sQr[pr, :], XH[(0, m2)][pr, hs],
                OFFC.ap()[pr, m2 * 2:m2 * 2 + 1], None, ALU.add))
            if not m1:
                vop(lambda e, m2=m, pr=pr, hs=hs: e.tensor_scalar(
                    sQi[pr, :], XH[(1, m2)][pr, hs],
                    OFFC.ap()[pr, m2 * 2 + 1:m2 * 2 + 2], None, ALU.add))
                vop(lambda e, pr=pr: e.tensor_mul(sKr[pr, :], sT1[pr, :], sQr[pr, :]))
                vop(lambda e, pr=pr: e.tensor_mul(sKi[pr, :], sT2[pr, :], sQi[pr, :]))
                vop(lambda e, pr=pr: e.tensor_add(sVr[pr, :], sKr[pr, :], sKi[pr, :]))
                vop(lambda e, m2=m, h2=hs, pr=pr: e.tensor_copy(
                    TA[(0, m2)][pr, h2], sVr[pr, :]))
                vop(lambda e, pr=pr: e.tensor_mul(sKr[pr, :], sT1[pr, :], sQi[pr, :]))
                vop(lambda e, pr=pr: e.tensor_mul(sKi[pr, :], sT2[pr, :], sQr[pr, :]))
                vop(lambda e, pr=pr: e.tensor_sub(sVi[pr, :], sKr[pr, :], sKi[pr, :]))
                vop(lambda e, m2=m, h2=hs, pr=pr: e.tensor_copy(
                    TA[(1, m2)][pr, h2], sVi[pr, :]))
                vop(lambda e, pr=pr: e.tensor_mul(sKr[pr, :], sT1[pr, :], sT1[pr, :]))
                vop(lambda e, pr=pr: e.tensor_mul(sKi[pr, :], sT2[pr, :], sT2[pr, :]))
                vop(lambda e, pr=pr: e.tensor_add(sKr[pr, :], sKr[pr, :], sKi[pr, :]))
                vop(lambda e, pr=pr: e.tensor_scalar(sKr[pr, :], sKr[pr, :],
                                              -NINV, NINV, ALU.mult, ALU.add))
                vop(lambda e, m2=m, h2=hs, pr=pr: e.tensor_mul(
                    XH[(0, m2)][pr, h2], sQr[pr, :], sKr[pr, :]))
                vop(lambda e, m2=m, h2=hs, pr=pr: e.tensor_mul(
                    XH[(1, m2)][pr, h2], sQi[pr, :], sKr[pr, :]))
            else:
                vop(lambda e, pr=pr: e.tensor_mul(sVr[pr, :], sT1[pr, :], sQr[pr, :]))
                vop(lambda e, m2=m, h2=hs, pr=pr: e.tensor_copy(
                    TA[(0, m2)][pr, h2], sVr[pr, :]))
                vop(lambda e, pr=pr: e.tensor_mul(sKr[pr, :], sT1[pr, :], sT1[pr, :]))
                vop(lambda e, pr=pr: e.tensor_scalar(sKr[pr, :], sKr[pr, :],
                                              -NINV, NINV, ALU.mult, ALU.add))
                vop(lambda e, m2=m, h2=hs, pr=pr: e.tensor_mul(
                    XH[(0, m2)][pr, h2], sQr[pr, :], sKr[pr, :]))
        p7_mm.append(mm.count)
    t_p7 = p.tok(dv)
    if tap("ta", [17 * 128, T], BF16):
        _ti = 0
        for pl in range(2):
            for i in range(9 if pl == 0 else 8):
                p.dma("sync", tap_out["ta"][_ti * 128:(_ti + 1) * 128, :],
                      TA[(pl, i)], st, waits=[t_p7])
                _ti += 1
    if tap("zs", [17 * 128, T], BF16):
        _ti = 0
        for pl in range(2):
            for i in range(9 if pl == 0 else 8):
                p.dma("sync", tap_out["zs"][_ti * 128:(_ti + 1) * 128, :],
                      XH[(pl, i)], st, waits=[t_p7])
                _ti += 1
    _ti = 0
    for pl in range(2):
        for i in range(9 if pl == 0 else 8):
            p.dma("sync", ta_sp[_ti * 128:(_ti + 1) * 128, :], TA[(pl, i)],
                  sp_s, waits=[t_p7] if _ti == 0 else ())
            p.dma("sync", zs_sp[_ti * 128:(_ti + 1) * 128, :], XH[(pl, i)], sp_s)
            _ti += 1
    t_spill = p.tok(sp_s)

    if _stop("p7"):
        return nc, stack, sorted(taps & set(tap_out))
    prev_done = None
    for half in range(2):
        h0 = half * TH
        relw = [t_spill]
        if prev_done is not None:
            relw.append(prev_done)
        _ti = 0
        for pl in range(2):
            for i in range(9 if pl == 0 else 8):
                p.dma("sync", TAH[(pl, i)],
                      ta_sp[_ti * 128:(_ti + 1) * 128, h0:h0 + TH], ta_s,
                      waits=relw if _ti == 0 else ())
                _ti += 1
        t_tah = p.tok(ta_s)
        mir_ev = []
        for pl in range(2):
            for Dd in range(8, 16):
                mi = (Dd - 8) + (0 if pl == 0 else 8)
                bank = ps[mi % 4]
                mw = [t_tah]
                if len(mir_ev) >= 4:
                    mw.append((dv, mir_ev[-4]))
                two = (pl == 0) or (Dd > 8)
                RA = RevAb if pl == 0 else RevAnb
                P0 = Pick0b if pl == 0 else Pick0nb
                p.emit("tensor", lambda e, b2=bank, ra=RA, s2=(pl, 15 - Dd),
                       tw=two:
                       e.matmul(b2[:, :], ra.ap()[:, :], TAH[s2],
                                start=True, stop=not tw),
                       waits=mw, inc=None if two else mm)
                if two:
                    p.emit("tensor", lambda e, b2=bank, p0=P0, s2=(pl, 16 - Dd):
                           e.matmul(b2[:, :], p0.ap()[:, :], TAH[s2],
                                    start=False, stop=True), inc=mm)
                t_mi = p.tok(mm)
                p.emit("vector", lambda e, m2=mi, b2=bank: e.tensor_copy(
                    MIR[m2], b2[:, :]), waits=[t_mi], inc=dv)
                mir_ev.append(dv.count)
        t_mir = p.tok(dv)

        def m1rhs(j):
            if j < 8:
                return TAH[(0, j)]
            if j < 16:
                return MIR[j - 8]
            if j < 24:
                return TAH[(1, j - 16)]
            return MIR[8 + (j - 24)]

        m1_mm = []
        m1_sc = []
        if half == 0:
            m1_bank = {}
        for blk in range(64):
            sl = blk % 4
            buf = W1B[sl]
            w = [t_mir, t_tah]
            if blk >= 4:
                w.append((mm, m1_mm[-4]))
            p.dma("sync", buf.rearrange("p (kc c) -> p kc c", c=128),
                  w1_bf.rearrange("(kc p) m -> p kc m", p=128)[
                      :, :, blk * 128:(blk + 1) * 128], w1s[sl],
                  waits=w + [t_pc1])
            t_w = p.tok(w1s[sl])
            bank = ps[blk % 3]
            mmw = [t_w, t_mir]
            if (blk % 3) in m1_bank:
                mmw.append((sc, m1_bank[blk % 3]))
            for kc in range(32):
                l = buf[:, kc * 128:(kc + 1) * 128]
                p.emit("tensor", lambda e, o=bank, l2=l, k2=kc:
                       e.matmul(o[:, :], l2, m1rhs(k2),
                                start=(k2 == 0), stop=(k2 == 31)),
                       waits=mmw if kc == 0 else (), inc=mm if kc == 31 else None)
            m1_mm.append(mm.count)
            t_m1 = p.tok(mm)
            p.emit("scalar", lambda e, b2=bank, m2=blk: e.activation(
                HB[m2][:, :], b2[:, :], AF.Relu,
                bias=b1cb.ap()[:, m2:m2 + 1]), waits=[t_m1], inc=sc)
            m1_sc.append(sc.count)
            m1_bank[blk % 3] = sc.count
        t_h = p.tok(sc)
        if half == 0 and _stop("p8m1"):
            return nc, stack, sorted(taps & set(tap_out))
        if half == 0 and tap("h0", [128, TH], BF16):
            p.dma("sync", tap_out["h0"][:, :], HB[0][:, :], st, waits=[t_h])

        _ti = 0
        for pl in range(2):
            for i in range(9 if pl == 0 else 8):
                p.dma("sync", ZSH[(pl, i)],
                      zs_sp[_ti * 128:(_ti + 1) * 128, h0:h0 + TH], zs_s,
                      waits=[(sc, m1_sc[-1])] if _ti == 0 else ())
                _ti += 1
        t_zsh = p.tok(zs_s)
        for i in range(DT):
            p.dma("sync", T1H[i], t1_sp[i * 128:(i + 1) * 128, h0:h0 + TH],
                  t1_s, waits=[(mm, m1_mm[-1])] if i == 0 else ())
        t_t1h = p.tok(t1_s)

        m2_mm = []
        tsg_sc = []
        zb_dv = []
        tg_mm = []
        mr_dv = []
        for j in range(16):
            for sub, mtile in ((0, j), (1, j + 16)):
                bank = ps[sub]
                for hh in range(2):
                    bi = (4 * j + 2 * sub + hh) % 4
                    buf = W2B[bi]
                    w = [t_h, t_t1h]
                    if len(m2_mm) >= 4:
                        w.append((mm, m2_mm[-4]))
                    p.dma("sync", buf.rearrange("p (kc c) -> p kc c", c=128),
                          w2_bf.rearrange("(kc p) m -> p kc m", p=128)[
                              :, hh * 32:(hh + 1) * 32,
                              mtile * 128:(mtile + 1) * 128], w2s[bi],
                          waits=w + [t_pc2])
                    t_w = p.tok(w2s[bi])
                    mmw = [t_w, t_h]
                    if len(tsg_sc) >= 1:
                        mmw.append((sc, tsg_sc[-1]))
                    for kc in range(32):
                        l = buf[:, kc * 128:(kc + 1) * 128]
                        kg = hh * 32 + kc
                        p.emit("tensor", lambda e, o=bank, l2=l, k2=kg:
                               e.matmul(o[:, :], l2, HB[k2][:, :],
                                        start=(k2 == 0), stop=(k2 == 63)),
                               waits=mmw if kc == 0 else (),
                               inc=mm if (kc == 31) else None)
                    m2_mm.append(mm.count)
            t_tr = p.tok(mm)
            fbuf = FBj[j % 2]
            fw = [t_t1h]
            if j >= 2:
                fw.append((mm, tg_mm[-2]))
            dma_fcols(fbuf, j * 128, 128, fw, fjs[j % 2])
            t_fj = p.tok(fjs[j % 2])
            for pl in range(2):
                bank = ps[2 + pl]
                mmw = [t_fj]
                if zb_dv:
                    mmw.append((dv, zb_dv[-1]))
                for kc in range(16):
                    base = pl * 2048 + kc * 128
                    l = fbuf[:, base:base + 128]
                    p.emit("tensor", lambda e, o=bank, l2=l, k2=kc:
                           e.matmul(o[:, :], l2, T1H[k2],
                                    start=(k2 == 0), stop=(k2 == 15)),
                           waits=mmw if (kc == 0 and pl == 0) else (),
                           inc=mm if kc == 15 else None)
            tg_mm.append(mm.count)
            t_tg = p.tok(mm)
            tsg_w = [(mm, m2_mm[-3])]
            if zb_dv:
                tsg_w.append((dv, zb_dv[-1]))
            p.emit("scalar", lambda e, j2=j: e.activation(
                TSG[:, 0:TH], ps[0][:, :], AF.Identity,
                bias=b2cb.ap()[:, j2:j2 + 1]), waits=tsg_w, inc=sc)
            p.emit("scalar", lambda e, j2=j: e.activation(
                TSG[:, TH:2 * TH], ps[1][:, :], AF.Identity,
                bias=b2cb.ap()[:, 16 + j2:17 + j2]), waits=[(mm, m2_mm[-1])],
                inc=sc)
            tsg_sc.append(sc.count)
            t_tsg = p.tok(sc)
            if j < 8:
                zsr = ZSH[(0, j)]
                zsi = ZSH[(1, j)]
                t_zm = None
            else:
                Dd = j
                for pl in range(2):
                    bank = ps[4 + pl]
                    two = (pl == 0) or (Dd > 8)
                    RA = RevAb if pl == 0 else RevAnb
                    P0 = Pick0b if pl == 0 else Pick0nb
                    mw = [t_zsh]
                    if mr_dv:
                        mw.append((dv, mr_dv[-1]))
                    p.emit("tensor", lambda e, b2=bank, ra=RA,
                           s2=(pl, 15 - Dd), tw=two:
                           e.matmul(b2[:, :], ra.ap()[:, :], ZSH[s2],
                                    start=True, stop=not tw),
                           waits=mw, inc=None if two else mm)
                    if two:
                        p.emit("tensor", lambda e, b2=bank, p0=P0,
                               s2=(pl, 16 - Dd):
                               e.matmul(b2[:, :], p0.ap()[:, :], ZSH[s2],
                                        start=False, stop=True), inc=mm)
                t_zmm = p.tok(mm)
                p.emit("vector", lambda e: e.tensor_copy(
                    ZMR[:, 0:TH], ps[4][:, :]), waits=[t_zmm], inc=dv)
                p.emit("vector", lambda e: e.tensor_copy(
                    ZMR[:, TH:2 * TH], ps[5][:, :]), inc=dv)
                mr_dv.append(dv.count)
                zsr = ZMR[:, 0:TH]
                zsi = ZMR[:, TH:2 * TH]

            def vop(fn, waits=(), inc=dv):
                p.emit("vector", fn, waits=list(waits) + [p.tok(dv)], inc=inc)

            vop(lambda e: e.scalar_tensor_tensor(
                SLOT[0][:, :], ps[2][:, :], NINV, TSG[:, 0:TH],
                ALU.mult, ALU.mult), waits=[t_tg, t_tsg, t_zsh])
            vop(lambda e: e.scalar_tensor_tensor(
                SLOT[1][:, :], ps[3][:, :], NINV, TSG[:, TH:2 * TH],
                ALU.mult, ALU.mult))
            vop(lambda e: e.scalar_tensor_tensor(
                SLOT[2][:, :], ps[2][:, :], NINV, TSG[:, TH:2 * TH],
                ALU.mult, ALU.mult))
            vop(lambda e: e.scalar_tensor_tensor(
                SLOT[3][:, :], ps[3][:, :], NINV, TSG[:, 0:TH],
                ALU.mult, ALU.mult))
            vop(lambda e: e.tensor_sub(SLOT[0][:, :], SLOT[0][:, :],
                                       SLOT[1][:, :]))
            vop(lambda e, z=zsr, j2=j: e.tensor_add(ZB[j2], SLOT[0][:, :], z))
            vop(lambda e: e.tensor_add(SLOT[2][:, :], SLOT[2][:, :],
                                       SLOT[3][:, :]))
            vop(lambda e, z=zsi, j2=j: e.tensor_add(ZB[16 + j2],
                                                    SLOT[2][:, :], z))
            zb_dv.append(dv.count)
        t_zb = p.tok(dv)
        if half == 0 and _stop("p9z"):
            return nc, stack, sorted(taps & set(tap_out))
        if half == 0 and tap("z", [32 * 128, TH], BF16):
            for i in range(32):
                p.dma("sync", tap_out["z"][i * 128:(i + 1) * 128, :], ZB[i],
                      st, waits=[t_zb])

        for i in range(DT):
            p.dma("sync", FFH[i], ff_sp[i * 128:(i + 1) * 128, h0:h0 + TH],
                  ff_s, waits=[(mm, m2_mm[-1])] if i == 0 else ())
        t_ffh = p.tok(ff_s)

        if_mm = []
        aos_dv = []
        for dt_ in range(DT):
            fbuf = FBI[dt_ % 2]
            fw = [t_zb]
            if dt_ >= 2:
                fw.append((mm, if_mm[-2]))
            dma_fcols(fbuf, dt_ * 128, 128, fw, fis[dt_ % 2])
            t_fi = p.tok(fis[dt_ % 2])
            bank = ps[dt_ % 2]
            mmw = [t_fi, t_zb]
            if len(aos_dv) >= 2:
                mmw.append((dv, aos_dv[-2]))
            for pl in range(2):
                for kc in range(16):
                    base = pl * 2048 + kc * 128
                    l = fbuf[:, base:base + 128]
                    zi = pl * 16 + kc
                    p.emit("tensor", lambda e, o=bank, l2=l, z2=zi:
                           e.matmul(o[:, :], l2, ZB[z2],
                                    start=(z2 == 0), stop=(z2 == 31)),
                           waits=mmw if (kc == 0 and pl == 0) else (),
                           inc=mm if (pl == 1 and kc == 15) else None)
            if_mm.append(mm.count)
            t_if = p.tok(mm)
            p.emit("vector", lambda e, d2=dt_, b2=bank: e.tensor_add(
                AOS[d2], b2[:, :], FFH[d2]), waits=[t_if, t_ffh], inc=dv)
            aos_dv.append(dv.count)
        t_aos = p.tok(dv)
        if half == 0 and tap("aos", [DT * 128, TH], BF16):
            for i in range(DT):
                p.dma("sync", tap_out["aos"][i * 128:(i + 1) * 128, :],
                      AOS[i], st, waits=[t_aos])

        st_done = []
        for tt in range(4):
            grow = half * 4 + tt
            xw = [t_aos]
            if st_done:
                xw.append((st, st_done[-1]))
            p.dma("sync", XRL[0], x_my[grow * 128:(grow + 1) * 128, :], xr_s,
                  waits=xw)
            t_x = p.tok(xr_s)
            add_dv = []
            for g in range(4):
                tw = [t_aos]
                if add_dv:
                    tw.append((dv, add_dv[-1]))
                elif st_done:
                    tw.append((st, st_done[-1]))
                for j in range(4):
                    d2 = g * 4 + j
                    p.emit("tensor", lambda e, d3=d2, t2=tt, jj=j:
                           e.transpose(pt[:, jj * 128:(jj + 1) * 128],
                                       AOS[d3][:, t2 * 128:(t2 + 1) * 128],
                                       identb.ap()[:, :]),
                           waits=tw if j == 0 else (),
                           inc=mm if j == 3 else None)
                t_t = p.tok(mm)
                aw = [t_t, t_x]
                if g == 0 and st_done:
                    aw.append((st, st_done[-1]))
                p.emit("vector", lambda e, g2=g: e.tensor_add(
                    OUTB[:, g2 * 512:(g2 + 1) * 512], pt[:, :],
                    XRL[0][:, g2 * 512:(g2 + 1) * 512]),
                    waits=aw, inc=dv)
                add_dv.append(dv.count)
            t_ob = p.tok(dv)
            p.dma("sync", y[grow * 128:(grow + 1) * 128, :], OUTB, st,
                  waits=[t_ob])
            st_done.append(st.count)
        prev_done = p.tok(st)
        if half == 0 and _stop("p10h0"):
            return nc, stack, sorted(taps & set(tap_out))

    p.emit("sync", lambda e: None, waits=[p.tok(st)])
    p.finalize()
    return nc, stack, sorted(taps & set(tap_out))


_CACHE = {}


def _consts():
    if "c" in _CACHE:
        return _CACHE["c"]
    n = np.arange(D, dtype=np.float64)
    th = 2.0 * np.pi * np.outer(n, n) / D
    Frv = np.cos(th).astype(ml_dtypes.bfloat16)
    Fiv = (-np.sin(th)).astype(ml_dtypes.bfloat16)
    ident = np.eye(128, dtype=ml_dtypes.bfloat16)
    RevAv = np.zeros((128, 128), dtype=np.float32)
    for q in range(1, 128):
        RevAv[128 - q, q] = 1.0
    Pick0v = np.zeros((128, 128), dtype=np.float32)
    Pick0v[0, 0] = 1.0
    onesrv = np.ones((1, 128), dtype=ml_dtypes.bfloat16)
    f1024v = np.ascontiguousarray(
        np.cos(th[:, 1024]).reshape(16, 128).T).astype(ml_dtypes.bfloat16)
    _CACHE["c"] = (Frv, Fiv, ident, RevAv.astype(ml_dtypes.bfloat16),
                   (-RevAv).astype(ml_dtypes.bfloat16),
                   Pick0v.astype(ml_dtypes.bfloat16),
                   (-Pick0v).astype(ml_dtypes.bfloat16), onesrv, f1024v)
    return _CACHE["c"]


def make_inputs(kw, core):
    (Frv, Fiv, ident, RevAv, RevAnv, Pick0v, Pick0nv, onesrv,
     f1024v) = _consts()
    x = np.asarray(kw["L0_attn_x"], dtype=np.float32)
    b = core // 4
    q = core % 4
    d = {
        "x_my": np.ascontiguousarray(x[b, q * T:(q + 1) * T, :]),
        "x_rt": np.ascontiguousarray(
            np.concatenate([x[0, core * RT:(core + 1) * RT, :],
                            x[1, core * RT:(core + 1) * RT, :]], axis=0)),
        "Wq": kw["L0_attn_Wq"], "Wk": kw["L0_attn_Wk"], "Wv": kw["L0_attn_Wv"],
        "Ws": kw["L0_attn_Ws"], "Wt": kw["L0_attn_Wt"],
        "W1": kw["L0_attn_W1"], "W2": kw["L0_attn_W2"],
        "w3t": np.ascontiguousarray(np.asarray(kw["L0_ffn_w3"]).T),
        "b1c": np.ascontiguousarray(
            np.asarray(kw["L0_attn_b1"]).reshape(64, 128).T),
        "b2c": np.ascontiguousarray(
            np.asarray(kw["L0_attn_b2"]).reshape(32, 128).T),
        "b3c": np.ascontiguousarray(
            np.asarray(kw["L0_ffn_b3"]).reshape(16, 128).T),
        "fw1t": np.ascontiguousarray(
            np.asarray(kw["L0_ffn_w1"]).transpose(0, 2, 1)),
        "fw2t": np.ascontiguousarray(
            np.asarray(kw["L0_ffn_w2"]).transpose(0, 2, 1)),
        "fb1c": np.ascontiguousarray(np.asarray(kw["L0_ffn_b1"]).T),
        "fb2": np.asarray(kw["L0_ffn_b2"]),
        "swc": np.ascontiguousarray(
            np.asarray(kw["L0_ffn_switch_w"]).reshape(NE, 2048, D)
            [:, core * RT:(core + 1) * RT, :]),
        "swb_bc": np.ascontiguousarray(
            np.tile(np.asarray(kw["L0_ffn_switch_b"])[None, :], (2, 1))),
        "mask_bc": np.ascontiguousarray(np.tile(
            np.array([[1.0 if (j // 4 == b and j % 4 < q) else 0.0
                       for j in range(NCORE)]], np.float32), (128, 1))),
        "bselc": np.array([[1.0 if b == 0 else 0.0],
                           [1.0 if b == 1 else 0.0]], np.float32),
        "Fr": Frv, "Fi": Fiv, "identc": ident,
        "RevA": RevAv, "RevAn": RevAnv, "Pick0": Pick0v, "Pick0n": Pick0nv,
        "onesr": onesrv,
        "f1024": f1024v,
    }
    for k in list(d):
        a = np.asarray(d[k])
        if a.dtype == np.float64:
            a = a.astype(np.float32)
        d[k] = np.ascontiguousarray(a)
    return d


_NC_CACHE = {}


def kernel(**inputs):
    key = ()
    if key not in _NC_CACHE:
        _NC_CACHE[key] = build_program(())
    nc, stack, _ = _NC_CACHE[key]
    in_maps = [make_inputs(inputs, c) for c in range(NCORE)]
    res = run_bass_kernel_spmd(nc, in_maps, core_ids=list(range(NCORE)))
    out = np.empty((B, S, D), dtype=np.float32)
    for c in range(NCORE):
        out[c // 4, (c % 4) * T:(c % 4 + 1) * T, :] = res.results[c]["y"]
    return out


# version 08
# speedup vs baseline: 1.0018x; 1.0018x over previous
from contextlib import ExitStack

import numpy as np
import ml_dtypes

import concourse.bass as bass
import concourse.mybir as mybir
from concourse.bass_utils import run_bass_kernel_spmd

F32 = mybir.dt.float32
BF16 = mybir.dt.bfloat16
AF = mybir.ActivationFunctionType
ALU = mybir.AluOpType
AX = mybir.AxisListType

ENGINES = ("tensor", "vector", "scalar", "gpsimd", "sync")

B, S, D = 2, 4096, 2048
NCORE = 8
T = 1024
TH = 512
DT = 16
NF = 8
NE = 8
NHEAD = 16
RT = 256
NINV = 1.0 / D


class Sem:
    __slots__ = ("h", "count", "name")

    def __init__(self, h, name):
        self.h = h
        self.name = name
        self.count = 0


class Prog:
    def __init__(self, nc, stack):
        self.nc = nc
        self.stack = stack
        self.ops = {e: [] for e in ENGINES}
        self._waited = {e: {} for e in ENGINES}

    def sem(self, name):
        return Sem(self.stack.enter_context(self.nc.semaphore(name)), name)

    def tok(self, sem):
        return (sem, sem.count)

    auto_self = None

    def emit(self, eng, fn, waits=(), inc=None, inc_n=1):
        if self.auto_self and eng in self.auto_self:
            waits = list(waits) + [self.tok(self.auto_self[eng])]
        waited = self._waited[eng]
        rw = []
        for s, v in waits:
            if v > 0 and waited.get(s.name, 0) < v:
                rw.append((s, v))
                waited[s.name] = v
        if inc is not None:
            inc.count += inc_n

        def run(e, _fn=fn, _w=rw, _i=inc, _n=inc_n):
            for s, v in _w:
                e.wait_ge(s.h, v)
            inst = _fn(e)
            if _i is not None:
                if inst is None:
                    e.sem_inc(_i.h, _n)
                else:
                    inst.then_inc(_i.h, _n)

        self.ops[eng].append(run)
        return None if inc is None else (inc, inc.count)

    def dma(self, eng, out, in_, sem, waits=(), **kw):
        return self.emit(eng, lambda e: e.dma_start(out=out, in_=in_, **kw),
                         waits=waits, inc=sem, inc_n=16)

    def finalize(self):
        with self.nc.Block() as block:
            for eng_name in ENGINES:
                ops = self.ops[eng_name]

                def body(e, _ops=ops):
                    for f in _ops:
                        f(e)

                getattr(block, eng_name)(body)


def build_program(taps=(), stop=None, detect=False):
    nc = bass.Bass("TRN2", target_bir_lowering=False, debug=False,
                   detect_race_conditions=detect)
    stack = ExitStack()
    p = Prog(nc, stack)
    taps = set(taps)

    def din(name, shape, dt=F32):
        return nc.dram_tensor(name, shape, dt, kind="ExternalInput").ap()

    x_my = din("x_my", [T, D])
    x_rt = din("x_rt", [2 * RT, D])
    Wq = din("Wq", [D, D]); Wk = din("Wk", [D, D]); Wv = din("Wv", [D, D])
    Ws = din("Ws", [D, D]); Wt = din("Wt", [D, D])
    W1 = din("W1", [2 * D, 4 * D]); W2 = din("W2", [4 * D, 2 * D])
    w3t = din("w3t", [D, D])
    b1c = din("b1c", [128, 64]); b2c = din("b2c", [128, 32]); b3c = din("b3c", [128, 16])
    fw1t = din("fw1t", [NE, 128, 128]); fw2t = din("fw2t", [NE, 128, 128])
    fb1c = din("fb1c", [128, NE]); fb2 = din("fb2", [NE, 128])
    swc = din("swc", [NE, RT, D])
    swb_bc = din("swb_bc", [2, NE])
    mask_bc = din("mask_bc", [128, NCORE])
    bselc = din("bselc", [2, 1])
    Fr = din("Fr", [D, D], BF16); Fi = din("Fi", [D, D], BF16)
    identc = din("identc", [128, 128], BF16)
    RevA = din("RevA", [128, 128], BF16); RevAn = din("RevAn", [128, 128], BF16)
    Pick0 = din("Pick0", [128, 128], BF16); Pick0n = din("Pick0n", [128, 128], BF16)
    onesr = din("onesr", [1, 128], BF16)
    f1024 = din("f1024", [128, 16], BF16)

    y = nc.dram_tensor("y", [T, D], F32, kind="ExternalOutput").ap()

    wq_bf = nc.dram_tensor("wq_bf", [D, D], BF16).ap()
    wk_bf = nc.dram_tensor("wk_bf", [D, D], BF16).ap()
    wv_bf = nc.dram_tensor("wv_bf", [D, D], BF16).ap()
    ws_bf = nc.dram_tensor("ws_bf", [D, D], BF16).ap()
    wt_bf = nc.dram_tensor("wt_bf", [D, D], BF16).ap()
    w3_bf = nc.dram_tensor("w3_bf", [D, D], BF16).ap()
    w1_bf = nc.dram_tensor("w1_bf", [2 * D, 4 * D], BF16).ap()
    swc_bf = nc.dram_tensor("swc_bf", [NE, RT, D], BF16).ap()
    w2_bf = nc.dram_tensor("w2_bf", [4 * D, 2 * D], BF16).ap()
    cc_in = nc.dram_tensor("cc_in", [1, 2320], F32)
    cc_out = nc.dram_tensor("cc_out", [NCORE, 2320], F32)
    ta_sp = nc.dram_tensor("ta_sp", [17 * 128, T], BF16).ap()
    zs_sp = nc.dram_tensor("zs_sp", [17 * 128, T], BF16).ap()
    t1_sp = nc.dram_tensor("t1_sp", [D, T], BF16).ap()
    ff_sp = nc.dram_tensor("ff_sp", [D, T], BF16).ap()

    tap_out = {}

    def tap(name, shape, dt=F32):
        if name in taps:
            tap_out[name] = nc.dram_tensor("tap_" + name, shape, dt,
                                           kind="ExternalOutput").ap()
            return True
        return False

    gq = p.sem("gq"); sq = p.sem("sq"); mm = p.sem("mm"); dv = p.sem("dv")
    sc = p.sem("sc"); cc = p.sem("cc"); st = p.sem("st")
    c1 = p.sem("c1"); c2 = p.sem("c2"); lq = p.sem("lq")
    wp = [p.sem(f"wp{i}") for i in range(4)]
    w1s = [p.sem(f"w1s{i}") for i in range(4)]
    w2s = [p.sem(f"w2s{i}") for i in range(4)]
    w3s = [p.sem(f"w3s{i}") for i in range(4)]
    fbs = [p.sem(f"fbs{i}") for i in range(2)]
    fjs = [p.sem(f"fjs{i}") for i in range(2)]
    fis = [p.sem(f"fis{i}") for i in range(2)]
    sws = [p.sem(f"sws{i}") for i in range(2)]
    ci_s = p.sem("ci_s"); gl_s = p.sem("gl_s"); sp_s = p.sem("sp_s")
    ta_s = p.sem("ta_s"); zs_s = p.sem("zs_s"); t1_s = p.sem("t1_s")
    ff_s = p.sem("ff_s"); xr_s = p.sem("xr_s")
    pc_q = p.sem("pc_q"); pc_k = p.sem("pc_k"); pc_v = p.sem("pc_v")
    pc_s = p.sem("pc_s"); pc_t = p.sem("pc_t"); pc_3 = p.sem("pc_3")
    pc_1 = p.sem("pc_1"); pc_2 = p.sem("pc_2"); pc_w = p.sem("pc_w")
    p.auto_self = {"vector": dv, "scalar": sc}

    def _stop(name):
        if stop == name:
            p.emit("sync", lambda e: None, waits=[p.tok(st), p.tok(dv),
                                                  p.tok(sc), p.tok(gq),
                                                  p.tok(sq)])
            p.finalize()
            return True
        return False


    R1 = stack.enter_context(nc.sbuf_tensor("R1", [128, 16384], BF16))
    R2 = stack.enter_context(nc.sbuf_tensor("R2", [128, 16384], BF16))
    R3 = stack.enter_context(nc.sbuf_tensor("R3", [128, 16384], BF16))
    R4 = stack.enter_context(nc.sbuf_tensor("R4", [128, 16384], BF16))
    R5 = stack.enter_context(nc.sbuf_tensor("R5", [128, 8192], BF16))
    RE = stack.enter_context(nc.sbuf_tensor("RE", [128, 17920], BF16))
    FA = stack.enter_context(nc.sbuf_tensor("FA", [128, 4608], F32))

    def vw(arena, off, n):
        return arena.ap()[:, off:off + n]

    sbt = lambda nm, sh, dt: stack.enter_context(nc.sbuf_tensor(nm, sh, dt))
    b1cb = sbt("b1cb", [128, 64], F32)
    b2cb = sbt("b2cb", [128, 32], F32)
    b3cb = sbt("b3cb", [128, 16], F32)
    fw1b = sbt("fw1b", [128, NE * 128], BF16)
    fw2b = sbt("fw2b", [128, NE * 128], BF16)
    w2gb = sbt("w2gb", [128, NE * 128], BF16)
    fb1b = sbt("fb1b", [128, NE], F32)
    fb2b = sbt("fb2b", [NE, 128], BF16)
    maskb = sbt("maskb", [128, NCORE], F32)
    bselb = sbt("bselb", [2, 1], BF16)
    identb = sbt("identb", [128, 128], BF16)
    RevAb = sbt("RevAb", [128, 128], BF16)
    RevAnb = sbt("RevAnb", [128, 128], BF16)
    Pick0b = sbt("Pick0b", [128, 128], BF16)
    Pick0nb = sbt("Pick0nb", [128, 128], BF16)
    onerb = sbt("onerb", [1, 128], BF16)
    F1B = sbt("F1B", [128, 16], BF16)
    swbb = sbt("swbb", [2, NE], F32)
    COLL = sbt("COLL", [128, 18], F32)
    CARR = sbt("CARR", [128, 2], F32)
    OFFC = sbt("OFFC", [128, 18], F32)
    OFFG = sbt("OFFG", [128, 18 * NCORE], F32)
    ST1 = sbt("ST1", [128, 32], F32)
    ST2B = sbt("ST2B", [128, 16], BF16)
    ONE1 = sbt("ONE1", [128, 1], BF16)
    RLOG = sbt("RLOG", [16, 1], F32)
    LG = sbt("LG", [2, 8 * NCORE], F32)
    GT = sbt("GT", [2, NE], F32)
    GTt1 = sbt("GTt1", [2, NE], F32)
    GTt2 = sbt("GTt2", [2, NE], F32)
    GTB = sbt("GTB", [2, NE], BF16)
    GRED = sbt("GRED", [2, 1], F32)
    GRED2 = sbt("GRED2", [128, 8], F32)
    EPSC = sbt("EPSC", [128, 2], F32)
    MGC = sbt("MGC", [NE, 1], BF16)
    MGR = sbt("MGR", [1, NE], BF16)
    GBC = sbt("GBC", [128, NE], F32)
    B2G = sbt("B2G", [128, 1], F32)

    LNS = FA.ap()[:, 0:2048]
    LNT = FA.ap()[:, 2048:4096]
    SLOT = [FA.ap()[:, i * 512:(i + 1) * 512] for i in range(8)]
    TSG = FA.ap()[:, 2048:3072]
    ZMR = FA.ap()[:, 3072:4096]
    SCAN = FA.ap()[:, 4096:4608]
    OUTB = FA.ap()[:, 0:2048]
    XRL = [FA.ap()[:, 2048:4096]]

    ps = [stack.enter_context(nc.psum_tensor(f"ps{i}", [128, TH], F32))
          for i in range(6)]
    pt = stack.enter_context(nc.psum_tensor("pt", [128, TH], BF16))
    px = stack.enter_context(nc.psum_tensor("px", [128, TH], F32))

    xn = [vw(R1, i * 1024, 1024) for i in range(DT)]
    qv = [vw(R2, i * 1024, 1024) for i in range(DT)]
    kv = [vw(R3, i * 1024, 1024) for i in range(DT)]
    vv = [vw(R4, i * 1024, 1024) for i in range(DT)]
    XH = {}
    for i in range(9):
        XH[(0, i)] = vw(RE, i * 1024, 1024)
    for i in range(8):
        XH[(1, i)] = vw(RE, (9 + i) * 1024, 1024)
    TA = {}
    for i in range(9):
        TA[(0, i)] = vw(R1, i * 1024, 1024)
    for i in range(7):
        TA[(1, i)] = vw(R1, (9 + i) * 1024, 1024)
    TA[(1, 7)] = vw(R3, 0, 1024)
    XNTM = [vw(RE, 0, 2048), vw(RE, 2048, 2048)]
    RPRD = vw(RE, 0, 2048)
    XRT = [vw(RE, 4096 + i * 2048, 2048) for i in range(4)]
    SWB = [vw(RE, 12288 + i * 2048, 2048) for i in range(2)]
    HB = [vw(R1, i * 512, 512) for i in range(32)] + \
         [vw(R3, i * 512, 512) for i in range(32)]
    MIR = [vw(RE, i * 512, 512) for i in range(16)]
    TAH = {}
    _ti = 0
    for _pl in range(2):
        for _i in range(9 if _pl == 0 else 8):
            TAH[(_pl, _i)] = vw(R2, _ti * 512, 512)
            _ti += 1
    ZSH = {}
    _ti = 0
    for _pl in range(2):
        for _i in range(9 if _pl == 0 else 8):
            ZSH[(_pl, _i)] = vw(RE, _ti * 512, 512)
            _ti += 1
    T1H = [vw(R2, i * 512, 512) for i in range(16)]
    ZB = [vw(R2, 8192 + i * 512, 512) for i in range(16)] + \
         [vw(RE, 8704 + i * 512, 512) for i in range(16)]
    AOS = [vw(R2, i * 512, 512) for i in range(16)]
    FFH = [vw(R4, i * 512, 512) for i in range(16)]
    W1B = [vw(R4, i * 4096, 4096) for i in range(4)]
    W2B = [vw(R4, i * 4096, 4096) for i in range(4)]
    WBpr = [vw(R5, i * 2048, 2048) for i in range(4)]
    WB3 = [vw(R5, i * 2048, 2048) for i in range(4)]
    HSTs = [vw(R5, 4096 + i * 512, 512) for i in range(2)]
    FB = [vw(R5, i * 4096, 4096) for i in range(2)]
    FBj = [vw(R5, i * 4096, 4096) for i in range(2)]
    FBI = [vw(R5, i * 4096, 4096) for i in range(2)]

    for dst, src in [(b1cb, b1c), (b2cb, b2c), (b3cb, b3c), (fb1b, fb1c),
                     (maskb, mask_bc), (swbb, swb_bc)]:
        p.dma("sync", dst.ap()[:, :], src[:, :], c1)
    for dst, src in [(identb, identc), (RevAb, RevA), (RevAnb, RevAn),
                     (Pick0b, Pick0), (Pick0nb, Pick0n), (onerb, onesr)]:
        p.dma("sync", dst.ap()[:, :], src[:, :], c1)
    p.dma("gpsimd", fw1b.ap().rearrange("p (e c) -> p e c", c=128),
          fw1t.rearrange("e p c -> p e c"), c2)
    p.dma("gpsimd", fw2b.ap().rearrange("p (e c) -> p e c", c=128),
          fw2t.rearrange("e p c -> p e c"), c2)
    p.dma("gpsimd", fb2b.ap()[:, :], fb2[:, :], c2)
    p.dma("gpsimd", bselb.ap()[:, :], bselc[:, :], c2)
    _pc_hist = []

    def _pc_dma(dst_ap, src_ap, sem8):
        if not _pc_hist:
            w = [(c1, c1.count), (c2, c2.count)]
        elif len(_pc_hist) >= 6:
            w = [_pc_hist[-6]]
        else:
            w = []
        p.dma("gpsimd", dst_ap, src_ap, sem8, waits=w)
        _pc_hist.append(p.tok(sem8))

    for ch in range(2):
        _pc_dma(wq_bf[ch * 1024:(ch + 1) * 1024, :],
                Wq[ch * 1024:(ch + 1) * 1024, :], pc_q)
    for ch in range(2):
        _pc_dma(swc_bf[ch * 4:(ch + 1) * 4, :, :],
                swc[ch * 4:(ch + 1) * 4, :, :], pc_w)
    for dst8, src8, sem8 in [(wk_bf, Wk, pc_k), (wv_bf, Wv, pc_v),
                             (ws_bf, Ws, pc_s), (wt_bf, Wt, pc_t),
                             (w3_bf, w3t, pc_3)]:
        for ch in range(2):
            _pc_dma(dst8[ch * 1024:(ch + 1) * 1024, :],
                    src8[ch * 1024:(ch + 1) * 1024, :], sem8)
    for ch in range(16):
        _pc_dma(w1_bf[ch * 256:(ch + 1) * 256, :],
                W1[ch * 256:(ch + 1) * 256, :], pc_1)
    for ch in range(16):
        _pc_dma(w2_bf[ch * 512:(ch + 1) * 512, :],
                W2[ch * 512:(ch + 1) * 512, :], pc_2)
    t_pc = {id(Wq): p.tok(pc_q), id(Wk): p.tok(pc_k), id(Wv): p.tok(pc_v),
            id(Ws): p.tok(pc_s), id(Wt): p.tok(pc_t)}
    bf_of = {id(Wq): wq_bf, id(Wk): wk_bf, id(Wv): wv_bf, id(Ws): ws_bf,
             id(Wt): wt_bf}
    t_pc3 = p.tok(pc_3); t_pc1 = p.tok(pc_1); t_pc2 = p.tok(pc_2)
    p.dma("sync", F1B.ap()[:, :], f1024[:, :], c1)
    t_cs = p.tok(c1)
    t_cg = p.tok(c2)
    p.emit("vector", lambda e: e.memset(COLL.ap()[:, :], 0.0), inc=dv)
    p.emit("vector", lambda e: e.memset(ONE1.ap()[:, :], 1.0), inc=dv)
    p.emit("vector", lambda e: e.memset(EPSC.ap()[:, 0:1], 1e-5), inc=dv)
    p.emit("vector", lambda e: e.memset(EPSC.ap()[:, 1:2], NINV), inc=dv)
    t_zero = p.tok(dv)

    ln_done = []
    tp_evac = []
    ln_idx = [0]

    def emit_ln_tile(src_rows, dst, dst_waits=()):
        par = (ln_idx[0] % 2) * 4
        ln_idx[0] += 1
        g2 = GRED2.ap()
        w = [t_zero, t_cs]
        if ln_done:
            w.append((dv, ln_done[-1]))
        p.dma("sync", LNS, src_rows, lq, waits=w)
        t_l = p.tok(lq)
        p.emit("vector", lambda e: e.reduce_sum(g2[:, par:par + 1], LNS,
                                                axis=AX.X), waits=[t_l], inc=dv)
        p.emit("vector", lambda e: e.tensor_scalar(
            g2[:, par + 1:par + 2], g2[:, par:par + 1], -NINV, None, ALU.mult),
            waits=[p.tok(dv)], inc=dv)
        t_m = p.tok(dv)
        p.emit("scalar", lambda e: e.activation(
            LNT, LNS, AF.Square, bias=g2[:, par + 1:par + 2],
            accum_out=g2[:, par + 2:par + 3]), waits=[t_m], inc=sc)
        t_s = p.tok(sc)
        p.emit("scalar", lambda e: e.activation(
            g2[:, par + 3:par + 4], g2[:, par + 2:par + 3], AF.Sqrt,
            bias=EPSC.ap()[:, 0:1], scale=EPSC.ap()[:, 1:2]),
            waits=[t_s], inc=sc)
        t_sq = p.tok(sc)
        p.emit("vector", lambda e: e.reciprocal(
            g2[:, par + 3:par + 4], g2[:, par + 3:par + 4]),
            waits=[t_sq], inc=dv)
        t_r = p.tok(dv)
        p.emit("vector", lambda e, d=dst: e.tensor_scalar(
            d, LNS, g2[:, par + 1:par + 2], g2[:, par + 3:par + 4],
            ALU.add, ALU.mult), waits=[t_r] + list(dst_waits), inc=dv)
        ln_done.append(dv.count)
        return p.tok(dv)

    tp_mm = []
    for i in range(8):
        dstbuf = XNTM[i % 2]
        dw = []
        if i >= 2:
            dw.append((mm, tp_mm[4 * (i - 2) + 3]))
        t_n = emit_ln_tile(x_my[i * 128:(i + 1) * 128, :], dstbuf, dst_waits=dw)
        for g in range(4):
            w = [t_n, t_cs]
            if tp_evac:
                w.append((dv, tp_evac[-1]))
            for j in range(4):
                dt_ = g * 4 + j
                p.emit("tensor", lambda e, s2=dstbuf, jj=j, d2=dt_:
                       e.transpose(pt[:, jj * 128:(jj + 1) * 128],
                                   s2[:, d2 * 128:(d2 + 1) * 128],
                                   identb.ap()[:, :]),
                       waits=w if j == 0 else (), inc=mm if j == 3 else None)
            t_t = p.tok(mm)
            for j in range(4):
                dt_ = g * 4 + j
                p.emit("vector", lambda e, jj=j, d2=dt_, i2=i: e.tensor_copy(
                    xn[d2][:, i2 * 128:(i2 + 1) * 128],
                    pt[:, jj * 128:(jj + 1) * 128]),
                    waits=[t_t] if j == 0 else (), inc=dv if j == 3 else None)
            tp_evac.append(dv.count)
            tp_mm.append(mm.count)
    t_xn = p.tok(dv)
    if tap("xn", [DT * 128, T], BF16):
        for i in range(DT):
            p.dma("sync", tap_out["xn"][i * 128:(i + 1) * 128, :], xn[i], st,
                  waits=[t_xn])

    if _stop("p1"):
        return nc, stack, sorted(taps & set(tap_out))
    for i in range(4):
        emit_ln_tile(x_rt[i * 128:(i + 1) * 128, :], XRT[i])
    t_rt = p.tok(dv)

    r_done = []
    for e_i in range(NE):
        for st_i in range(2):
            sl = (e_i * 2 + st_i) % 2
            buf = SWB[sl]
            w = [(pc_w, 32), t_rt]
            if len(r_done) >= 2:
                w.append((dv, r_done[-2]))
            p.dma("sync", buf, swc_bf[e_i, st_i * 128:(st_i + 1) * 128, :],
                  sws[sl], waits=w)
            t_w = p.tok(sws[sl])
            for b_i in range(2):
                col = b_i * 16 + e_i * 2 + st_i
                p.emit("vector", lambda e, bu=buf, b2=b_i, s2=st_i:
                       e.tensor_mul(RPRD, bu, XRT[b2 * 2 + s2]),
                       waits=[t_w], inc=dv)
                p.emit("vector", lambda e, c2=col: e.reduce_sum(
                    ST1.ap()[:, c2:c2 + 1], RPRD, axis=AX.X),
                    waits=[p.tok(dv)], inc=dv)
            r_done.append(dv.count)
    for b_i in range(2):
        for e_i in range(NE):
            c0 = b_i * 16 + e_i * 2
            o = b_i * 8 + e_i
            p.emit("vector", lambda e, c2=c0, o2=o: e.tensor_add(
                ST2B.ap()[:, o2:o2 + 1], ST1.ap()[:, c2:c2 + 1],
                ST1.ap()[:, c2 + 1:c2 + 2]), waits=[p.tok(dv)], inc=dv)
    t_st2 = p.tok(dv)
    p.emit("tensor", lambda e: e.matmul(px[0:16, 0:1], ST2B.ap()[:, :],
                                        ONE1.ap()[:, :], start=True, stop=True),
           waits=[t_st2], inc=mm)
    t_rl = p.tok(mm)
    p.emit("vector", lambda e: e.tensor_copy(RLOG.ap()[:, :], px[0:16, 0:1]),
           waits=[t_rl], inc=dv)
    t_rlog = p.tok(dv)

    if _stop("p2"):
        return nc, stack, sorted(taps & set(tap_out))
    prst = {"blocks": 0, "block_mm": [], "evac": [], "bank": {}}

    def emit_proj(Wap, dst, rhs, first_waits=()):
        for mg in range(16):
            sl = mg % 4
            buf = WBpr[sl]
            w = list(first_waits) + [t_pc[id(Wap)]]
            if prst["blocks"] >= 4:
                w.append((mm, prst["block_mm"][-4]))
            p.dma("sync", buf.rearrange("p (kc c) -> p kc c", c=128),
                  bf_of[id(Wap)].rearrange("(kc p) m -> p kc m", p=128)[
                      :, :, mg * 128:(mg + 1) * 128], wp[sl], waits=w)
            t_w = p.tok(wp[sl])
            mtile = mg
            for half in range(2):
                bidx = (mtile * 2 + half) % 6
                bank = ps[bidx]
                mmw = [t_w]
                if bidx in prst["bank"]:
                    mmw.append((dv, prst["bank"][bidx]))
                for kc in range(16):
                    lhsT = buf[:, kc * 128:(kc + 1) * 128]
                    p.emit("tensor", lambda e, o=bank, l=lhsT, k2=kc, h=half:
                           e.matmul(o[:, :], l,
                                    rhs[k2][:, h * TH:(h + 1) * TH],
                                    start=(k2 == 0), stop=(k2 == 15)),
                           waits=mmw if kc == 0 else (),
                           inc=mm if kc == 15 else None)
                t_mmv = p.tok(mm)
                p.emit("vector", lambda e, d=dst[mtile], h=half, b2=bank:
                       e.tensor_copy(d[:, h * TH:(h + 1) * TH], b2[:, :]),
                       waits=[t_mmv], inc=dv)
                prst["evac"].append(dv.count)
                prst["bank"][bidx] = dv.count
            prst["block_mm"].append(mm.count)
            prst["blocks"] += 1
        return p.tok(dv)

    t_q = emit_proj(Wq, qv, xn, first_waits=[t_xn])
    if tap("qf", [D, T], BF16):
        for i in range(DT):
            p.dma("sync", tap_out["qf"][i * 128:(i + 1) * 128, :], qv[i], st,
                  waits=[t_q])
    t_k = emit_proj(Wk, kv, xn)
    t_v = emit_proj(Wv, vv, xn)

    if _stop("p3"):
        return nc, stack, sorted(taps & set(tap_out))
    def dma_fcols(fbuf, col0, n, wl, sem):
        if n == 1:
            return
        p.dma("sync", fbuf[:, 0:2048].rearrange("p (kc c) -> p kc c", c=128),
              Fr.rearrange("(kc p) m -> p kc m", p=128)[:, :, col0:col0 + n],
              sem, waits=wl)
        p.dma("sync", fbuf[:, 2048:4096].rearrange("p (kc c) -> p kc c", c=128),
              Fi.rearrange("(kc p) m -> p kc m", p=128)[:, :, col0:col0 + n],
              sem)

    p.emit("vector", lambda e: e.memset(XH[(0, NF)][:, :], 0.0), inc=dv)
    ps_free = []
    fblk_mm = []
    for m in range(NF + 1):
        m1 = (m == NF)
        fbuf = FB[m % 2]
        w = [t_cs]
        if m >= 2:
            w.append((mm, fblk_mm[-2]))
        if m < 2:
            w += [t_v]
        dma_fcols(fbuf, m * 128 if not m1 else 1024, 1 if m1 else 128, w,
                  fbs[m % 2])
        t_f = p.tok(fbs[m % 2])
        pr = slice(0, 1) if m1 else slice(0, 128)
        nplanes = 1 if m1 else 2
        for half in range(2):
            hs = slice(half * TH, (half + 1) * TH)
            first = True
            for ri, rhs in ((0, qv), (1, kv), (2, vv)):
                for pl in range(nplanes):
                    bank = ps[ri * 2 + pl]
                    mmw = [t_f, t_k, t_v]
                    if ps_free:
                        mmw.append((dv, ps_free[-1]))
                    for kc in range(16):
                        base = pl * 2048 + kc * 128
                        l = F1B.ap()[:, kc:kc + 1] if m1 else fbuf[:, base:base + 128]
                        p.emit("tensor", lambda e, o=bank, l2=l, k2=kc, r=rhs,
                               h2=hs, pr2=pr:
                               e.matmul(o[pr2, :], l2, r[k2][:, h2],
                                        start=(k2 == 0), stop=(k2 == 15)),
                               waits=mmw if (kc == 0 and first) else (),
                               inc=mm if kc == 15 else None)
                        first = False
            t_qkv = p.tok(mm)
            banks_used = [ri * 2 + pl for ri in range(3) for pl in range(nplanes)]
            for n_i, i in enumerate(banks_used):
                p.emit("vector", lambda e, i2=i, pr2=pr: e.tensor_copy(
                    SLOT[i2][pr2, :], ps[i2][pr2, :]),
                    waits=[t_qkv] if n_i == 0 else (),
                    inc=dv if n_i == len(banks_used) - 1 else None)
            t_ev = p.tok(dv)
            if m == 0 and half == 0 and tap("fqs", [2 * 128, TH]):
                p.dma("sync", tap_out["fqs"][0:128, :], SLOT[0], st,
                      waits=[t_ev])
                p.dma("sync", tap_out["fqs"][128:256, :], SLOT[1], st)
            for pl in range(nplanes):
                bank = ps[pl]
                for kc in range(16):
                    base = pl * 2048 + kc * 128
                    l = F1B.ap()[:, kc:kc + 1] if m1 else fbuf[:, base:base + 128]
                    p.emit("tensor", lambda e, o=bank, l2=l, k2=kc, h2=hs, pr2=pr:
                           e.matmul(o[pr2, :], l2, xn[k2][:, h2],
                                    start=(k2 == 0), stop=(k2 == 15)),
                           waits=[t_ev] if (kc == 0 and pl == 0) else (),
                           inc=mm if kc == 15 else None)
            t_xf = p.tok(mm)

            sQr, sQi, sKr, sKi, sVr, sVi, sT1, sT2 = SLOT

            def vop(fn, waits=(), inc=dv):
                p.emit("vector", fn, waits=list(waits) + [p.tok(dv)], inc=inc)

            if not m1:
                vop(lambda e, pr=pr: e.tensor_mul(sT1[pr, :], sQr[pr, :], sKr[pr, :]))
                vop(lambda e, pr=pr: e.tensor_mul(sT2[pr, :], sQi[pr, :], sKi[pr, :]))
                vop(lambda e, pr=pr: e.tensor_add(sT1[pr, :], sT1[pr, :], sT2[pr, :]))
                vop(lambda e, pr=pr: e.tensor_mul(sT2[pr, :], sQr[pr, :], sKi[pr, :]))
                vop(lambda e, pr=pr: e.tensor_mul(sKr[pr, :], sQi[pr, :], sKr[pr, :]))
                vop(lambda e, pr=pr: e.tensor_sub(sT2[pr, :], sT2[pr, :], sKr[pr, :]))
                vop(lambda e, pr=pr: e.tensor_mul(sQr[pr, :], sT1[pr, :], sVr[pr, :]))
                vop(lambda e, pr=pr: e.tensor_mul(sKr[pr, :], sT2[pr, :], sVi[pr, :]))
                vop(lambda e, pr=pr: e.tensor_mul(sQi[pr, :], sT1[pr, :], sVi[pr, :]))
                vop(lambda e, pr=pr: e.tensor_mul(sKi[pr, :], sT2[pr, :], sVr[pr, :]))
                vop(lambda e, pr=pr: e.tensor_sub(sQr[pr, :], sQr[pr, :], sKr[pr, :]))
                vop(lambda e, pr=pr: e.tensor_add(sQi[pr, :], sQi[pr, :], sKi[pr, :]))
            else:
                vop(lambda e, pr=pr: e.tensor_mul(sQr[pr, :], sQr[pr, :], sKr[pr, :]))
                vop(lambda e, pr=pr: e.tensor_mul(sQr[pr, :], sQr[pr, :], sVr[pr, :]))
            vop(lambda e, pr=pr: e.tensor_copy(sKr[pr, :], ps[0][pr, :]), waits=[t_xf])
            if not m1:
                vop(lambda e, pr=pr: e.tensor_copy(sKi[pr, :], ps[1][pr, :]))
            ps_free.append(dv.count)
            if m == 0 and half == 0 and tap("attn0", [2 * 128, TH]):
                p.dma("sync", tap_out["attn0"][0:128, :], SLOT[0], st,
                      waits=[p.tok(dv)])
                p.dma("sync", tap_out["attn0"][128:256, :], SLOT[1], st)
            for pl in range(nplanes):
                asl = (sQr, sQi)[pl]
                xsl = (sKr, sKi)[pl]
                vop(lambda e, a=asl, pr=pr: e.tensor_tensor_scan(
                    SCAN[pr, :], a[pr, :], a[pr, :], 0.0, ALU.add, ALU.bypass))
                if half == 0:
                    vop(lambda e, pl2=pl, pr=pr: e.tensor_copy(
                        CARR.ap()[pr, pl2:pl2 + 1], SCAN[pr, TH - 1:TH]),
                        waits=[p.tok(dv)])
                    vop(lambda e, pl2=pl, m2=m, x2=xsl, pr=pr: e.tensor_add(
                        XH[(pl2, m2)][pr, 0:TH], SCAN[pr, :], x2[pr, :]))
                else:
                    vop(lambda e, pl2=pl, pr=pr: e.tensor_scalar(
                        SCAN[pr, :], SCAN[pr, :], CARR.ap()[pr, pl2:pl2 + 1],
                        None, ALU.add), waits=[p.tok(dv)])
                    vop(lambda e, pl2=pl, m2=m, x2=xsl, pr=pr: e.tensor_add(
                        XH[(pl2, m2)][pr, TH:T], SCAN[pr, :], x2[pr, :]))
                    vop(lambda e, pl2=pl, m2=m, pr=pr: e.tensor_copy(
                        COLL.ap()[pr, m2 * 2 + pl2:m2 * 2 + pl2 + 1],
                        SCAN[pr, TH - 1:TH]), waits=[p.tok(dv)])
        fblk_mm.append(mm.count)
    t_p4 = p.tok(dv)
    if tap("xh", [17 * 128, T], BF16):
        _ti = 0
        for pl in range(2):
            for i in range(9 if pl == 0 else 8):
                p.dma("sync", tap_out["xh"][_ti * 128:(_ti + 1) * 128, :],
                      XH[(pl, i)], st, waits=[t_p4])
                _ti += 1
    if tap("coll", [128, 18]):
        p.dma("sync", tap_out["coll"][:, :], COLL.ap()[:, :], st, waits=[t_p4])

    if _stop("p4"):
        return nc, stack, sorted(taps & set(tap_out))
    p.dma("gpsimd",
          cc_in.ap()[0:1, 0:2304].rearrange("one (p c) -> (one p) c", p=128),
          COLL.ap()[:, :], ci_s, waits=[t_p4])
    p.dma("gpsimd",
          cc_in.ap()[0:1, 2304:2320].rearrange("one (p c) -> (one p) c", p=16),
          RLOG.ap()[:, :], ci_s, waits=[t_rlog])
    t_ci = p.tok(ci_s)
    p.emit("gpsimd", lambda e: e.collective_compute(
        "AllGather", ALU.bypass, replica_groups=[list(range(NCORE))],
        ins=[cc_in.ap()], outs=[cc_out.ap()]), waits=[t_ci], inc=cc)
    t_cc = p.tok(cc)

    t_s1 = emit_proj(Ws, qv, xn, first_waits=[t_p4])
    t_t1 = emit_proj(Wt, kv, xn)
    for i in range(DT):
        p.dma("sync", t1_sp[i * 128:(i + 1) * 128, :], kv[i], sp_s,
              waits=[t_t1] if i == 0 else ())
    t_t1sp = p.tok(sp_s)

    if _stop("p5"):
        return nc, stack, sorted(taps & set(tap_out))
    p.dma("sync", OFFG.ap().rearrange("p (c k) -> p c k", k=18),
          cc_out.ap()[:, 0:2304].rearrange("c (p k) -> p c k", p=128),
          gl_s, waits=[t_cc])
    p.dma("sync", LG.ap().rearrange("p (c e2) -> p c e2", e2=8),
          cc_out.ap()[:, 2304:2320].rearrange("c (b e2) -> b c e2", b=2),
          gl_s, waits=[t_cc])
    t_gl = p.tok(gl_s)
    p.emit("vector", lambda e: e.memset(OFFC.ap()[:, :], 0.0), inc=dv)
    for c_i in range(NCORE):
        p.emit("vector", lambda e, c2=c_i: e.scalar_tensor_tensor(
            OFFC.ap()[:, :], OFFG.ap()[:, c2 * 18:(c2 + 1) * 18],
            maskb.ap()[:, c2:c2 + 1], OFFC.ap()[:, :], ALU.mult, ALU.add),
            waits=[t_gl, p.tok(dv)], inc=dv)
    t_off = p.tok(dv)

    p.emit("vector", lambda e: e.tensor_add(
        GT.ap()[:, :], LG.ap()[:, 0:8], LG.ap()[:, 8:16]), waits=[t_gl], inc=dv)
    for c_i in range(2, NCORE):
        p.emit("vector", lambda e, c2=c_i: e.tensor_add(
            GT.ap()[:, :], GT.ap()[:, :], LG.ap()[:, c2 * 8:(c2 + 1) * 8]),
            waits=[p.tok(dv)], inc=dv)
    p.emit("vector", lambda e: e.tensor_add(GT.ap()[:, :], GT.ap()[:, :],
                                            swbb.ap()[:, :]),
           waits=[p.tok(dv)], inc=dv)
    t_lgt = p.tok(dv)
    if tap("logits", [2, 8]):
        p.dma("sync", tap_out["logits"][:, :], GT.ap()[:, :], st, waits=[t_lgt])
    p.emit("vector", lambda e: e.reduce_max(GRED.ap()[:, :], GT.ap()[:, :],
                                            axis=AX.X, negate=True),
           waits=[p.tok(dv)], inc=dv)
    t_gm = p.tok(dv)
    p.emit("scalar", lambda e: e.activation(GTt1.ap()[:, :], GT.ap()[:, :],
                                            AF.Exp, bias=GRED.ap()[:, 0:1]),
           waits=[t_gm], inc=sc)
    t_ge = p.tok(sc)
    p.emit("vector", lambda e: e.reduce_sum(GRED.ap()[:, :], GTt1.ap()[:, :],
                                            axis=AX.X), waits=[t_ge], inc=dv)
    p.emit("vector", lambda e: e.reciprocal(GRED.ap()[:, :], GRED.ap()[:, :]),
           waits=[p.tok(dv)], inc=dv)
    p.emit("vector", lambda e: e.tensor_scalar(GTt1.ap()[:, :], GTt1.ap()[:, :],
                                               GRED.ap()[:, 0:1], None,
                                               ALU.mult), waits=[p.tok(dv)], inc=dv)
    p.emit("vector", lambda e: e.reduce_max(GRED.ap()[:, :], GTt1.ap()[:, :],
                                            axis=AX.X), waits=[p.tok(dv)], inc=dv)
    p.emit("vector", lambda e: e.tensor_scalar(GTt2.ap()[:, :], GTt1.ap()[:, :],
                                               GRED.ap()[:, 0:1], None,
                                               ALU.is_lt), waits=[p.tok(dv)], inc=dv)
    p.emit("vector", lambda e: e.tensor_mul(GTt2.ap()[:, :], GTt2.ap()[:, :],
                                            GTt1.ap()[:, :]), waits=[p.tok(dv)], inc=dv)
    p.emit("vector", lambda e: e.reduce_max(GRED.ap()[:, :], GTt2.ap()[:, :],
                                            axis=AX.X), waits=[p.tok(dv)], inc=dv)
    p.emit("vector", lambda e: e.tensor_scalar(GTt2.ap()[:, :], GTt1.ap()[:, :],
                                               GRED.ap()[:, 0:1], None,
                                               ALU.is_ge), waits=[p.tok(dv)], inc=dv)
    p.emit("vector", lambda e: e.tensor_mul(GT.ap()[:, :], GTt1.ap()[:, :],
                                            GTt2.ap()[:, :]), waits=[p.tok(dv)], inc=dv)
    p.emit("vector", lambda e: e.tensor_copy(GTB.ap()[:, :], GT.ap()[:, :]),
           waits=[p.tok(dv)], inc=dv)
    t_gt = p.tok(dv)
    if tap("gates", [2, 8]):
        p.dma("sync", tap_out["gates"][:, :], GT.ap()[:, :], st, waits=[t_gt])
    p.emit("tensor", lambda e: e.matmul(px[0:NE, 0:1], GTB.ap()[:, :],
                                        bselb.ap()[:, :], start=True, stop=True),
           waits=[t_gt, t_cg], inc=mm)
    t_a = p.tok(mm)
    p.emit("vector", lambda e: e.tensor_copy(MGC.ap()[:, :], px[0:NE, 0:1]),
           waits=[t_a], inc=dv)
    t_b = p.tok(dv)
    p.emit("tensor", lambda e: e.matmul(px[0:128, 1:2], fb2b.ap()[:, :],
                                        MGC.ap()[:, :], start=True, stop=True),
           waits=[t_b], inc=mm)
    p.emit("tensor", lambda e: e.matmul(px[0:1, 2:10], bselb.ap()[:, :],
                                        GTB.ap()[:, :], start=True, stop=True),
           inc=mm)
    t_c = p.tok(mm)
    p.emit("vector", lambda e: e.tensor_copy(B2G.ap()[:, :], px[0:128, 1:2]),
           waits=[t_c], inc=dv)
    p.emit("vector", lambda e: e.tensor_copy(MGR.ap()[:, :], px[0:1, 2:10]),
           inc=dv)
    t_d = p.tok(dv)
    p.emit("tensor", lambda e: e.matmul(px[0:128, 16:24], onerb.ap()[:, :],
                                        MGR.ap()[:, :], start=True, stop=True),
           waits=[t_d], inc=mm)
    t_e2 = p.tok(mm)
    p.emit("vector", lambda e: e.tensor_copy(GBC.ap()[:, :], px[0:128, 16:24]),
           waits=[t_e2], inc=dv)
    t_gbc = p.tok(dv)
    for e_i in range(NE):
        p.emit("vector", lambda e, e3=e_i: e.tensor_scalar(
            w2gb.ap()[:, e3 * 128:(e3 + 1) * 128],
            fw2b.ap()[:, e3 * 128:(e3 + 1) * 128],
            GBC.ap()[:, e3:e3 + 1], None, ALU.mult), waits=[t_gbc], inc=dv)
    t_w2g = p.tok(dv)

    ob_guard = {}
    for band in range(NHEAD):
        for half in range(2):
            hs = slice(half * TH, (half + 1) * TH)
            obank = ps[4 + half]
            gelu_sc = []

            def mm1(e_i):
                hbank = ps[e_i % 2]
                w = [t_w2g]
                if e_i >= 2:
                    w.append((mm, mm2_mm[e_i - 2]))
                p.emit("tensor", lambda e, b2=band, e3=e_i, h2=hs, hb=hbank:
                       e.matmul(hb[:, :],
                                fw1b.ap()[:, e3 * 128:(e3 + 1) * 128],
                                xn[b2][:, h2], start=True, stop=True),
                       waits=w, inc=mm)
                mm1_mm.append(mm.count)

            mm1_mm = []
            mm2_mm = []
            mm1(0)
            mm1(1)
            for e_i in range(NE):
                hbank = ps[e_i % 2]
                hst = HSTs[e_i % 2]
                w = [(mm, mm1_mm[e_i])]
                if e_i >= 2:
                    w.append((mm, mm2_mm[e_i - 2]))
                p.emit("scalar", lambda e, e3=e_i, hb=hbank, h4=hst:
                       e.activation(h4, hb[:, :], AF.Gelu,
                                    bias=fb1b.ap()[:, e3:e3 + 1]),
                       waits=w, inc=sc)
                gelu_sc.append(sc.count)
                ow = [(sc, gelu_sc[e_i])]
                if e_i == 0 and half in ob_guard:
                    ow.append((dv, ob_guard[half]))
                p.emit("tensor", lambda e, e3=e_i, ob=obank, h4=hst:
                       e.matmul(ob[:, :],
                                w2gb.ap()[:, e3 * 128:(e3 + 1) * 128],
                                h4[:, 0:TH], start=(e3 == 0),
                                stop=(e3 == NE - 1)),
                       waits=ow, inc=mm)
                mm2_mm.append(mm.count)
                if e_i + 2 < NE:
                    mm1(e_i + 2)
            t_o = p.tok(mm)
            p.emit("vector", lambda e, ob=obank: e.tensor_scalar(
                SLOT[6][:, :], ob[:, :], B2G.ap()[:, 0:1], None, ALU.add),
                waits=[t_o, t_t1sp, t_w2g], inc=dv)
            p.emit("vector", lambda e, b2=band, h2=hs: e.tensor_add(
                kv[b2][:, h2], SLOT[6][:, :], xn[b2][:, h2]), inc=dv)
            ob_guard[half] = dv.count
    t_u = p.tok(dv)
    if tap("u", [DT * 128, T], BF16):
        for i in range(DT):
            p.dma("sync", tap_out["u"][i * 128:(i + 1) * 128, :], kv[i], st,
                  waits=[t_u])

    w3_mm = []
    w3_ev = []
    for mg in range(16):
        sl = mg % 4
        buf = WB3[sl]
        w = [t_u]
        if mg >= 4:
            w.append((mm, w3_mm[-4]))
        p.dma("sync", buf.rearrange("p (kc c) -> p kc c", c=128),
              w3_bf.rearrange("(kc p) m -> p kc m", p=128)[
                  :, :, mg * 128:(mg + 1) * 128], w3s[sl], waits=w + [t_pc3],
              )
        t_w = p.tok(w3s[sl])
        for half in range(2):
            bank = ps[half]
            mmw = [t_w]
            if len(w3_ev) >= 2:
                mmw.append((dv, w3_ev[-2]))
            for kc in range(16):
                lhsT = buf[:, kc * 128:(kc + 1) * 128]
                p.emit("tensor", lambda e, o=bank, l=lhsT, k2=kc, h=half:
                       e.matmul(o[:, :], l, kv[k2][:, h * TH:(h + 1) * TH],
                                start=(k2 == 0), stop=(k2 == 15)),
                       waits=mmw if kc == 0 else (), inc=mm if kc == 15 else None)
            t_m2 = p.tok(mm)
            p.emit("vector", lambda e, h=half, m2=mg, b2=bank: e.tensor_scalar(
                vv[m2][:, h * TH:(h + 1) * TH], b2[:, :],
                b3cb.ap()[:, m2:m2 + 1], None, ALU.add), waits=[t_m2], inc=dv)
            w3_ev.append(dv.count)
        w3_mm.append(mm.count)
        t_ffv = p.tok(dv)
        p.dma("sync", ff_sp[mg * 128:(mg + 1) * 128, :], vv[mg], sp_s,
              waits=[t_ffv])
    t_ffsp = p.tok(sp_s)
    if tap("ffn", [D, T], BF16):
        for i in range(DT):
            p.dma("sync", tap_out["ffn"][i * 128:(i + 1) * 128, :], vv[i], st,
                  waits=[t_ffsp])

    if _stop("p6"):
        return nc, stack, sorted(taps & set(tap_out))
    p.emit("vector", lambda e: e.memset(TA[(0, NF)][:, :], 0.0), inc=dv)
    p7_mm = []
    p7_free = []
    for m in range(NF + 1):
        m1 = (m == NF)
        fbuf = FB[m % 2]
        w = [t_s1, t_ffsp, t_off]
        if m >= 2:
            w.append((mm, p7_mm[-2]))
        dma_fcols(fbuf, m * 128 if not m1 else 1024, 1 if m1 else 128, w,
                  fbs[m % 2])
        t_f = p.tok(fbs[m % 2])
        pr = slice(0, 1) if m1 else slice(0, 128)
        nplanes = 1 if m1 else 2
        for half in range(2):
            hs = slice(half * TH, (half + 1) * TH)
            for pl in range(nplanes):
                bank = ps[pl]
                mmw = [t_f]
                if p7_free:
                    mmw.append((dv, p7_free[-1]))
                for kc in range(16):
                    base = pl * 2048 + kc * 128
                    l = F1B.ap()[:, kc:kc + 1] if m1 else fbuf[:, base:base + 128]
                    p.emit("tensor", lambda e, o=bank, l2=l, k2=kc, h2=hs,
                           pr2=pr:
                           e.matmul(o[pr2, :], l2, qv[k2][:, h2],
                                    start=(k2 == 0), stop=(k2 == 15)),
                           waits=mmw if (kc == 0 and pl == 0) else (),
                           inc=mm if kc == 15 else None)
            t_sf = p.tok(mm)
            sQr, sQi, sKr, sKi, sVr, sVi, sT1, sT2 = SLOT

            def vop(fn, waits=(), inc=dv):
                p.emit("vector", fn, waits=list(waits) + [p.tok(dv)], inc=inc)

            vop(lambda e, pr=pr: e.tensor_copy(sT1[pr, :], ps[0][pr, :]), waits=[t_sf])
            if not m1:
                vop(lambda e, pr=pr: e.tensor_copy(sT2[pr, :], ps[1][pr, :]))
            p7_free.append(dv.count)
            vop(lambda e, m2=m, pr=pr, hs=hs: e.tensor_scalar(
                sQr[pr, :], XH[(0, m2)][pr, hs],
                OFFC.ap()[pr, m2 * 2:m2 * 2 + 1], None, ALU.add))
            if not m1:
                vop(lambda e, m2=m, pr=pr, hs=hs: e.tensor_scalar(
                    sQi[pr, :], XH[(1, m2)][pr, hs],
                    OFFC.ap()[pr, m2 * 2 + 1:m2 * 2 + 2], None, ALU.add))
                vop(lambda e, pr=pr: e.tensor_mul(sKr[pr, :], sT1[pr, :], sQr[pr, :]))
                vop(lambda e, pr=pr: e.tensor_mul(sKi[pr, :], sT2[pr, :], sQi[pr, :]))
                vop(lambda e, pr=pr: e.tensor_add(sVr[pr, :], sKr[pr, :], sKi[pr, :]))
                vop(lambda e, m2=m, h2=hs, pr=pr: e.tensor_copy(
                    TA[(0, m2)][pr, h2], sVr[pr, :]))
                vop(lambda e, pr=pr: e.tensor_mul(sKr[pr, :], sT1[pr, :], sQi[pr, :]))
                vop(lambda e, pr=pr: e.tensor_mul(sKi[pr, :], sT2[pr, :], sQr[pr, :]))
                vop(lambda e, pr=pr: e.tensor_sub(sVi[pr, :], sKr[pr, :], sKi[pr, :]))
                vop(lambda e, m2=m, h2=hs, pr=pr: e.tensor_copy(
                    TA[(1, m2)][pr, h2], sVi[pr, :]))
                vop(lambda e, pr=pr: e.tensor_mul(sKr[pr, :], sT1[pr, :], sT1[pr, :]))
                vop(lambda e, pr=pr: e.tensor_mul(sKi[pr, :], sT2[pr, :], sT2[pr, :]))
                vop(lambda e, pr=pr: e.tensor_add(sKr[pr, :], sKr[pr, :], sKi[pr, :]))
                vop(lambda e, pr=pr: e.tensor_scalar(sKr[pr, :], sKr[pr, :],
                                              -NINV, NINV, ALU.mult, ALU.add))
                vop(lambda e, m2=m, h2=hs, pr=pr: e.tensor_mul(
                    XH[(0, m2)][pr, h2], sQr[pr, :], sKr[pr, :]))
                vop(lambda e, m2=m, h2=hs, pr=pr: e.tensor_mul(
                    XH[(1, m2)][pr, h2], sQi[pr, :], sKr[pr, :]))
            else:
                vop(lambda e, pr=pr: e.tensor_mul(sVr[pr, :], sT1[pr, :], sQr[pr, :]))
                vop(lambda e, m2=m, h2=hs, pr=pr: e.tensor_copy(
                    TA[(0, m2)][pr, h2], sVr[pr, :]))
                vop(lambda e, pr=pr: e.tensor_mul(sKr[pr, :], sT1[pr, :], sT1[pr, :]))
                vop(lambda e, pr=pr: e.tensor_scalar(sKr[pr, :], sKr[pr, :],
                                              -NINV, NINV, ALU.mult, ALU.add))
                vop(lambda e, m2=m, h2=hs, pr=pr: e.tensor_mul(
                    XH[(0, m2)][pr, h2], sQr[pr, :], sKr[pr, :]))
        p7_mm.append(mm.count)
    t_p7 = p.tok(dv)
    if tap("ta", [17 * 128, T], BF16):
        _ti = 0
        for pl in range(2):
            for i in range(9 if pl == 0 else 8):
                p.dma("sync", tap_out["ta"][_ti * 128:(_ti + 1) * 128, :],
                      TA[(pl, i)], st, waits=[t_p7])
                _ti += 1
    if tap("zs", [17 * 128, T], BF16):
        _ti = 0
        for pl in range(2):
            for i in range(9 if pl == 0 else 8):
                p.dma("sync", tap_out["zs"][_ti * 128:(_ti + 1) * 128, :],
                      XH[(pl, i)], st, waits=[t_p7])
                _ti += 1
    _ti = 0
    for pl in range(2):
        for i in range(9 if pl == 0 else 8):
            p.dma("sync", ta_sp[_ti * 128:(_ti + 1) * 128, :], TA[(pl, i)],
                  sp_s, waits=[t_p7] if _ti == 0 else ())
            p.dma("sync", zs_sp[_ti * 128:(_ti + 1) * 128, :], XH[(pl, i)], sp_s)
            _ti += 1
    t_spill = p.tok(sp_s)

    if _stop("p7"):
        return nc, stack, sorted(taps & set(tap_out))
    prev_done = None
    for half in range(2):
        h0 = half * TH
        relw = [t_spill]
        if prev_done is not None:
            relw.append(prev_done)
        _ti = 0
        for pl in range(2):
            for i in range(9 if pl == 0 else 8):
                p.dma("sync", TAH[(pl, i)],
                      ta_sp[_ti * 128:(_ti + 1) * 128, h0:h0 + TH], ta_s,
                      waits=relw if _ti == 0 else ())
                _ti += 1
        t_tah = p.tok(ta_s)
        mir_ev = []
        for pl in range(2):
            for Dd in range(8, 16):
                mi = (Dd - 8) + (0 if pl == 0 else 8)
                bank = ps[mi % 4]
                mw = [t_tah]
                if len(mir_ev) >= 4:
                    mw.append((dv, mir_ev[-4]))
                two = (pl == 0) or (Dd > 8)
                RA = RevAb if pl == 0 else RevAnb
                P0 = Pick0b if pl == 0 else Pick0nb
                p.emit("tensor", lambda e, b2=bank, ra=RA, s2=(pl, 15 - Dd),
                       tw=two:
                       e.matmul(b2[:, :], ra.ap()[:, :], TAH[s2],
                                start=True, stop=not tw),
                       waits=mw, inc=None if two else mm)
                if two:
                    p.emit("tensor", lambda e, b2=bank, p0=P0, s2=(pl, 16 - Dd):
                           e.matmul(b2[:, :], p0.ap()[:, :], TAH[s2],
                                    start=False, stop=True), inc=mm)
                t_mi = p.tok(mm)
                p.emit("vector", lambda e, m2=mi, b2=bank: e.tensor_copy(
                    MIR[m2], b2[:, :]), waits=[t_mi], inc=dv)
                mir_ev.append(dv.count)
        t_mir = p.tok(dv)

        def m1rhs(j):
            if j < 8:
                return TAH[(0, j)]
            if j < 16:
                return MIR[j - 8]
            if j < 24:
                return TAH[(1, j - 16)]
            return MIR[8 + (j - 24)]

        m1_mm = []
        m1_sc = []
        if half == 0:
            m1_bank = {}
        for blk in range(64):
            sl = blk % 4
            buf = W1B[sl]
            w = [t_mir, t_tah]
            if blk >= 4:
                w.append((mm, m1_mm[-4]))
            p.dma("sync", buf.rearrange("p (kc c) -> p kc c", c=128),
                  w1_bf.rearrange("(kc p) m -> p kc m", p=128)[
                      :, :, blk * 128:(blk + 1) * 128], w1s[sl],
                  waits=w + [t_pc1])
            t_w = p.tok(w1s[sl])
            bank = ps[blk % 3]
            mmw = [t_w, t_mir]
            if (blk % 3) in m1_bank:
                mmw.append((sc, m1_bank[blk % 3]))
            for kc in range(32):
                l = buf[:, kc * 128:(kc + 1) * 128]
                p.emit("tensor", lambda e, o=bank, l2=l, k2=kc:
                       e.matmul(o[:, :], l2, m1rhs(k2),
                                start=(k2 == 0), stop=(k2 == 31)),
                       waits=mmw if kc == 0 else (), inc=mm if kc == 31 else None)
            m1_mm.append(mm.count)
            t_m1 = p.tok(mm)
            p.emit("scalar", lambda e, b2=bank, m2=blk: e.activation(
                HB[m2][:, :], b2[:, :], AF.Relu,
                bias=b1cb.ap()[:, m2:m2 + 1]), waits=[t_m1], inc=sc)
            m1_sc.append(sc.count)
            m1_bank[blk % 3] = sc.count
        t_h = p.tok(sc)
        if half == 0 and _stop("p8m1"):
            return nc, stack, sorted(taps & set(tap_out))
        if half == 0 and tap("h0", [128, TH], BF16):
            p.dma("sync", tap_out["h0"][:, :], HB[0][:, :], st, waits=[t_h])

        _ti = 0
        for pl in range(2):
            for i in range(9 if pl == 0 else 8):
                p.dma("sync", ZSH[(pl, i)],
                      zs_sp[_ti * 128:(_ti + 1) * 128, h0:h0 + TH], zs_s,
                      waits=[(sc, m1_sc[-1])] if _ti == 0 else ())
                _ti += 1
        t_zsh = p.tok(zs_s)
        for i in range(DT):
            p.dma("sync", T1H[i], t1_sp[i * 128:(i + 1) * 128, h0:h0 + TH],
                  t1_s, waits=[(mm, m1_mm[-1])] if i == 0 else ())
        t_t1h = p.tok(t1_s)

        m2_mm = []
        tsg_sc = []
        zb_dv = []
        tg_mm = []
        mr_dv = []
        for j in range(16):
            for sub, mtile in ((0, j), (1, j + 16)):
                bank = ps[sub]
                for hh in range(2):
                    bi = (4 * j + 2 * sub + hh) % 4
                    buf = W2B[bi]
                    w = [t_h, t_t1h]
                    if len(m2_mm) >= 4:
                        w.append((mm, m2_mm[-4]))
                    p.dma("sync", buf.rearrange("p (kc c) -> p kc c", c=128),
                          w2_bf.rearrange("(kc p) m -> p kc m", p=128)[
                              :, hh * 32:(hh + 1) * 32,
                              mtile * 128:(mtile + 1) * 128], w2s[bi],
                          waits=w + [t_pc2])
                    t_w = p.tok(w2s[bi])
                    mmw = [t_w, t_h]
                    if len(tsg_sc) >= 1:
                        mmw.append((sc, tsg_sc[-1]))
                    for kc in range(32):
                        l = buf[:, kc * 128:(kc + 1) * 128]
                        kg = hh * 32 + kc
                        p.emit("tensor", lambda e, o=bank, l2=l, k2=kg:
                               e.matmul(o[:, :], l2, HB[k2][:, :],
                                        start=(k2 == 0), stop=(k2 == 63)),
                               waits=mmw if kc == 0 else (),
                               inc=mm if (kc == 31) else None)
                    m2_mm.append(mm.count)
            t_tr = p.tok(mm)
            fbuf = FBj[j % 2]
            fw = [t_t1h]
            if j >= 2:
                fw.append((mm, tg_mm[-2]))
            dma_fcols(fbuf, j * 128, 128, fw, fjs[j % 2])
            t_fj = p.tok(fjs[j % 2])
            for pl in range(2):
                bank = ps[2 + pl]
                mmw = [t_fj]
                if zb_dv:
                    mmw.append((dv, zb_dv[-1]))
                for kc in range(16):
                    base = pl * 2048 + kc * 128
                    l = fbuf[:, base:base + 128]
                    p.emit("tensor", lambda e, o=bank, l2=l, k2=kc:
                           e.matmul(o[:, :], l2, T1H[k2],
                                    start=(k2 == 0), stop=(k2 == 15)),
                           waits=mmw if (kc == 0 and pl == 0) else (),
                           inc=mm if kc == 15 else None)
            tg_mm.append(mm.count)
            t_tg = p.tok(mm)
            tsg_w = [(mm, m2_mm[-3])]
            if zb_dv:
                tsg_w.append((dv, zb_dv[-1]))
            p.emit("scalar", lambda e, j2=j: e.activation(
                TSG[:, 0:TH], ps[0][:, :], AF.Identity,
                bias=b2cb.ap()[:, j2:j2 + 1]), waits=tsg_w, inc=sc)
            p.emit("scalar", lambda e, j2=j: e.activation(
                TSG[:, TH:2 * TH], ps[1][:, :], AF.Identity,
                bias=b2cb.ap()[:, 16 + j2:17 + j2]), waits=[(mm, m2_mm[-1])],
                inc=sc)
            tsg_sc.append(sc.count)
            t_tsg = p.tok(sc)
            if j < 8:
                zsr = ZSH[(0, j)]
                zsi = ZSH[(1, j)]
                t_zm = None
            else:
                Dd = j
                for pl in range(2):
                    bank = ps[4 + pl]
                    two = (pl == 0) or (Dd > 8)
                    RA = RevAb if pl == 0 else RevAnb
                    P0 = Pick0b if pl == 0 else Pick0nb
                    mw = [t_zsh]
                    if mr_dv:
                        mw.append((dv, mr_dv[-1]))
                    p.emit("tensor", lambda e, b2=bank, ra=RA,
                           s2=(pl, 15 - Dd), tw=two:
                           e.matmul(b2[:, :], ra.ap()[:, :], ZSH[s2],
                                    start=True, stop=not tw),
                           waits=mw, inc=None if two else mm)
                    if two:
                        p.emit("tensor", lambda e, b2=bank, p0=P0,
                               s2=(pl, 16 - Dd):
                               e.matmul(b2[:, :], p0.ap()[:, :], ZSH[s2],
                                        start=False, stop=True), inc=mm)
                t_zmm = p.tok(mm)
                p.emit("vector", lambda e: e.tensor_copy(
                    ZMR[:, 0:TH], ps[4][:, :]), waits=[t_zmm], inc=dv)
                p.emit("vector", lambda e: e.tensor_copy(
                    ZMR[:, TH:2 * TH], ps[5][:, :]), inc=dv)
                mr_dv.append(dv.count)
                zsr = ZMR[:, 0:TH]
                zsi = ZMR[:, TH:2 * TH]

            def vop(fn, waits=(), inc=dv):
                p.emit("vector", fn, waits=list(waits) + [p.tok(dv)], inc=inc)

            vop(lambda e: e.scalar_tensor_tensor(
                SLOT[0][:, :], ps[2][:, :], NINV, TSG[:, 0:TH],
                ALU.mult, ALU.mult), waits=[t_tg, t_tsg, t_zsh])
            vop(lambda e: e.scalar_tensor_tensor(
                SLOT[1][:, :], ps[3][:, :], NINV, TSG[:, TH:2 * TH],
                ALU.mult, ALU.mult))
            vop(lambda e: e.scalar_tensor_tensor(
                SLOT[2][:, :], ps[2][:, :], NINV, TSG[:, TH:2 * TH],
                ALU.mult, ALU.mult))
            vop(lambda e: e.scalar_tensor_tensor(
                SLOT[3][:, :], ps[3][:, :], NINV, TSG[:, 0:TH],
                ALU.mult, ALU.mult))
            vop(lambda e: e.tensor_sub(SLOT[0][:, :], SLOT[0][:, :],
                                       SLOT[1][:, :]))
            vop(lambda e, z=zsr, j2=j: e.tensor_add(ZB[j2], SLOT[0][:, :], z))
            vop(lambda e: e.tensor_add(SLOT[2][:, :], SLOT[2][:, :],
                                       SLOT[3][:, :]))
            vop(lambda e, z=zsi, j2=j: e.tensor_add(ZB[16 + j2],
                                                    SLOT[2][:, :], z))
            zb_dv.append(dv.count)
        t_zb = p.tok(dv)
        if half == 0 and _stop("p9z"):
            return nc, stack, sorted(taps & set(tap_out))
        if half == 0 and tap("z", [32 * 128, TH], BF16):
            for i in range(32):
                p.dma("sync", tap_out["z"][i * 128:(i + 1) * 128, :], ZB[i],
                      st, waits=[t_zb])

        for i in range(DT):
            p.dma("sync", FFH[i], ff_sp[i * 128:(i + 1) * 128, h0:h0 + TH],
                  ff_s, waits=[(mm, m2_mm[-1])] if i == 0 else ())
        t_ffh = p.tok(ff_s)

        if_mm = []
        aos_dv = []
        for dt_ in range(DT):
            fbuf = FBI[dt_ % 2]
            fw = [t_zb]
            if dt_ >= 2:
                fw.append((mm, if_mm[-2]))
            dma_fcols(fbuf, dt_ * 128, 128, fw, fis[dt_ % 2])
            t_fi = p.tok(fis[dt_ % 2])
            bank = ps[dt_ % 2]
            mmw = [t_fi, t_zb]
            if len(aos_dv) >= 2:
                mmw.append((dv, aos_dv[-2]))
            for pl in range(2):
                for kc in range(16):
                    base = pl * 2048 + kc * 128
                    l = fbuf[:, base:base + 128]
                    zi = pl * 16 + kc
                    p.emit("tensor", lambda e, o=bank, l2=l, z2=zi:
                           e.matmul(o[:, :], l2, ZB[z2],
                                    start=(z2 == 0), stop=(z2 == 31)),
                           waits=mmw if (kc == 0 and pl == 0) else (),
                           inc=mm if (pl == 1 and kc == 15) else None)
            if_mm.append(mm.count)
            t_if = p.tok(mm)
            p.emit("vector", lambda e, d2=dt_, b2=bank: e.tensor_add(
                AOS[d2], b2[:, :], FFH[d2]), waits=[t_if, t_ffh], inc=dv)
            aos_dv.append(dv.count)
        t_aos = p.tok(dv)
        if half == 0 and tap("aos", [DT * 128, TH], BF16):
            for i in range(DT):
                p.dma("sync", tap_out["aos"][i * 128:(i + 1) * 128, :],
                      AOS[i], st, waits=[t_aos])

        st_done = []
        for tt in range(4):
            grow = half * 4 + tt
            xw = [t_aos]
            if st_done:
                xw.append((st, st_done[-1]))
            p.dma("sync", XRL[0], x_my[grow * 128:(grow + 1) * 128, :], xr_s,
                  waits=xw)
            t_x = p.tok(xr_s)
            add_dv = []
            for g in range(4):
                tw = [t_aos]
                if add_dv:
                    tw.append((dv, add_dv[-1]))
                elif st_done:
                    tw.append((st, st_done[-1]))
                for j in range(4):
                    d2 = g * 4 + j
                    p.emit("tensor", lambda e, d3=d2, t2=tt, jj=j:
                           e.transpose(pt[:, jj * 128:(jj + 1) * 128],
                                       AOS[d3][:, t2 * 128:(t2 + 1) * 128],
                                       identb.ap()[:, :]),
                           waits=tw if j == 0 else (),
                           inc=mm if j == 3 else None)
                t_t = p.tok(mm)
                aw = [t_t, t_x]
                if g == 0 and st_done:
                    aw.append((st, st_done[-1]))
                p.emit("vector", lambda e, g2=g: e.tensor_add(
                    OUTB[:, g2 * 512:(g2 + 1) * 512], pt[:, :],
                    XRL[0][:, g2 * 512:(g2 + 1) * 512]),
                    waits=aw, inc=dv)
                add_dv.append(dv.count)
            t_ob = p.tok(dv)
            p.dma("sync", y[grow * 128:(grow + 1) * 128, :], OUTB, st,
                  waits=[t_ob])
            st_done.append(st.count)
        prev_done = p.tok(st)
        if half == 0 and _stop("p10h0"):
            return nc, stack, sorted(taps & set(tap_out))

    p.emit("sync", lambda e: None, waits=[p.tok(st)])
    p.finalize()
    return nc, stack, sorted(taps & set(tap_out))


_CACHE = {}


def _consts():
    if "c" in _CACHE:
        return _CACHE["c"]
    n = np.arange(D, dtype=np.float64)
    th = 2.0 * np.pi * np.outer(n, n) / D
    Frv = np.cos(th).astype(ml_dtypes.bfloat16)
    Fiv = (-np.sin(th)).astype(ml_dtypes.bfloat16)
    ident = np.eye(128, dtype=ml_dtypes.bfloat16)
    RevAv = np.zeros((128, 128), dtype=np.float32)
    for q in range(1, 128):
        RevAv[128 - q, q] = 1.0
    Pick0v = np.zeros((128, 128), dtype=np.float32)
    Pick0v[0, 0] = 1.0
    onesrv = np.ones((1, 128), dtype=ml_dtypes.bfloat16)
    f1024v = np.ascontiguousarray(
        np.cos(th[:, 1024]).reshape(16, 128).T).astype(ml_dtypes.bfloat16)
    _CACHE["c"] = (Frv, Fiv, ident, RevAv.astype(ml_dtypes.bfloat16),
                   (-RevAv).astype(ml_dtypes.bfloat16),
                   Pick0v.astype(ml_dtypes.bfloat16),
                   (-Pick0v).astype(ml_dtypes.bfloat16), onesrv, f1024v)
    return _CACHE["c"]


def make_inputs(kw, core):
    (Frv, Fiv, ident, RevAv, RevAnv, Pick0v, Pick0nv, onesrv,
     f1024v) = _consts()
    x = np.asarray(kw["L0_attn_x"], dtype=np.float32)
    b = core // 4
    q = core % 4
    d = {
        "x_my": np.ascontiguousarray(x[b, q * T:(q + 1) * T, :]),
        "x_rt": np.ascontiguousarray(
            np.concatenate([x[0, core * RT:(core + 1) * RT, :],
                            x[1, core * RT:(core + 1) * RT, :]], axis=0)),
        "Wq": kw["L0_attn_Wq"], "Wk": kw["L0_attn_Wk"], "Wv": kw["L0_attn_Wv"],
        "Ws": kw["L0_attn_Ws"], "Wt": kw["L0_attn_Wt"],
        "W1": kw["L0_attn_W1"], "W2": kw["L0_attn_W2"],
        "w3t": np.ascontiguousarray(np.asarray(kw["L0_ffn_w3"]).T),
        "b1c": np.ascontiguousarray(
            np.asarray(kw["L0_attn_b1"]).reshape(64, 128).T),
        "b2c": np.ascontiguousarray(
            np.asarray(kw["L0_attn_b2"]).reshape(32, 128).T),
        "b3c": np.ascontiguousarray(
            np.asarray(kw["L0_ffn_b3"]).reshape(16, 128).T),
        "fw1t": np.ascontiguousarray(
            np.asarray(kw["L0_ffn_w1"]).transpose(0, 2, 1)),
        "fw2t": np.ascontiguousarray(
            np.asarray(kw["L0_ffn_w2"]).transpose(0, 2, 1)),
        "fb1c": np.ascontiguousarray(np.asarray(kw["L0_ffn_b1"]).T),
        "fb2": np.asarray(kw["L0_ffn_b2"]),
        "swc": np.ascontiguousarray(
            np.asarray(kw["L0_ffn_switch_w"]).reshape(NE, 2048, D)
            [:, core * RT:(core + 1) * RT, :]),
        "swb_bc": np.ascontiguousarray(
            np.tile(np.asarray(kw["L0_ffn_switch_b"])[None, :], (2, 1))),
        "mask_bc": np.ascontiguousarray(np.tile(
            np.array([[1.0 if (j // 4 == b and j % 4 < q) else 0.0
                       for j in range(NCORE)]], np.float32), (128, 1))),
        "bselc": np.array([[1.0 if b == 0 else 0.0],
                           [1.0 if b == 1 else 0.0]], np.float32),
        "Fr": Frv, "Fi": Fiv, "identc": ident,
        "RevA": RevAv, "RevAn": RevAnv, "Pick0": Pick0v, "Pick0n": Pick0nv,
        "onesr": onesrv,
        "f1024": f1024v,
    }
    for k in list(d):
        a = np.asarray(d[k])
        if a.dtype == np.float64:
            a = a.astype(np.float32)
        d[k] = np.ascontiguousarray(a)
    return d


_NC_CACHE = {}


def kernel(**inputs):
    key = ()
    if key not in _NC_CACHE:
        _NC_CACHE[key] = build_program(())
    nc, stack, _ = _NC_CACHE[key]
    in_maps = [make_inputs(inputs, c) for c in range(NCORE)]
    res = run_bass_kernel_spmd(nc, in_maps, core_ids=list(range(NCORE)))
    out = np.empty((B, S, D), dtype=np.float32)
    for c in range(NCORE):
        out[c // 4, (c % 4) * T:(c % 4 + 1) * T, :] = res.results[c]["y"]
    return out


# version 09
# speedup vs baseline: 1.2892x; 1.2868x over previous
from contextlib import ExitStack

import numpy as np
import ml_dtypes

import concourse.bass as bass
import concourse.mybir as mybir
from concourse.bass_utils import run_bass_kernel_spmd

F32 = mybir.dt.float32
BF16 = mybir.dt.bfloat16
AF = mybir.ActivationFunctionType
ALU = mybir.AluOpType
AX = mybir.AxisListType

ENGINES = ("tensor", "vector", "scalar", "gpsimd", "sync")

B, S, D = 2, 4096, 2048
NCORE = 8
T = 1024
TH = 512
DT = 16
NF = 8
NE = 8
NHEAD = 16
RT = 256
NINV = 1.0 / D


class Sem:
    __slots__ = ("h", "count", "name")

    def __init__(self, h, name):
        self.h = h
        self.name = name
        self.count = 0


class Prog:
    def __init__(self, nc, stack):
        self.nc = nc
        self.stack = stack
        self.ops = {e: [] for e in ENGINES}
        self._waited = {e: {} for e in ENGINES}

    def sem(self, name):
        return Sem(self.stack.enter_context(self.nc.semaphore(name)), name)

    def tok(self, sem):
        return (sem, sem.count)

    auto_self = None

    def emit(self, eng, fn, waits=(), inc=None, inc_n=1):
        if self.auto_self and eng in self.auto_self:
            waits = list(waits) + [self.tok(self.auto_self[eng])]
        waited = self._waited[eng]
        rw = []
        for s, v in waits:
            if v > 0 and waited.get(s.name, 0) < v:
                rw.append((s, v))
                waited[s.name] = v
        if inc is not None:
            inc.count += inc_n

        def run(e, _fn=fn, _w=rw, _i=inc, _n=inc_n):
            for s, v in _w:
                e.wait_ge(s.h, v)
            inst = _fn(e)
            if _i is not None:
                if inst is None:
                    e.sem_inc(_i.h, _n)
                else:
                    inst.then_inc(_i.h, _n)

        self.ops[eng].append(run)
        return None if inc is None else (inc, inc.count)

    def dma(self, eng, out, in_, sem, waits=(), **kw):
        return self.emit(eng, lambda e: e.dma_start(out=out, in_=in_, **kw),
                         waits=waits, inc=sem, inc_n=16)

    def finalize(self):
        with self.nc.Block() as block:
            for eng_name in ENGINES:
                ops = self.ops[eng_name]

                def body(e, _ops=ops):
                    for f in _ops:
                        f(e)

                getattr(block, eng_name)(body)


def build_program(taps=(), stop=None, detect=False):
    nc = bass.Bass("TRN2", target_bir_lowering=False, debug=False,
                   detect_race_conditions=detect)
    stack = ExitStack()
    p = Prog(nc, stack)
    taps = set(taps)

    def din(name, shape, dt=F32):
        return nc.dram_tensor(name, shape, dt, kind="ExternalInput").ap()

    x_my = din("x_my", [T, D])
    x_rt = din("x_rt", [2 * RT, D])
    Wq = din("Wq", [16, 128, 16, 128]); Wk = din("Wk", [16, 128, 16, 128])
    Wv = din("Wv", [16, 128, 16, 128]); Ws = din("Ws", [16, 128, 16, 128])
    Wt = din("Wt", [16, 128, 16, 128])
    W1 = din("W1", [64, 128, 32, 128]); W2 = din("W2", [32, 2, 128, 32, 128])
    w3t = din("w3t", [16, 128, 16, 128])
    b1c = din("b1c", [128, 64]); b2c = din("b2c", [128, 32]); b3c = din("b3c", [128, 16])
    fw1t = din("fw1t", [NE, 128, 128]); fw2t = din("fw2t", [NE, 128, 128])
    fb1c = din("fb1c", [128, NE]); fb2 = din("fb2", [NE, 128])
    swc = din("swc", [NE, RT, D])
    swb_bc = din("swb_bc", [2, NE])
    mask_bc = din("mask_bc", [128, NCORE])
    bselc = din("bselc", [2, 1])
    Fr = din("Fr", [16, 128, 16, 128], BF16)
    Fi = din("Fi", [16, 128, 16, 128], BF16)
    identc = din("identc", [128, 128], BF16)
    RevA = din("RevA", [128, 128], BF16); RevAn = din("RevAn", [128, 128], BF16)
    Pick0 = din("Pick0", [128, 128], BF16); Pick0n = din("Pick0n", [128, 128], BF16)
    onesr = din("onesr", [1, 128], BF16)
    f1024 = din("f1024", [128, 16], BF16)

    y = nc.dram_tensor("y", [T, D], F32, kind="ExternalOutput").ap()

    cc_in = nc.dram_tensor("cc_in", [1, 2320], F32)
    cc_out = nc.dram_tensor("cc_out", [NCORE, 2320], F32)
    ta_sp = nc.dram_tensor("ta_sp", [17 * 128, T], BF16).ap()
    zs_sp = nc.dram_tensor("zs_sp", [17 * 128, T], BF16).ap()
    t1_sp = nc.dram_tensor("t1_sp", [D, T], BF16).ap()
    ff_sp = nc.dram_tensor("ff_sp", [D, T], BF16).ap()

    tap_out = {}

    def tap(name, shape, dt=F32):
        if name in taps:
            tap_out[name] = nc.dram_tensor("tap_" + name, shape, dt,
                                           kind="ExternalOutput").ap()
            return True
        return False

    gq = p.sem("gq"); sq = p.sem("sq"); mm = p.sem("mm"); dv = p.sem("dv")
    sc = p.sem("sc"); cc = p.sem("cc"); st = p.sem("st")
    c1 = p.sem("c1"); c2 = p.sem("c2"); lq = p.sem("lq")
    wp = [p.sem(f"wp{i}") for i in range(4)]
    w1s = [p.sem(f"w1s{i}") for i in range(4)]
    w2s = [p.sem(f"w2s{i}") for i in range(4)]
    w3s = [p.sem(f"w3s{i}") for i in range(4)]
    fbs = [p.sem(f"fbs{i}") for i in range(2)]
    fjs = [p.sem(f"fjs{i}") for i in range(2)]
    fis = [p.sem(f"fis{i}") for i in range(2)]
    sws = [p.sem(f"sws{i}") for i in range(2)]
    ci_s = p.sem("ci_s"); gl_s = p.sem("gl_s"); sp_s = p.sem("sp_s")
    ta_s = p.sem("ta_s"); zs_s = p.sem("zs_s"); t1_s = p.sem("t1_s")
    ff_s = p.sem("ff_s"); xr_s = p.sem("xr_s")
    p.auto_self = {"vector": dv, "scalar": sc}

    def _stop(name):
        if stop == name:
            p.emit("sync", lambda e: None, waits=[p.tok(st), p.tok(dv),
                                                  p.tok(sc), p.tok(gq),
                                                  p.tok(sq)])
            p.finalize()
            return True
        return False


    R1 = stack.enter_context(nc.sbuf_tensor("R1", [128, 16384], BF16))
    R2 = stack.enter_context(nc.sbuf_tensor("R2", [128, 16384], BF16))
    R3 = stack.enter_context(nc.sbuf_tensor("R3", [128, 16384], BF16))
    R4 = stack.enter_context(nc.sbuf_tensor("R4", [128, 16384], BF16))
    R5 = stack.enter_context(nc.sbuf_tensor("R5", [128, 8192], BF16))
    RE = stack.enter_context(nc.sbuf_tensor("RE", [128, 17920], BF16))
    FA = stack.enter_context(nc.sbuf_tensor("FA", [128, 4608], F32))

    def vw(arena, off, n):
        return arena.ap()[:, off:off + n]

    sbt = lambda nm, sh, dt: stack.enter_context(nc.sbuf_tensor(nm, sh, dt))
    b1cb = sbt("b1cb", [128, 64], F32)
    b2cb = sbt("b2cb", [128, 32], F32)
    b3cb = sbt("b3cb", [128, 16], F32)
    fw1b = sbt("fw1b", [128, NE * 128], BF16)
    fw2b = sbt("fw2b", [128, NE * 128], BF16)
    w2gb = sbt("w2gb", [128, NE * 128], BF16)
    fb1b = sbt("fb1b", [128, NE], F32)
    fb2b = sbt("fb2b", [NE, 128], BF16)
    maskb = sbt("maskb", [128, NCORE], F32)
    bselb = sbt("bselb", [2, 1], BF16)
    identb = sbt("identb", [128, 128], BF16)
    RevAb = sbt("RevAb", [128, 128], BF16)
    RevAnb = sbt("RevAnb", [128, 128], BF16)
    Pick0b = sbt("Pick0b", [128, 128], BF16)
    Pick0nb = sbt("Pick0nb", [128, 128], BF16)
    onerb = sbt("onerb", [1, 128], BF16)
    F1B = sbt("F1B", [128, 16], BF16)
    swbb = sbt("swbb", [2, NE], F32)
    COLL = sbt("COLL", [128, 18], F32)
    CARR = sbt("CARR", [128, 2], F32)
    OFFC = sbt("OFFC", [128, 18], F32)
    OFFG = sbt("OFFG", [128, 18 * NCORE], F32)
    ST1 = sbt("ST1", [128, 32], F32)
    ST2B = sbt("ST2B", [128, 16], BF16)
    ONE1 = sbt("ONE1", [128, 1], BF16)
    RLOG = sbt("RLOG", [16, 1], F32)
    LG = sbt("LG", [2, 8 * NCORE], F32)
    GT = sbt("GT", [2, NE], F32)
    GTt1 = sbt("GTt1", [2, NE], F32)
    GTt2 = sbt("GTt2", [2, NE], F32)
    GTB = sbt("GTB", [2, NE], BF16)
    GRED = sbt("GRED", [2, 1], F32)
    GRED2 = sbt("GRED2", [128, 8], F32)
    EPSC = sbt("EPSC", [128, 2], F32)
    MGC = sbt("MGC", [NE, 1], BF16)
    MGR = sbt("MGR", [1, NE], BF16)
    GBC = sbt("GBC", [128, NE], F32)
    B2G = sbt("B2G", [128, 1], F32)

    LNS = FA.ap()[:, 0:2048]
    LNT = FA.ap()[:, 2048:4096]
    SLOT = [FA.ap()[:, i * 512:(i + 1) * 512] for i in range(8)]
    TSG = FA.ap()[:, 2048:3072]
    ZMR = FA.ap()[:, 3072:4096]
    SCAN = FA.ap()[:, 4096:4608]
    OUTB = FA.ap()[:, 0:2048]
    XRL = [FA.ap()[:, 2048:4096]]

    ps = [stack.enter_context(nc.psum_tensor(f"ps{i}", [128, TH], F32))
          for i in range(6)]
    pt = stack.enter_context(nc.psum_tensor("pt", [128, TH], BF16))
    px = stack.enter_context(nc.psum_tensor("px", [128, TH], F32))

    xn = [vw(R1, i * 1024, 1024) for i in range(DT)]
    qv = [vw(R2, i * 1024, 1024) for i in range(DT)]
    kv = [vw(R3, i * 1024, 1024) for i in range(DT)]
    vv = [vw(R4, i * 1024, 1024) for i in range(DT)]
    XH = {}
    for i in range(9):
        XH[(0, i)] = vw(RE, i * 1024, 1024)
    for i in range(8):
        XH[(1, i)] = vw(RE, (9 + i) * 1024, 1024)
    TA = {}
    for i in range(9):
        TA[(0, i)] = vw(R1, i * 1024, 1024)
    for i in range(7):
        TA[(1, i)] = vw(R1, (9 + i) * 1024, 1024)
    TA[(1, 7)] = vw(R3, 0, 1024)
    XNTM = [vw(RE, 0, 2048), vw(RE, 2048, 2048)]
    RPRD = vw(RE, 0, 2048)
    XRT = [vw(RE, 4096 + i * 2048, 2048) for i in range(4)]
    SWB = [vw(RE, 12288 + i * 2048, 2048) for i in range(2)]
    HB = [vw(R1, i * 512, 512) for i in range(32)] + \
         [vw(R3, i * 512, 512) for i in range(32)]
    MIR = [vw(RE, i * 512, 512) for i in range(16)]
    TAH = {}
    _ti = 0
    for _pl in range(2):
        for _i in range(9 if _pl == 0 else 8):
            TAH[(_pl, _i)] = vw(R2, _ti * 512, 512)
            _ti += 1
    ZSH = {}
    _ti = 0
    for _pl in range(2):
        for _i in range(9 if _pl == 0 else 8):
            ZSH[(_pl, _i)] = vw(RE, _ti * 512, 512)
            _ti += 1
    T1H = [vw(R2, i * 512, 512) for i in range(16)]
    ZB = [vw(R2, 8192 + i * 512, 512) for i in range(16)] + \
         [vw(RE, 8704 + i * 512, 512) for i in range(16)]
    AOS = [vw(R2, i * 512, 512) for i in range(16)]
    FFH = [vw(R4, i * 512, 512) for i in range(16)]
    W1B = [vw(R4, i * 4096, 4096) for i in range(4)]
    W2B = [vw(R4, i * 4096, 4096) for i in range(4)]
    WBpr = [vw(R5, i * 2048, 2048) for i in range(4)]
    WB3 = [vw(R5, i * 2048, 2048) for i in range(4)]
    HSTs = [vw(R5, 4096 + i * 512, 512) for i in range(2)]
    FB = [vw(R5, i * 4096, 4096) for i in range(2)]
    FBj = [vw(R5, i * 4096, 4096) for i in range(2)]
    FBI = [vw(R5, i * 4096, 4096) for i in range(2)]

    for dst, src in [(b1cb, b1c), (b2cb, b2c), (b3cb, b3c), (fb1b, fb1c),
                     (maskb, mask_bc), (swbb, swb_bc)]:
        p.dma("sync", dst.ap()[:, :], src[:, :], c1)
    for dst, src in [(identb, identc), (RevAb, RevA), (RevAnb, RevAn),
                     (Pick0b, Pick0), (Pick0nb, Pick0n), (onerb, onesr)]:
        p.dma("sync", dst.ap()[:, :], src[:, :], c1)
    p.dma("gpsimd", fw1b.ap().rearrange("p (e c) -> p e c", c=128),
          fw1t.rearrange("e p c -> p e c"), c2)
    p.dma("gpsimd", fw2b.ap().rearrange("p (e c) -> p e c", c=128),
          fw2t.rearrange("e p c -> p e c"), c2)
    p.dma("gpsimd", fb2b.ap()[:, :], fb2[:, :], c2)
    p.dma("gpsimd", bselb.ap()[:, :], bselc[:, :], c2)
    p.dma("sync", F1B.ap()[:, :], f1024[:, :], c1)
    t_cs = p.tok(c1)
    t_cg = p.tok(c2)
    p.emit("vector", lambda e: e.memset(COLL.ap()[:, :], 0.0), inc=dv)
    p.emit("vector", lambda e: e.memset(ONE1.ap()[:, :], 1.0), inc=dv)
    p.emit("vector", lambda e: e.memset(EPSC.ap()[:, 0:1], 1e-5), inc=dv)
    p.emit("vector", lambda e: e.memset(EPSC.ap()[:, 1:2], NINV), inc=dv)
    t_zero = p.tok(dv)

    ln_done = []
    tp_evac = []
    ln_idx = [0]

    def emit_ln_tile(src_rows, dst, dst_waits=()):
        par = (ln_idx[0] % 2) * 4
        ln_idx[0] += 1
        g2 = GRED2.ap()
        w = [t_zero, t_cs]
        if ln_done:
            w.append((dv, ln_done[-1]))
        p.dma("sync", LNS, src_rows, lq, waits=w)
        t_l = p.tok(lq)
        p.emit("vector", lambda e: e.reduce_sum(g2[:, par:par + 1], LNS,
                                                axis=AX.X), waits=[t_l], inc=dv)
        p.emit("vector", lambda e: e.tensor_scalar(
            g2[:, par + 1:par + 2], g2[:, par:par + 1], -NINV, None, ALU.mult),
            waits=[p.tok(dv)], inc=dv)
        t_m = p.tok(dv)
        p.emit("scalar", lambda e: e.activation(
            LNT, LNS, AF.Square, bias=g2[:, par + 1:par + 2],
            accum_out=g2[:, par + 2:par + 3]), waits=[t_m], inc=sc)
        t_s = p.tok(sc)
        p.emit("scalar", lambda e: e.activation(
            g2[:, par + 3:par + 4], g2[:, par + 2:par + 3], AF.Sqrt,
            bias=EPSC.ap()[:, 0:1], scale=EPSC.ap()[:, 1:2]),
            waits=[t_s], inc=sc)
        t_sq = p.tok(sc)
        p.emit("vector", lambda e: e.reciprocal(
            g2[:, par + 3:par + 4], g2[:, par + 3:par + 4]),
            waits=[t_sq], inc=dv)
        t_r = p.tok(dv)
        p.emit("vector", lambda e, d=dst: e.tensor_scalar(
            d, LNS, g2[:, par + 1:par + 2], g2[:, par + 3:par + 4],
            ALU.add, ALU.mult), waits=[t_r] + list(dst_waits), inc=dv)
        ln_done.append(dv.count)
        return p.tok(dv)

    tp_mm = []
    for i in range(8):
        dstbuf = XNTM[i % 2]
        dw = []
        if i >= 2:
            dw.append((mm, tp_mm[4 * (i - 2) + 3]))
        t_n = emit_ln_tile(x_my[i * 128:(i + 1) * 128, :], dstbuf, dst_waits=dw)
        for g in range(4):
            w = [t_n, t_cs]
            if tp_evac:
                w.append((dv, tp_evac[-1]))
            for j in range(4):
                dt_ = g * 4 + j
                p.emit("tensor", lambda e, s2=dstbuf, jj=j, d2=dt_:
                       e.transpose(pt[:, jj * 128:(jj + 1) * 128],
                                   s2[:, d2 * 128:(d2 + 1) * 128],
                                   identb.ap()[:, :]),
                       waits=w if j == 0 else (), inc=mm if j == 3 else None)
            t_t = p.tok(mm)
            for j in range(4):
                dt_ = g * 4 + j
                p.emit("vector", lambda e, jj=j, d2=dt_, i2=i: e.tensor_copy(
                    xn[d2][:, i2 * 128:(i2 + 1) * 128],
                    pt[:, jj * 128:(jj + 1) * 128]),
                    waits=[t_t] if j == 0 else (), inc=dv if j == 3 else None)
            tp_evac.append(dv.count)
            tp_mm.append(mm.count)
    t_xn = p.tok(dv)
    if tap("xn", [DT * 128, T], BF16):
        for i in range(DT):
            p.dma("sync", tap_out["xn"][i * 128:(i + 1) * 128, :], xn[i], st,
                  waits=[t_xn])

    if _stop("p1"):
        return nc, stack, sorted(taps & set(tap_out))
    for i in range(4):
        emit_ln_tile(x_rt[i * 128:(i + 1) * 128, :], XRT[i])
    t_rt = p.tok(dv)

    r_done = []
    for e_i in range(NE):
        for st_i in range(2):
            sl = (e_i * 2 + st_i) % 2
            buf = SWB[sl]
            w = [t_cg, t_rt]
            if len(r_done) >= 2:
                w.append((dv, r_done[-2]))
            p.dma("gpsimd", buf, swc[e_i, st_i * 128:(st_i + 1) * 128, :],
                  sws[sl], waits=w)
            t_w = p.tok(sws[sl])
            for b_i in range(2):
                col = b_i * 16 + e_i * 2 + st_i
                p.emit("vector", lambda e, bu=buf, b2=b_i, s2=st_i:
                       e.tensor_mul(RPRD, bu, XRT[b2 * 2 + s2]),
                       waits=[t_w], inc=dv)
                p.emit("vector", lambda e, c2=col: e.reduce_sum(
                    ST1.ap()[:, c2:c2 + 1], RPRD, axis=AX.X),
                    waits=[p.tok(dv)], inc=dv)
            r_done.append(dv.count)
    for b_i in range(2):
        for e_i in range(NE):
            c0 = b_i * 16 + e_i * 2
            o = b_i * 8 + e_i
            p.emit("vector", lambda e, c2=c0, o2=o: e.tensor_add(
                ST2B.ap()[:, o2:o2 + 1], ST1.ap()[:, c2:c2 + 1],
                ST1.ap()[:, c2 + 1:c2 + 2]), waits=[p.tok(dv)], inc=dv)
    t_st2 = p.tok(dv)
    p.emit("tensor", lambda e: e.matmul(px[0:16, 0:1], ST2B.ap()[:, :],
                                        ONE1.ap()[:, :], start=True, stop=True),
           waits=[t_st2], inc=mm)
    t_rl = p.tok(mm)
    p.emit("vector", lambda e: e.tensor_copy(RLOG.ap()[:, :], px[0:16, 0:1]),
           waits=[t_rl], inc=dv)
    t_rlog = p.tok(dv)

    if _stop("p2"):
        return nc, stack, sorted(taps & set(tap_out))
    prst = {"blocks": 0, "block_mm": [], "evac": [], "bank": {}}

    def emit_proj(Wap, dst, rhs, first_waits=()):
        for mg in range(16):
            sl = mg % 4
            buf = WBpr[sl]
            w = list(first_waits) + [t_cg]
            if prst["blocks"] >= 4:
                w.append((mm, prst["block_mm"][-4]))
            p.dma("gpsimd", buf.rearrange("p (kc c) -> p kc c", c=128),
                  Wap[mg], wp[sl], waits=w)
            t_w = p.tok(wp[sl])
            mtile = mg
            for half in range(2):
                bidx = (mtile * 2 + half) % 6
                bank = ps[bidx]
                mmw = [t_w]
                if bidx in prst["bank"]:
                    mmw.append((dv, prst["bank"][bidx]))
                for kc in range(16):
                    lhsT = buf[:, kc * 128:(kc + 1) * 128]
                    p.emit("tensor", lambda e, o=bank, l=lhsT, k2=kc, h=half:
                           e.matmul(o[:, :], l,
                                    rhs[k2][:, h * TH:(h + 1) * TH],
                                    start=(k2 == 0), stop=(k2 == 15)),
                           waits=mmw if kc == 0 else (),
                           inc=mm if kc == 15 else None)
                t_mmv = p.tok(mm)
                p.emit("vector", lambda e, d=dst[mtile], h=half, b2=bank:
                       e.tensor_copy(d[:, h * TH:(h + 1) * TH], b2[:, :]),
                       waits=[t_mmv], inc=dv)
                prst["evac"].append(dv.count)
                prst["bank"][bidx] = dv.count
            prst["block_mm"].append(mm.count)
            prst["blocks"] += 1
        return p.tok(dv)

    t_q = emit_proj(Wq, qv, xn, first_waits=[t_xn])
    if tap("qf", [D, T], BF16):
        for i in range(DT):
            p.dma("sync", tap_out["qf"][i * 128:(i + 1) * 128, :], qv[i], st,
                  waits=[t_q])
    t_k = emit_proj(Wk, kv, xn)
    t_v = emit_proj(Wv, vv, xn)

    if _stop("p3"):
        return nc, stack, sorted(taps & set(tap_out))
    def dma_fcols(fbuf, col0, n, wl, sem):
        if n == 1:
            return
        mb = col0 // 128
        p.dma("sync", fbuf[:, 0:2048].rearrange("p (kc c) -> p kc c", c=128),
              Fr[mb], sem, waits=wl)
        p.dma("sync", fbuf[:, 2048:4096].rearrange("p (kc c) -> p kc c", c=128),
              Fi[mb], sem)

    p.emit("vector", lambda e: e.memset(XH[(0, NF)][:, :], 0.0), inc=dv)
    ps_free = []
    fblk_mm = []
    for m in range(NF + 1):
        m1 = (m == NF)
        fbuf = FB[m % 2]
        w = [t_cs]
        if m >= 2:
            w.append((mm, fblk_mm[-2]))
        if m < 2:
            w += [t_v]
        dma_fcols(fbuf, m * 128 if not m1 else 1024, 1 if m1 else 128, w,
                  fbs[m % 2])
        t_f = p.tok(fbs[m % 2])
        pr = slice(0, 1) if m1 else slice(0, 128)
        nplanes = 1 if m1 else 2
        for half in range(2):
            hs = slice(half * TH, (half + 1) * TH)
            first = True
            for ri, rhs in ((0, qv), (1, kv), (2, vv)):
                for pl in range(nplanes):
                    bank = ps[ri * 2 + pl]
                    mmw = [t_f, t_k, t_v]
                    if ps_free:
                        mmw.append((dv, ps_free[-1]))
                    for kc in range(16):
                        base = pl * 2048 + kc * 128
                        l = F1B.ap()[:, kc:kc + 1] if m1 else fbuf[:, base:base + 128]
                        p.emit("tensor", lambda e, o=bank, l2=l, k2=kc, r=rhs,
                               h2=hs, pr2=pr:
                               e.matmul(o[pr2, :], l2, r[k2][:, h2],
                                        start=(k2 == 0), stop=(k2 == 15)),
                               waits=mmw if (kc == 0 and first) else (),
                               inc=mm if kc == 15 else None)
                        first = False
            t_qkv = p.tok(mm)
            banks_used = [ri * 2 + pl for ri in range(3) for pl in range(nplanes)]
            for n_i, i in enumerate(banks_used):
                p.emit("vector", lambda e, i2=i, pr2=pr: e.tensor_copy(
                    SLOT[i2][pr2, :], ps[i2][pr2, :]),
                    waits=[t_qkv] if n_i == 0 else (),
                    inc=dv if n_i == len(banks_used) - 1 else None)
            t_ev = p.tok(dv)
            if m == 0 and half == 0 and tap("fqs", [2 * 128, TH]):
                p.dma("sync", tap_out["fqs"][0:128, :], SLOT[0], st,
                      waits=[t_ev])
                p.dma("sync", tap_out["fqs"][128:256, :], SLOT[1], st)
            for pl in range(nplanes):
                bank = ps[pl]
                for kc in range(16):
                    base = pl * 2048 + kc * 128
                    l = F1B.ap()[:, kc:kc + 1] if m1 else fbuf[:, base:base + 128]
                    p.emit("tensor", lambda e, o=bank, l2=l, k2=kc, h2=hs, pr2=pr:
                           e.matmul(o[pr2, :], l2, xn[k2][:, h2],
                                    start=(k2 == 0), stop=(k2 == 15)),
                           waits=[t_ev] if (kc == 0 and pl == 0) else (),
                           inc=mm if kc == 15 else None)
            t_xf = p.tok(mm)

            sQr, sQi, sKr, sKi, sVr, sVi, sT1, sT2 = SLOT

            def vop(fn, waits=(), inc=dv):
                p.emit("vector", fn, waits=list(waits) + [p.tok(dv)], inc=inc)

            if not m1:
                vop(lambda e, pr=pr: e.tensor_mul(sT1[pr, :], sQr[pr, :], sKr[pr, :]))
                vop(lambda e, pr=pr: e.tensor_mul(sT2[pr, :], sQi[pr, :], sKi[pr, :]))
                vop(lambda e, pr=pr: e.tensor_add(sT1[pr, :], sT1[pr, :], sT2[pr, :]))
                vop(lambda e, pr=pr: e.tensor_mul(sT2[pr, :], sQr[pr, :], sKi[pr, :]))
                vop(lambda e, pr=pr: e.tensor_mul(sKr[pr, :], sQi[pr, :], sKr[pr, :]))
                vop(lambda e, pr=pr: e.tensor_sub(sT2[pr, :], sT2[pr, :], sKr[pr, :]))
                vop(lambda e, pr=pr: e.tensor_mul(sQr[pr, :], sT1[pr, :], sVr[pr, :]))
                vop(lambda e, pr=pr: e.tensor_mul(sKr[pr, :], sT2[pr, :], sVi[pr, :]))
                vop(lambda e, pr=pr: e.tensor_mul(sQi[pr, :], sT1[pr, :], sVi[pr, :]))
                vop(lambda e, pr=pr: e.tensor_mul(sKi[pr, :], sT2[pr, :], sVr[pr, :]))
                vop(lambda e, pr=pr: e.tensor_sub(sQr[pr, :], sQr[pr, :], sKr[pr, :]))
                vop(lambda e, pr=pr: e.tensor_add(sQi[pr, :], sQi[pr, :], sKi[pr, :]))
            else:
                vop(lambda e, pr=pr: e.tensor_mul(sQr[pr, :], sQr[pr, :], sKr[pr, :]))
                vop(lambda e, pr=pr: e.tensor_mul(sQr[pr, :], sQr[pr, :], sVr[pr, :]))
            vop(lambda e, pr=pr: e.tensor_copy(sKr[pr, :], ps[0][pr, :]), waits=[t_xf])
            if not m1:
                vop(lambda e, pr=pr: e.tensor_copy(sKi[pr, :], ps[1][pr, :]))
            ps_free.append(dv.count)
            if m == 0 and half == 0 and tap("attn0", [2 * 128, TH]):
                p.dma("sync", tap_out["attn0"][0:128, :], SLOT[0], st,
                      waits=[p.tok(dv)])
                p.dma("sync", tap_out["attn0"][128:256, :], SLOT[1], st)
            for pl in range(nplanes):
                asl = (sQr, sQi)[pl]
                xsl = (sKr, sKi)[pl]
                vop(lambda e, a=asl, pr=pr: e.tensor_tensor_scan(
                    SCAN[pr, :], a[pr, :], a[pr, :], 0.0, ALU.add, ALU.bypass))
                if half == 0:
                    vop(lambda e, pl2=pl, pr=pr: e.tensor_copy(
                        CARR.ap()[pr, pl2:pl2 + 1], SCAN[pr, TH - 1:TH]),
                        waits=[p.tok(dv)])
                    vop(lambda e, pl2=pl, m2=m, x2=xsl, pr=pr: e.tensor_add(
                        XH[(pl2, m2)][pr, 0:TH], SCAN[pr, :], x2[pr, :]))
                else:
                    vop(lambda e, pl2=pl, pr=pr: e.tensor_scalar(
                        SCAN[pr, :], SCAN[pr, :], CARR.ap()[pr, pl2:pl2 + 1],
                        None, ALU.add), waits=[p.tok(dv)])
                    vop(lambda e, pl2=pl, m2=m, x2=xsl, pr=pr: e.tensor_add(
                        XH[(pl2, m2)][pr, TH:T], SCAN[pr, :], x2[pr, :]))
                    vop(lambda e, pl2=pl, m2=m, pr=pr: e.tensor_copy(
                        COLL.ap()[pr, m2 * 2 + pl2:m2 * 2 + pl2 + 1],
                        SCAN[pr, TH - 1:TH]), waits=[p.tok(dv)])
        fblk_mm.append(mm.count)
    t_p4 = p.tok(dv)
    if tap("xh", [17 * 128, T], BF16):
        _ti = 0
        for pl in range(2):
            for i in range(9 if pl == 0 else 8):
                p.dma("sync", tap_out["xh"][_ti * 128:(_ti + 1) * 128, :],
                      XH[(pl, i)], st, waits=[t_p4])
                _ti += 1
    if tap("coll", [128, 18]):
        p.dma("sync", tap_out["coll"][:, :], COLL.ap()[:, :], st, waits=[t_p4])

    if _stop("p4"):
        return nc, stack, sorted(taps & set(tap_out))
    p.dma("gpsimd",
          cc_in.ap()[0:1, 0:2304].rearrange("one (p c) -> (one p) c", p=128),
          COLL.ap()[:, :], ci_s, waits=[t_p4])
    p.dma("gpsimd",
          cc_in.ap()[0:1, 2304:2320].rearrange("one (p c) -> (one p) c", p=16),
          RLOG.ap()[:, :], ci_s, waits=[t_rlog])
    t_ci = p.tok(ci_s)
    p.emit("gpsimd", lambda e: e.collective_compute(
        "AllGather", ALU.bypass, replica_groups=[list(range(NCORE))],
        ins=[cc_in.ap()], outs=[cc_out.ap()]), waits=[t_ci], inc=cc)
    t_cc = p.tok(cc)

    t_s1 = emit_proj(Ws, qv, xn, first_waits=[t_p4])
    t_t1 = emit_proj(Wt, kv, xn)
    for i in range(DT):
        p.dma("sync", t1_sp[i * 128:(i + 1) * 128, :], kv[i], sp_s,
              waits=[t_t1] if i == 0 else ())
    t_t1sp = p.tok(sp_s)

    if _stop("p5"):
        return nc, stack, sorted(taps & set(tap_out))
    p.dma("sync", OFFG.ap().rearrange("p (c k) -> p c k", k=18),
          cc_out.ap()[:, 0:2304].rearrange("c (p k) -> p c k", p=128),
          gl_s, waits=[t_cc])
    p.dma("sync", LG.ap().rearrange("p (c e2) -> p c e2", e2=8),
          cc_out.ap()[:, 2304:2320].rearrange("c (b e2) -> b c e2", b=2),
          gl_s, waits=[t_cc])
    t_gl = p.tok(gl_s)
    p.emit("vector", lambda e: e.memset(OFFC.ap()[:, :], 0.0), inc=dv)
    for c_i in range(NCORE):
        p.emit("vector", lambda e, c2=c_i: e.scalar_tensor_tensor(
            OFFC.ap()[:, :], OFFG.ap()[:, c2 * 18:(c2 + 1) * 18],
            maskb.ap()[:, c2:c2 + 1], OFFC.ap()[:, :], ALU.mult, ALU.add),
            waits=[t_gl, p.tok(dv)], inc=dv)
    t_off = p.tok(dv)

    p.emit("vector", lambda e: e.tensor_add(
        GT.ap()[:, :], LG.ap()[:, 0:8], LG.ap()[:, 8:16]), waits=[t_gl], inc=dv)
    for c_i in range(2, NCORE):
        p.emit("vector", lambda e, c2=c_i: e.tensor_add(
            GT.ap()[:, :], GT.ap()[:, :], LG.ap()[:, c2 * 8:(c2 + 1) * 8]),
            waits=[p.tok(dv)], inc=dv)
    p.emit("vector", lambda e: e.tensor_add(GT.ap()[:, :], GT.ap()[:, :],
                                            swbb.ap()[:, :]),
           waits=[p.tok(dv)], inc=dv)
    t_lgt = p.tok(dv)
    if tap("logits", [2, 8]):
        p.dma("sync", tap_out["logits"][:, :], GT.ap()[:, :], st, waits=[t_lgt])
    p.emit("vector", lambda e: e.reduce_max(GRED.ap()[:, :], GT.ap()[:, :],
                                            axis=AX.X, negate=True),
           waits=[p.tok(dv)], inc=dv)
    t_gm = p.tok(dv)
    p.emit("scalar", lambda e: e.activation(GTt1.ap()[:, :], GT.ap()[:, :],
                                            AF.Exp, bias=GRED.ap()[:, 0:1]),
           waits=[t_gm], inc=sc)
    t_ge = p.tok(sc)
    p.emit("vector", lambda e: e.reduce_sum(GRED.ap()[:, :], GTt1.ap()[:, :],
                                            axis=AX.X), waits=[t_ge], inc=dv)
    p.emit("vector", lambda e: e.reciprocal(GRED.ap()[:, :], GRED.ap()[:, :]),
           waits=[p.tok(dv)], inc=dv)
    p.emit("vector", lambda e: e.tensor_scalar(GTt1.ap()[:, :], GTt1.ap()[:, :],
                                               GRED.ap()[:, 0:1], None,
                                               ALU.mult), waits=[p.tok(dv)], inc=dv)
    p.emit("vector", lambda e: e.reduce_max(GRED.ap()[:, :], GTt1.ap()[:, :],
                                            axis=AX.X), waits=[p.tok(dv)], inc=dv)
    p.emit("vector", lambda e: e.tensor_scalar(GTt2.ap()[:, :], GTt1.ap()[:, :],
                                               GRED.ap()[:, 0:1], None,
                                               ALU.is_lt), waits=[p.tok(dv)], inc=dv)
    p.emit("vector", lambda e: e.tensor_mul(GTt2.ap()[:, :], GTt2.ap()[:, :],
                                            GTt1.ap()[:, :]), waits=[p.tok(dv)], inc=dv)
    p.emit("vector", lambda e: e.reduce_max(GRED.ap()[:, :], GTt2.ap()[:, :],
                                            axis=AX.X), waits=[p.tok(dv)], inc=dv)
    p.emit("vector", lambda e: e.tensor_scalar(GTt2.ap()[:, :], GTt1.ap()[:, :],
                                               GRED.ap()[:, 0:1], None,
                                               ALU.is_ge), waits=[p.tok(dv)], inc=dv)
    p.emit("vector", lambda e: e.tensor_mul(GT.ap()[:, :], GTt1.ap()[:, :],
                                            GTt2.ap()[:, :]), waits=[p.tok(dv)], inc=dv)
    p.emit("vector", lambda e: e.tensor_copy(GTB.ap()[:, :], GT.ap()[:, :]),
           waits=[p.tok(dv)], inc=dv)
    t_gt = p.tok(dv)
    if tap("gates", [2, 8]):
        p.dma("sync", tap_out["gates"][:, :], GT.ap()[:, :], st, waits=[t_gt])
    p.emit("tensor", lambda e: e.matmul(px[0:NE, 0:1], GTB.ap()[:, :],
                                        bselb.ap()[:, :], start=True, stop=True),
           waits=[t_gt, t_cg], inc=mm)
    t_a = p.tok(mm)
    p.emit("vector", lambda e: e.tensor_copy(MGC.ap()[:, :], px[0:NE, 0:1]),
           waits=[t_a], inc=dv)
    t_b = p.tok(dv)
    p.emit("tensor", lambda e: e.matmul(px[0:128, 1:2], fb2b.ap()[:, :],
                                        MGC.ap()[:, :], start=True, stop=True),
           waits=[t_b], inc=mm)
    p.emit("tensor", lambda e: e.matmul(px[0:1, 2:10], bselb.ap()[:, :],
                                        GTB.ap()[:, :], start=True, stop=True),
           inc=mm)
    t_c = p.tok(mm)
    p.emit("vector", lambda e: e.tensor_copy(B2G.ap()[:, :], px[0:128, 1:2]),
           waits=[t_c], inc=dv)
    p.emit("vector", lambda e: e.tensor_copy(MGR.ap()[:, :], px[0:1, 2:10]),
           inc=dv)
    t_d = p.tok(dv)
    p.emit("tensor", lambda e: e.matmul(px[0:128, 16:24], onerb.ap()[:, :],
                                        MGR.ap()[:, :], start=True, stop=True),
           waits=[t_d], inc=mm)
    t_e2 = p.tok(mm)
    p.emit("vector", lambda e: e.tensor_copy(GBC.ap()[:, :], px[0:128, 16:24]),
           waits=[t_e2], inc=dv)
    t_gbc = p.tok(dv)
    for e_i in range(NE):
        p.emit("vector", lambda e, e3=e_i: e.tensor_scalar(
            w2gb.ap()[:, e3 * 128:(e3 + 1) * 128],
            fw2b.ap()[:, e3 * 128:(e3 + 1) * 128],
            GBC.ap()[:, e3:e3 + 1], None, ALU.mult), waits=[t_gbc], inc=dv)
    t_w2g = p.tok(dv)

    ob_guard = {}
    for band in range(NHEAD):
        for half in range(2):
            hs = slice(half * TH, (half + 1) * TH)
            obank = ps[4 + half]
            gelu_sc = []

            def mm1(e_i):
                hbank = ps[e_i % 2]
                w = [t_w2g]
                if e_i >= 2:
                    w.append((mm, mm2_mm[e_i - 2]))
                p.emit("tensor", lambda e, b2=band, e3=e_i, h2=hs, hb=hbank:
                       e.matmul(hb[:, :],
                                fw1b.ap()[:, e3 * 128:(e3 + 1) * 128],
                                xn[b2][:, h2], start=True, stop=True),
                       waits=w, inc=mm)
                mm1_mm.append(mm.count)

            mm1_mm = []
            mm2_mm = []
            mm1(0)
            mm1(1)
            for e_i in range(NE):
                hbank = ps[e_i % 2]
                hst = HSTs[e_i % 2]
                w = [(mm, mm1_mm[e_i])]
                if e_i >= 2:
                    w.append((mm, mm2_mm[e_i - 2]))
                p.emit("scalar", lambda e, e3=e_i, hb=hbank, h4=hst:
                       e.activation(h4, hb[:, :], AF.Gelu,
                                    bias=fb1b.ap()[:, e3:e3 + 1]),
                       waits=w, inc=sc)
                gelu_sc.append(sc.count)
                ow = [(sc, gelu_sc[e_i])]
                if e_i == 0 and half in ob_guard:
                    ow.append((dv, ob_guard[half]))
                p.emit("tensor", lambda e, e3=e_i, ob=obank, h4=hst:
                       e.matmul(ob[:, :],
                                w2gb.ap()[:, e3 * 128:(e3 + 1) * 128],
                                h4[:, 0:TH], start=(e3 == 0),
                                stop=(e3 == NE - 1)),
                       waits=ow, inc=mm)
                mm2_mm.append(mm.count)
                if e_i + 2 < NE:
                    mm1(e_i + 2)
            t_o = p.tok(mm)
            p.emit("vector", lambda e, ob=obank: e.tensor_scalar(
                SLOT[6][:, :], ob[:, :], B2G.ap()[:, 0:1], None, ALU.add),
                waits=[t_o, t_t1sp, t_w2g], inc=dv)
            p.emit("vector", lambda e, b2=band, h2=hs: e.tensor_add(
                kv[b2][:, h2], SLOT[6][:, :], xn[b2][:, h2]), inc=dv)
            ob_guard[half] = dv.count
    t_u = p.tok(dv)
    if tap("u", [DT * 128, T], BF16):
        for i in range(DT):
            p.dma("sync", tap_out["u"][i * 128:(i + 1) * 128, :], kv[i], st,
                  waits=[t_u])

    w3_mm = []
    w3_ev = []
    for mg in range(16):
        sl = mg % 4
        buf = WB3[sl]
        w = [t_u]
        if mg >= 4:
            w.append((mm, w3_mm[-4]))
        p.dma("gpsimd", buf.rearrange("p (kc c) -> p kc c", c=128),
              w3t[mg], w3s[sl], waits=w)
        t_w = p.tok(w3s[sl])
        for half in range(2):
            bank = ps[half]
            mmw = [t_w]
            if len(w3_ev) >= 2:
                mmw.append((dv, w3_ev[-2]))
            for kc in range(16):
                lhsT = buf[:, kc * 128:(kc + 1) * 128]
                p.emit("tensor", lambda e, o=bank, l=lhsT, k2=kc, h=half:
                       e.matmul(o[:, :], l, kv[k2][:, h * TH:(h + 1) * TH],
                                start=(k2 == 0), stop=(k2 == 15)),
                       waits=mmw if kc == 0 else (), inc=mm if kc == 15 else None)
            t_m2 = p.tok(mm)
            p.emit("vector", lambda e, h=half, m2=mg, b2=bank: e.tensor_scalar(
                vv[m2][:, h * TH:(h + 1) * TH], b2[:, :],
                b3cb.ap()[:, m2:m2 + 1], None, ALU.add), waits=[t_m2], inc=dv)
            w3_ev.append(dv.count)
        w3_mm.append(mm.count)
        t_ffv = p.tok(dv)
        p.dma("sync", ff_sp[mg * 128:(mg + 1) * 128, :], vv[mg], sp_s,
              waits=[t_ffv])
    t_ffsp = p.tok(sp_s)
    if tap("ffn", [D, T], BF16):
        for i in range(DT):
            p.dma("sync", tap_out["ffn"][i * 128:(i + 1) * 128, :], vv[i], st,
                  waits=[t_ffsp])

    if _stop("p6"):
        return nc, stack, sorted(taps & set(tap_out))
    p.emit("vector", lambda e: e.memset(TA[(0, NF)][:, :], 0.0), inc=dv)
    p7_mm = []
    p7_free = []
    for m in range(NF + 1):
        m1 = (m == NF)
        fbuf = FB[m % 2]
        w = [t_s1, t_ffsp, t_off]
        if m >= 2:
            w.append((mm, p7_mm[-2]))
        dma_fcols(fbuf, m * 128 if not m1 else 1024, 1 if m1 else 128, w,
                  fbs[m % 2])
        t_f = p.tok(fbs[m % 2])
        pr = slice(0, 1) if m1 else slice(0, 128)
        nplanes = 1 if m1 else 2
        for half in range(2):
            hs = slice(half * TH, (half + 1) * TH)
            for pl in range(nplanes):
                bank = ps[pl]
                mmw = [t_f]
                if p7_free:
                    mmw.append((dv, p7_free[-1]))
                for kc in range(16):
                    base = pl * 2048 + kc * 128
                    l = F1B.ap()[:, kc:kc + 1] if m1 else fbuf[:, base:base + 128]
                    p.emit("tensor", lambda e, o=bank, l2=l, k2=kc, h2=hs,
                           pr2=pr:
                           e.matmul(o[pr2, :], l2, qv[k2][:, h2],
                                    start=(k2 == 0), stop=(k2 == 15)),
                           waits=mmw if (kc == 0 and pl == 0) else (),
                           inc=mm if kc == 15 else None)
            t_sf = p.tok(mm)
            sQr, sQi, sKr, sKi, sVr, sVi, sT1, sT2 = SLOT

            def vop(fn, waits=(), inc=dv):
                p.emit("vector", fn, waits=list(waits) + [p.tok(dv)], inc=inc)

            vop(lambda e, pr=pr: e.tensor_copy(sT1[pr, :], ps[0][pr, :]), waits=[t_sf])
            if not m1:
                vop(lambda e, pr=pr: e.tensor_copy(sT2[pr, :], ps[1][pr, :]))
            p7_free.append(dv.count)
            vop(lambda e, m2=m, pr=pr, hs=hs: e.tensor_scalar(
                sQr[pr, :], XH[(0, m2)][pr, hs],
                OFFC.ap()[pr, m2 * 2:m2 * 2 + 1], None, ALU.add))
            if not m1:
                vop(lambda e, m2=m, pr=pr, hs=hs: e.tensor_scalar(
                    sQi[pr, :], XH[(1, m2)][pr, hs],
                    OFFC.ap()[pr, m2 * 2 + 1:m2 * 2 + 2], None, ALU.add))
                vop(lambda e, pr=pr: e.tensor_mul(sKr[pr, :], sT1[pr, :], sQr[pr, :]))
                vop(lambda e, pr=pr: e.tensor_mul(sKi[pr, :], sT2[pr, :], sQi[pr, :]))
                vop(lambda e, pr=pr: e.tensor_add(sVr[pr, :], sKr[pr, :], sKi[pr, :]))
                vop(lambda e, m2=m, h2=hs, pr=pr: e.tensor_copy(
                    TA[(0, m2)][pr, h2], sVr[pr, :]))
                vop(lambda e, pr=pr: e.tensor_mul(sKr[pr, :], sT1[pr, :], sQi[pr, :]))
                vop(lambda e, pr=pr: e.tensor_mul(sKi[pr, :], sT2[pr, :], sQr[pr, :]))
                vop(lambda e, pr=pr: e.tensor_sub(sVi[pr, :], sKr[pr, :], sKi[pr, :]))
                vop(lambda e, m2=m, h2=hs, pr=pr: e.tensor_copy(
                    TA[(1, m2)][pr, h2], sVi[pr, :]))
                vop(lambda e, pr=pr: e.tensor_mul(sKr[pr, :], sT1[pr, :], sT1[pr, :]))
                vop(lambda e, pr=pr: e.tensor_mul(sKi[pr, :], sT2[pr, :], sT2[pr, :]))
                vop(lambda e, pr=pr: e.tensor_add(sKr[pr, :], sKr[pr, :], sKi[pr, :]))
                vop(lambda e, pr=pr: e.tensor_scalar(sKr[pr, :], sKr[pr, :],
                                              -NINV, NINV, ALU.mult, ALU.add))
                vop(lambda e, m2=m, h2=hs, pr=pr: e.tensor_mul(
                    XH[(0, m2)][pr, h2], sQr[pr, :], sKr[pr, :]))
                vop(lambda e, m2=m, h2=hs, pr=pr: e.tensor_mul(
                    XH[(1, m2)][pr, h2], sQi[pr, :], sKr[pr, :]))
            else:
                vop(lambda e, pr=pr: e.tensor_mul(sVr[pr, :], sT1[pr, :], sQr[pr, :]))
                vop(lambda e, m2=m, h2=hs, pr=pr: e.tensor_copy(
                    TA[(0, m2)][pr, h2], sVr[pr, :]))
                vop(lambda e, pr=pr: e.tensor_mul(sKr[pr, :], sT1[pr, :], sT1[pr, :]))
                vop(lambda e, pr=pr: e.tensor_scalar(sKr[pr, :], sKr[pr, :],
                                              -NINV, NINV, ALU.mult, ALU.add))
                vop(lambda e, m2=m, h2=hs, pr=pr: e.tensor_mul(
                    XH[(0, m2)][pr, h2], sQr[pr, :], sKr[pr, :]))
        p7_mm.append(mm.count)
    t_p7 = p.tok(dv)
    if tap("ta", [17 * 128, T], BF16):
        _ti = 0
        for pl in range(2):
            for i in range(9 if pl == 0 else 8):
                p.dma("sync", tap_out["ta"][_ti * 128:(_ti + 1) * 128, :],
                      TA[(pl, i)], st, waits=[t_p7])
                _ti += 1
    if tap("zs", [17 * 128, T], BF16):
        _ti = 0
        for pl in range(2):
            for i in range(9 if pl == 0 else 8):
                p.dma("sync", tap_out["zs"][_ti * 128:(_ti + 1) * 128, :],
                      XH[(pl, i)], st, waits=[t_p7])
                _ti += 1
    _ti = 0
    for pl in range(2):
        for i in range(9 if pl == 0 else 8):
            p.dma("sync", ta_sp[_ti * 128:(_ti + 1) * 128, :], TA[(pl, i)],
                  sp_s, waits=[t_p7] if _ti == 0 else ())
            p.dma("sync", zs_sp[_ti * 128:(_ti + 1) * 128, :], XH[(pl, i)], sp_s)
            _ti += 1
    t_spill = p.tok(sp_s)

    if _stop("p7"):
        return nc, stack, sorted(taps & set(tap_out))
    prev_done = None
    for half in range(2):
        h0 = half * TH
        relw = [t_spill]
        if prev_done is not None:
            relw.append(prev_done)
        _ti = 0
        for pl in range(2):
            for i in range(9 if pl == 0 else 8):
                p.dma("sync", TAH[(pl, i)],
                      ta_sp[_ti * 128:(_ti + 1) * 128, h0:h0 + TH], ta_s,
                      waits=relw if _ti == 0 else ())
                _ti += 1
        t_tah = p.tok(ta_s)
        mir_ev = []
        for pl in range(2):
            for Dd in range(8, 16):
                mi = (Dd - 8) + (0 if pl == 0 else 8)
                bank = ps[mi % 4]
                mw = [t_tah]
                if len(mir_ev) >= 4:
                    mw.append((dv, mir_ev[-4]))
                two = (pl == 0) or (Dd > 8)
                RA = RevAb if pl == 0 else RevAnb
                P0 = Pick0b if pl == 0 else Pick0nb
                p.emit("tensor", lambda e, b2=bank, ra=RA, s2=(pl, 15 - Dd),
                       tw=two:
                       e.matmul(b2[:, :], ra.ap()[:, :], TAH[s2],
                                start=True, stop=not tw),
                       waits=mw, inc=None if two else mm)
                if two:
                    p.emit("tensor", lambda e, b2=bank, p0=P0, s2=(pl, 16 - Dd):
                           e.matmul(b2[:, :], p0.ap()[:, :], TAH[s2],
                                    start=False, stop=True), inc=mm)
                t_mi = p.tok(mm)
                p.emit("vector", lambda e, m2=mi, b2=bank: e.tensor_copy(
                    MIR[m2], b2[:, :]), waits=[t_mi], inc=dv)
                mir_ev.append(dv.count)
        t_mir = p.tok(dv)

        def m1rhs(j):
            if j < 8:
                return TAH[(0, j)]
            if j < 16:
                return MIR[j - 8]
            if j < 24:
                return TAH[(1, j - 16)]
            return MIR[8 + (j - 24)]

        m1_mm = []
        m1_sc = []
        if half == 0:
            m1_bank = {}
        for blk in range(64):
            sl = blk % 4
            buf = W1B[sl]
            w = [t_mir, t_tah]
            if blk >= 4:
                w.append((mm, m1_mm[-4]))
            p.dma("gpsimd", buf.rearrange("p (kc c) -> p kc c", c=128),
                  W1[blk], w1s[sl], waits=w)
            t_w = p.tok(w1s[sl])
            bank = ps[blk % 3]
            mmw = [t_w, t_mir]
            if (blk % 3) in m1_bank:
                mmw.append((sc, m1_bank[blk % 3]))
            for kc in range(32):
                l = buf[:, kc * 128:(kc + 1) * 128]
                p.emit("tensor", lambda e, o=bank, l2=l, k2=kc:
                       e.matmul(o[:, :], l2, m1rhs(k2),
                                start=(k2 == 0), stop=(k2 == 31)),
                       waits=mmw if kc == 0 else (), inc=mm if kc == 31 else None)
            m1_mm.append(mm.count)
            t_m1 = p.tok(mm)
            p.emit("scalar", lambda e, b2=bank, m2=blk: e.activation(
                HB[m2][:, :], b2[:, :], AF.Relu,
                bias=b1cb.ap()[:, m2:m2 + 1]), waits=[t_m1], inc=sc)
            m1_sc.append(sc.count)
            m1_bank[blk % 3] = sc.count
        t_h = p.tok(sc)
        if half == 0 and _stop("p8m1"):
            return nc, stack, sorted(taps & set(tap_out))
        if half == 0 and tap("h0", [128, TH], BF16):
            p.dma("sync", tap_out["h0"][:, :], HB[0][:, :], st, waits=[t_h])

        _ti = 0
        for pl in range(2):
            for i in range(9 if pl == 0 else 8):
                p.dma("sync", ZSH[(pl, i)],
                      zs_sp[_ti * 128:(_ti + 1) * 128, h0:h0 + TH], zs_s,
                      waits=[(sc, m1_sc[-1])] if _ti == 0 else ())
                _ti += 1
        t_zsh = p.tok(zs_s)
        for i in range(DT):
            p.dma("sync", T1H[i], t1_sp[i * 128:(i + 1) * 128, h0:h0 + TH],
                  t1_s, waits=[(mm, m1_mm[-1])] if i == 0 else ())
        t_t1h = p.tok(t1_s)

        m2_mm = []
        tsg_sc = []
        zb_dv = []
        tg_mm = []
        mr_dv = []
        for j in range(16):
            for sub, mtile in ((0, j), (1, j + 16)):
                bank = ps[sub]
                for hh in range(2):
                    bi = (4 * j + 2 * sub + hh) % 4
                    buf = W2B[bi]
                    w = [t_h, t_t1h]
                    if len(m2_mm) >= 4:
                        w.append((mm, m2_mm[-4]))
                    p.dma("gpsimd", buf.rearrange("p (kc c) -> p kc c", c=128),
                          W2[mtile, hh], w2s[bi], waits=w)
                    t_w = p.tok(w2s[bi])
                    mmw = [t_w, t_h]
                    if len(tsg_sc) >= 1:
                        mmw.append((sc, tsg_sc[-1]))
                    for kc in range(32):
                        l = buf[:, kc * 128:(kc + 1) * 128]
                        kg = hh * 32 + kc
                        p.emit("tensor", lambda e, o=bank, l2=l, k2=kg:
                               e.matmul(o[:, :], l2, HB[k2][:, :],
                                        start=(k2 == 0), stop=(k2 == 63)),
                               waits=mmw if kc == 0 else (),
                               inc=mm if (kc == 31) else None)
                    m2_mm.append(mm.count)
            t_tr = p.tok(mm)
            fbuf = FBj[j % 2]
            fw = [t_t1h]
            if j >= 2:
                fw.append((mm, tg_mm[-2]))
            dma_fcols(fbuf, j * 128, 128, fw, fjs[j % 2])
            t_fj = p.tok(fjs[j % 2])
            for pl in range(2):
                bank = ps[2 + pl]
                mmw = [t_fj]
                if zb_dv:
                    mmw.append((dv, zb_dv[-1]))
                for kc in range(16):
                    base = pl * 2048 + kc * 128
                    l = fbuf[:, base:base + 128]
                    p.emit("tensor", lambda e, o=bank, l2=l, k2=kc:
                           e.matmul(o[:, :], l2, T1H[k2],
                                    start=(k2 == 0), stop=(k2 == 15)),
                           waits=mmw if (kc == 0 and pl == 0) else (),
                           inc=mm if kc == 15 else None)
            tg_mm.append(mm.count)
            t_tg = p.tok(mm)
            tsg_w = [(mm, m2_mm[-3])]
            if zb_dv:
                tsg_w.append((dv, zb_dv[-1]))
            p.emit("scalar", lambda e, j2=j: e.activation(
                TSG[:, 0:TH], ps[0][:, :], AF.Identity,
                bias=b2cb.ap()[:, j2:j2 + 1]), waits=tsg_w, inc=sc)
            p.emit("scalar", lambda e, j2=j: e.activation(
                TSG[:, TH:2 * TH], ps[1][:, :], AF.Identity,
                bias=b2cb.ap()[:, 16 + j2:17 + j2]), waits=[(mm, m2_mm[-1])],
                inc=sc)
            tsg_sc.append(sc.count)
            t_tsg = p.tok(sc)
            if j < 8:
                zsr = ZSH[(0, j)]
                zsi = ZSH[(1, j)]
                t_zm = None
            else:
                Dd = j
                for pl in range(2):
                    bank = ps[4 + pl]
                    two = (pl == 0) or (Dd > 8)
                    RA = RevAb if pl == 0 else RevAnb
                    P0 = Pick0b if pl == 0 else Pick0nb
                    mw = [t_zsh]
                    if mr_dv:
                        mw.append((dv, mr_dv[-1]))
                    p.emit("tensor", lambda e, b2=bank, ra=RA,
                           s2=(pl, 15 - Dd), tw=two:
                           e.matmul(b2[:, :], ra.ap()[:, :], ZSH[s2],
                                    start=True, stop=not tw),
                           waits=mw, inc=None if two else mm)
                    if two:
                        p.emit("tensor", lambda e, b2=bank, p0=P0,
                               s2=(pl, 16 - Dd):
                               e.matmul(b2[:, :], p0.ap()[:, :], ZSH[s2],
                                        start=False, stop=True), inc=mm)
                t_zmm = p.tok(mm)
                p.emit("vector", lambda e: e.tensor_copy(
                    ZMR[:, 0:TH], ps[4][:, :]), waits=[t_zmm], inc=dv)
                p.emit("vector", lambda e: e.tensor_copy(
                    ZMR[:, TH:2 * TH], ps[5][:, :]), inc=dv)
                mr_dv.append(dv.count)
                zsr = ZMR[:, 0:TH]
                zsi = ZMR[:, TH:2 * TH]

            def vop(fn, waits=(), inc=dv):
                p.emit("vector", fn, waits=list(waits) + [p.tok(dv)], inc=inc)

            vop(lambda e: e.scalar_tensor_tensor(
                SLOT[0][:, :], ps[2][:, :], NINV, TSG[:, 0:TH],
                ALU.mult, ALU.mult), waits=[t_tg, t_tsg, t_zsh])
            vop(lambda e: e.scalar_tensor_tensor(
                SLOT[1][:, :], ps[3][:, :], NINV, TSG[:, TH:2 * TH],
                ALU.mult, ALU.mult))
            vop(lambda e: e.scalar_tensor_tensor(
                SLOT[2][:, :], ps[2][:, :], NINV, TSG[:, TH:2 * TH],
                ALU.mult, ALU.mult))
            vop(lambda e: e.scalar_tensor_tensor(
                SLOT[3][:, :], ps[3][:, :], NINV, TSG[:, 0:TH],
                ALU.mult, ALU.mult))
            vop(lambda e: e.tensor_sub(SLOT[0][:, :], SLOT[0][:, :],
                                       SLOT[1][:, :]))
            vop(lambda e, z=zsr, j2=j: e.tensor_add(ZB[j2], SLOT[0][:, :], z))
            vop(lambda e: e.tensor_add(SLOT[2][:, :], SLOT[2][:, :],
                                       SLOT[3][:, :]))
            vop(lambda e, z=zsi, j2=j: e.tensor_add(ZB[16 + j2],
                                                    SLOT[2][:, :], z))
            zb_dv.append(dv.count)
        t_zb = p.tok(dv)
        if half == 0 and _stop("p9z"):
            return nc, stack, sorted(taps & set(tap_out))
        if half == 0 and tap("z", [32 * 128, TH], BF16):
            for i in range(32):
                p.dma("sync", tap_out["z"][i * 128:(i + 1) * 128, :], ZB[i],
                      st, waits=[t_zb])

        for i in range(DT):
            p.dma("sync", FFH[i], ff_sp[i * 128:(i + 1) * 128, h0:h0 + TH],
                  ff_s, waits=[(mm, m2_mm[-1])] if i == 0 else ())
        t_ffh = p.tok(ff_s)

        if_mm = []
        aos_dv = []
        for dt_ in range(DT):
            fbuf = FBI[dt_ % 2]
            fw = [t_zb]
            if dt_ >= 2:
                fw.append((mm, if_mm[-2]))
            dma_fcols(fbuf, dt_ * 128, 128, fw, fis[dt_ % 2])
            t_fi = p.tok(fis[dt_ % 2])
            bank = ps[dt_ % 2]
            mmw = [t_fi, t_zb]
            if len(aos_dv) >= 2:
                mmw.append((dv, aos_dv[-2]))
            for pl in range(2):
                for kc in range(16):
                    base = pl * 2048 + kc * 128
                    l = fbuf[:, base:base + 128]
                    zi = pl * 16 + kc
                    p.emit("tensor", lambda e, o=bank, l2=l, z2=zi:
                           e.matmul(o[:, :], l2, ZB[z2],
                                    start=(z2 == 0), stop=(z2 == 31)),
                           waits=mmw if (kc == 0 and pl == 0) else (),
                           inc=mm if (pl == 1 and kc == 15) else None)
            if_mm.append(mm.count)
            t_if = p.tok(mm)
            p.emit("vector", lambda e, d2=dt_, b2=bank: e.tensor_add(
                AOS[d2], b2[:, :], FFH[d2]), waits=[t_if, t_ffh], inc=dv)
            aos_dv.append(dv.count)
        t_aos = p.tok(dv)
        if half == 0 and tap("aos", [DT * 128, TH], BF16):
            for i in range(DT):
                p.dma("sync", tap_out["aos"][i * 128:(i + 1) * 128, :],
                      AOS[i], st, waits=[t_aos])

        st_done = []
        for tt in range(4):
            grow = half * 4 + tt
            xw = [t_aos]
            if st_done:
                xw.append((st, st_done[-1]))
            p.dma("sync", XRL[0], x_my[grow * 128:(grow + 1) * 128, :], xr_s,
                  waits=xw)
            t_x = p.tok(xr_s)
            add_dv = []
            for g in range(4):
                tw = [t_aos]
                if add_dv:
                    tw.append((dv, add_dv[-1]))
                elif st_done:
                    tw.append((st, st_done[-1]))
                for j in range(4):
                    d2 = g * 4 + j
                    p.emit("tensor", lambda e, d3=d2, t2=tt, jj=j:
                           e.transpose(pt[:, jj * 128:(jj + 1) * 128],
                                       AOS[d3][:, t2 * 128:(t2 + 1) * 128],
                                       identb.ap()[:, :]),
                           waits=tw if j == 0 else (),
                           inc=mm if j == 3 else None)
                t_t = p.tok(mm)
                aw = [t_t, t_x]
                if g == 0 and st_done:
                    aw.append((st, st_done[-1]))
                p.emit("vector", lambda e, g2=g: e.tensor_add(
                    OUTB[:, g2 * 512:(g2 + 1) * 512], pt[:, :],
                    XRL[0][:, g2 * 512:(g2 + 1) * 512]),
                    waits=aw, inc=dv)
                add_dv.append(dv.count)
            t_ob = p.tok(dv)
            p.dma("sync", y[grow * 128:(grow + 1) * 128, :], OUTB, st,
                  waits=[t_ob])
            st_done.append(st.count)
        prev_done = p.tok(st)
        if half == 0 and _stop("p10h0"):
            return nc, stack, sorted(taps & set(tap_out))

    p.emit("sync", lambda e: None, waits=[p.tok(st)])
    p.finalize()
    return nc, stack, sorted(taps & set(tap_out))


_CACHE = {}


def _consts():
    if "c" in _CACHE:
        return _CACHE["c"]
    n = np.arange(D, dtype=np.float64)
    th = 2.0 * np.pi * np.outer(n, n) / D
    def _tile2b(a):
        return np.ascontiguousarray(
            a.reshape(16, 128, 16, 128).transpose(2, 1, 0, 3))
    Frv = _tile2b(np.cos(th)).astype(ml_dtypes.bfloat16)
    Fiv = _tile2b(-np.sin(th)).astype(ml_dtypes.bfloat16)
    ident = np.eye(128, dtype=ml_dtypes.bfloat16)
    RevAv = np.zeros((128, 128), dtype=np.float32)
    for q in range(1, 128):
        RevAv[128 - q, q] = 1.0
    Pick0v = np.zeros((128, 128), dtype=np.float32)
    Pick0v[0, 0] = 1.0
    onesrv = np.ones((1, 128), dtype=ml_dtypes.bfloat16)
    f1024v = np.ascontiguousarray(
        np.cos(th[:, 1024]).reshape(16, 128).T).astype(ml_dtypes.bfloat16)
    _CACHE["c"] = (Frv, Fiv, ident, RevAv.astype(ml_dtypes.bfloat16),
                   (-RevAv).astype(ml_dtypes.bfloat16),
                   Pick0v.astype(ml_dtypes.bfloat16),
                   (-Pick0v).astype(ml_dtypes.bfloat16), onesrv, f1024v)
    return _CACHE["c"]


def _tile2(w):
    w = np.asarray(w, np.float32)
    return np.ascontiguousarray(
        w.reshape(16, 128, 16, 128).transpose(2, 1, 0, 3))


def _tile_w1(w):
    w = np.asarray(w, np.float32)
    return np.ascontiguousarray(
        w.reshape(32, 128, 64, 128).transpose(2, 1, 0, 3))


def _tile_w2(w):
    w = np.asarray(w, np.float32)
    return np.ascontiguousarray(
        w.reshape(2, 32, 128, 32, 128).transpose(3, 0, 2, 1, 4))


def make_inputs(kw, core):
    (Frv, Fiv, ident, RevAv, RevAnv, Pick0v, Pick0nv, onesrv,
     f1024v) = _consts()
    x = np.asarray(kw["L0_attn_x"], dtype=np.float32)
    b = core // 4
    q = core % 4
    d = {
        "x_my": np.ascontiguousarray(x[b, q * T:(q + 1) * T, :]),
        "x_rt": np.ascontiguousarray(
            np.concatenate([x[0, core * RT:(core + 1) * RT, :],
                            x[1, core * RT:(core + 1) * RT, :]], axis=0)),
        "Wq": _tile2(kw["L0_attn_Wq"]), "Wk": _tile2(kw["L0_attn_Wk"]),
        "Wv": _tile2(kw["L0_attn_Wv"]), "Ws": _tile2(kw["L0_attn_Ws"]),
        "Wt": _tile2(kw["L0_attn_Wt"]),
        "W1": _tile_w1(kw["L0_attn_W1"]), "W2": _tile_w2(kw["L0_attn_W2"]),
        "w3t": _tile2(np.asarray(kw["L0_ffn_w3"]).T),
        "b1c": np.ascontiguousarray(
            np.asarray(kw["L0_attn_b1"]).reshape(64, 128).T),
        "b2c": np.ascontiguousarray(
            np.asarray(kw["L0_attn_b2"]).reshape(32, 128).T),
        "b3c": np.ascontiguousarray(
            np.asarray(kw["L0_ffn_b3"]).reshape(16, 128).T),
        "fw1t": np.ascontiguousarray(
            np.asarray(kw["L0_ffn_w1"]).transpose(0, 2, 1)),
        "fw2t": np.ascontiguousarray(
            np.asarray(kw["L0_ffn_w2"]).transpose(0, 2, 1)),
        "fb1c": np.ascontiguousarray(np.asarray(kw["L0_ffn_b1"]).T),
        "fb2": np.asarray(kw["L0_ffn_b2"]),
        "swc": np.ascontiguousarray(
            np.asarray(kw["L0_ffn_switch_w"]).reshape(NE, 2048, D)
            [:, core * RT:(core + 1) * RT, :]),
        "swb_bc": np.ascontiguousarray(
            np.tile(np.asarray(kw["L0_ffn_switch_b"])[None, :], (2, 1))),
        "mask_bc": np.ascontiguousarray(np.tile(
            np.array([[1.0 if (j // 4 == b and j % 4 < q) else 0.0
                       for j in range(NCORE)]], np.float32), (128, 1))),
        "bselc": np.array([[1.0 if b == 0 else 0.0],
                           [1.0 if b == 1 else 0.0]], np.float32),
        "Fr": Frv, "Fi": Fiv, "identc": ident,
        "RevA": RevAv, "RevAn": RevAnv, "Pick0": Pick0v, "Pick0n": Pick0nv,
        "onesr": onesrv,
        "f1024": f1024v,
    }
    for k in list(d):
        a = np.asarray(d[k])
        if a.dtype == np.float64:
            a = a.astype(np.float32)
        d[k] = np.ascontiguousarray(a)
    return d


_NC_CACHE = {}


def kernel(**inputs):
    key = ()
    if key not in _NC_CACHE:
        _NC_CACHE[key] = build_program(())
    nc, stack, _ = _NC_CACHE[key]
    in_maps = [make_inputs(inputs, c) for c in range(NCORE)]
    res = run_bass_kernel_spmd(nc, in_maps, core_ids=list(range(NCORE)))
    out = np.empty((B, S, D), dtype=np.float32)
    for c in range(NCORE):
        out[c // 4, (c % 4) * T:(c % 4 + 1) * T, :] = res.results[c]["y"]
    return out
